# version 27
# baseline (speedup 1.0000x reference)
import numpy as np
import concourse.bass as bass
import concourse.mybir as mybir
from concourse.bass_utils import run_bass_kernel_spmd

F32 = mybir.dt.float32
F16 = mybir.dt.float16
BF16 = mybir.dt.bfloat16
I32 = mybir.dt.int32
AF = mybir.ActivationFunctionType
OP = mybir.AluOpType

D = 1024
HID = 3072
KC = 8
M = 24
NCORES = 8
LN_EPS = 1e-5
QS = 1.0 / 127.0

ENGINES = ("sync", "gpsimd", "tensor", "vector", "scalar")


class Planner:

    def __init__(self):
        self.items = {e: [] for e in ENGINES}
        self.cnt = {}

    def op(self, eng, fn, waits=(), incs=()):
        for s, k in incs:
            self.cnt[s] = self.cnt.get(s, 0) + k
        self.items[eng].append((fn, list(waits), list(incs)))

    def now(self, sem):
        return self.cnt.get(sem, 0)


def build_nc(T):
    assert T % 128 == 0
    TC = T // 128
    MV = min(512, T)
    NTS = T // MV
    nc = bass.Bass("TRN2", target_bir_lowering=False)

    xT = nc.dram_tensor("xT", [D, T], F32, kind="ExternalInput").ap()
    xN = nc.dram_tensor("xN", [T, D], F32, kind="ExternalInput").ap()
    Wi = nc.dram_tensor("Wi", [D, D], F32, kind="ExternalInput").ap()
    Wf = nc.dram_tensor("Wf", [D, D], F32, kind="ExternalInput").ap()
    Wo = nc.dram_tensor("Wo", [D, D], F32, kind="ExternalInput").ap()
    Ws = nc.dram_tensor("Ws", [D, D], F32, kind="ExternalInput").ap()
    biases = nc.dram_tensor("biases", [128, 32], F32, kind="ExternalInput").ap()
    gamma = nc.dram_tensor("gamma", [D], F32, kind="ExternalInput").ap()
    beta = nc.dram_tensor("beta", [D], F32, kind="ExternalInput").ap()
    sel = nc.dram_tensor("sel", [128, 8], F32, kind="ExternalInput").ap()
    ident = nc.dram_tensor("ident", [128, 128], F32, kind="ExternalInput").ap()
    wgT = nc.dram_tensor("wgT", [D, HID], I32, kind="ExternalInput").ap()
    wuT = nc.dram_tensor("wuT", [D, HID], I32, kind="ExternalInput").ap()
    wdT = nc.dram_tensor("wdT", [HID, D], I32, kind="ExternalInput").ap()
    out_d = nc.dram_tensor("out", [T, D], F32, kind="ExternalOutput").ap()
    cc_in = nc.dram_tensor("cc_in", [128, 16], F32, kind="Internal").ap()
    cc_out = nc.dram_tensor("cc_out", [4, 128, 16], F32, kind="Internal",
                            addr_space="Local").ap()
    x1s = nc.dram_tensor("x1s", [T, D], F16, kind="Internal").ap()

    GW = 4 * T
    O_XFM = 0
    O_G1 = GW
    O_G2 = 2 * GW
    O_G3 = 3 * GW
    O_Q = 4 * GW
    QW_W0 = O_Q
    QW_W1 = O_Q + 4096
    QW_SCR = O_Q + 8192
    QW_Z = QW_SCR + max(T // 2, 1024)
    QW_N1 = QW_Z + 1024
    Q_END = QW_N1 + 1024
    O_WD = O_Q
    O_X1BF = Q_END
    assert Q_END + 1024 - O_Q <= 12288
    O_P = O_Q + 12288
    P_GAM = O_P
    P_BET = O_P + 512
    P_BIA = O_P + 1024
    P_SEL = P_BIA + 32
    P_IDF = P_SEL + 8
    P_IDB = P_IDF + 64
    P_SUM = P_IDB + 64
    P_GATH = P_SUM + 16
    P_HRUN = P_GATH + 128
    P_HSELA = P_HRUN + 8
    P_HSELB = P_HSELA + 8
    P_COLS = P_HSELB + 8
    P_NH = P_COLS + 144
    P_SL = P_NH + 8
    P_XT = P_SL + 8
    P_OUTST = P_XT + 2048
    P_WGU = P_OUTST + 2048
    P_END = P_WGU + 2048
    NWORDS = P_END
    assert NWORDS <= 53000, NWORDS

    P = Planner()
    semnames = ([f"g_xf{k}" for k in range(KC)]
                + [f"g_wA{m}" for m in range(8)] + [f"g_wB{m}" for m in range(8)]
                + ["g_w0", "g_w1", "g_small", "g_gb", "g_wd", "r_bc", "l_bc", "g_wgu0",
                "g_wgu1", "s_x0", "s_x1", "s_x2", "s_x3", "s_cc", "s_x1w0", "s_x1w1", "s_x1w2", "s_x1w3", "s_wf0", "s_wf1", "s_out0", "s_out1",
                "T_", "V_", "S_", "G_"])

    arena_ctx = nc.sbuf_tensor("AR", [128, NWORDS], F32)
    AR = arena_ctx.__enter__()
    ps_ctx = nc.psum_tensor("PS", [128, 4096], F32)
    PS = ps_ctx.__enter__()
    ar = AR[:]
    ar16 = ar.bitcast(F16)
    arbf = ar.bitcast(BF16)
    ps = PS[:]
    ps16 = ps.bitcast(F16)
    psbf = ps.bitcast(BF16)

    def f32v(off, n):
        return ar[:, off:off + n]

    def f16v(off, n):
        return ar16[:, 2 * off: 2 * off + n]

    def bf16v(off, n):
        return arbf[:, 2 * off: 2 * off + n]

    xfm = bf16v(O_XFM, 8 * T)
    G1 = f16v(O_G1, 8 * T)
    G2 = f16v(O_G2, 8 * T)
    G3 = f16v(O_G3, 8 * T)
    x1fm = f16v(O_G3, 8 * T)
    hh = bf16v(O_XFM, 24 * T)
    Wb = [bf16v(QW_W0, 8 * 1024), bf16v(QW_W1, 8 * 1024)]
    scr = f16v(QW_SCR, T)
    zb = [f16v(QW_Z, 1024), f16v(QW_Z + 512, 1024),
          f16v(QW_SCR, 1024), f16v(QW_SCR + 512, 1024),
          f16v(P_OUTST, 1024), f16v(P_OUTST + 512, 1024),
          f16v(P_OUTST + 1024, 1024), f16v(P_OUTST + 1536, 1024)]
    ab = [f16v(QW_N1, 1024), f16v(QW_N1 + 512, 1024)]
    wd_sb = bf16v(O_WD, 24 * 1024)
    x1bf = [f16v(O_X1BF, 1024), f16v(O_X1BF + 512, 1024)]
    gam = f16v(P_GAM, 1024)
    bet = f16v(P_BET, 1024)
    bia = f32v(P_BIA, 32)
    selv = f32v(P_SEL, 8)
    idf = f16v(P_IDF, 128)
    idb = bf16v(P_IDB, 128)
    summ = f32v(P_SUM, 16)
    gath = f32v(P_GATH, 128)
    Hrun = f32v(P_HRUN, 8)
    HselA = f32v(P_HSELA, 8)
    HselB = f32v(P_HSELB, 8)
    nh = f32v(P_NH, 1)
    nh4 = f32v(P_NH, 4)
    sumlog = f32v(P_SL, 8)

    def col(i):
        return f32v(P_COLS + i, 1)

    xt = [f16v(P_XT, 1024), f16v(P_XT + 512, 1024),
          f16v(P_XT + 1024, 1024), f16v(P_XT + 1536, 1024)]
    sg = [f16v(P_XT, T), f16v(P_XT + 1024, T)]
    outst = [f32v(P_OUTST, 1024), f32v(P_OUTST + 1024, 1024)]
    wg_b = [bf16v(P_WGU, 1024), bf16v(P_WGU + 512, 1024)]
    wu_b = [bf16v(P_WGU + 1024, 1024), bf16v(P_WGU + 1536, 1024)]

    def grpA(par):
        return ps[:, par * 2048: par * 2048 + T]

    psB = [ps16[:, 0:1024], ps16[:, 1024:2048],
           ps16[:, 6144:7168], ps16[:, 7168:8192]]
    dumpB = G2[:, 0:1024]
    psC = [ps16[:, (2 + j) * 1024:(3 + j) * 1024] for j in range(4)]
    psG = ps[:, 0:T]
    psU = ps[:, 2048:2048 + T]
    psD = [ps[:, 0:1024], ps[:, 1024:2048]]
    dumpC = ps[:, 2048:3072]

    for fn in [
        lambda g: g.dma_start(out=bia, in_=biases),
        lambda g: g.dma_start(out=selv, in_=sel),
        lambda g: g.dma_start(out=idf, in_=ident),
        lambda g: g.dma_start(out=idb, in_=ident),
        lambda g: g.dma_start(out=gam, in_=gamma.partition_broadcast(128)),
        lambda g: g.dma_start(out=bet, in_=beta.partition_broadcast(128)),
    ]:
        P.op("gpsimd", fn, incs=[("g_small", 16)])
    staged_wf = (4 * T >= 8192)
    if staged_wf:
        for h in range(2):
            P.op("sync", lambda s, h=h: s.dma_start(
                out=f32v(O_G3 + h * 4096, 4096).rearrange("p (kc e) -> p kc e", kc=4),
                in_=Wf[h * 512:(h + 1) * 512, :].rearrange("(kc p) e -> p kc e", p=128)),
                incs=[(f"s_wf{h}", 16)])
        for h in range(2):
            P.op("vector", lambda v, h=h: v.tensor_copy(
                out=Wb[0][:, h * 4096:(h + 1) * 4096],
                in_=f32v(O_G3 + h * 4096, 4096)),
                waits=[(f"s_wf{h}", 16)], incs=[("V_", 1)])
    else:
        P.op("gpsimd", lambda g: g.dma_start(
            out=Wb[0].rearrange("p (kc e) -> p kc e", kc=KC),
            in_=Wf.rearrange("(kc p) e -> p kc e", p=128)), incs=[("g_w0", 16)])
    v_wf = P.now("V_")
    P.op("gpsimd", lambda g: g.dma_start(
        out=Wb[1].rearrange("p (kc e) -> p kc e", kc=KC),
        in_=Wi.rearrange("(kc p) e -> p kc e", p=128)), incs=[("g_w1", 16)])
    for k in range(KC):
        P.op("gpsimd", lambda g, k=k: g.dma_start(
            out=xfm[:, k * T:(k + 1) * T],
            in_=xT[k * 128:(k + 1) * 128, :]), incs=[(f"g_xf{k}", 16)])
    P.op("gpsimd", lambda g: g.memset(nh4[:], -0.5), incs=[("G_", 1)])

    for j in range(min(4, TC)):
        P.op("gpsimd", lambda g, j=j: g.dma_start(
            out=xt[j], in_=xN[j * 128:(j + 1) * 128, :]),
            incs=[(f"s_x{j}", 16)])
    sx_cnt = [16, 16, 16, 16]

    projs = [
        (0, AF.Sigmoid, G1),
        (1, AF.Sigmoid, G3),
        (0, AF.Tanh, G2),
        (1, AF.Sigmoid, G2),
    ]
    evA = []
    uV = []
    projT = []
    def plan_proj(p):
        slot, act, dst = projs[p]
        waits = []
        if p == 0:
            waits = [("V_", v_wf)] if staged_wf else [("g_w0", 16)]
        elif p == 1:
            waits = [("g_w1", 16)]
        elif p == 2:
            waits = [("g_w0", 16 if staged_wf else 32)]
        elif p == 3:
            waits = [("g_w1", 32)]
        for m in range(8):
            gidx = p * 8 + m
            par = gidx % 2
            w = list(waits) if m == 0 else []
            if gidx >= 2:
                w.append(("S_", evA[gidx - 2]))
            if p == 3:
                pass
            for k in range(KC):
                if p == 0 and m == 0:
                    w.append((f"g_xf{k}", 16))
                for ts in range(NTS):
                    def mm(t, par=par, k=k, ts=ts, slot=slot, m=m):
                        return t.matmul(
                            out=grpA(par)[:, ts * MV:(ts + 1) * MV],
                            lhsT=Wb[slot][:, k * 1024 + m * 128: k * 1024 + (m + 1) * 128],
                            rhs=xfm[:, k * T + ts * MV: k * T + ts * MV + MV],
                            start=(k == 0), stop=(k == KC - 1))
                    last = (k == KC - 1 and ts == NTS - 1)
                    P.op("tensor", mm, waits=w, incs=[("T_", 1)] if last else [])
                    w = []
            tdone = P.now("T_")
            ew = [("T_", tdone)]
            if p == 3:
                ew.append(("V_", uV[m]))

            def ev(sc, par=par, act=act, dst=dst, m=m, p=p):
                return sc.activation(out=dst[:, m * T:(m + 1) * T],
                                     in_=grpA(par)[:, 0:T], func=act,
                                     bias=bia[:, p * 8 + m: p * 8 + m + 1])
            if p == 0 and m == 0:
                ew.append(("g_small", 96))
            if p == 1 and m == 0 and staged_wf:
                ew.append(("V_", v_wf))
            P.op("scalar", ev, waits=ew, incs=[("S_", 1)])
            evA.append(P.now("S_"))
        projT.append(P.now("T_"))
        if p == 0:
            P.op("gpsimd", lambda g: g.dma_start(
                out=Wb[0].rearrange("p (kc e) -> p kc e", kc=KC),
                in_=Ws.rearrange("(kc p) e -> p kc e", p=128)),
                waits=[("T_", projT[0])], incs=[("g_w0", 16)])
        if p == 1:
            P.op("gpsimd", lambda g: g.dma_start(
                out=Wb[1].rearrange("p (kc e) -> p kc e", kc=KC),
                in_=Wo.rearrange("(kc p) e -> p kc e", p=128)),
                waits=[("T_", projT[1])], incs=[("g_w1", 16)])


    plan_proj(0)
    plan_proj(1)
    plan_proj(2)

    lndump = f16v(O_X1BF, 2 * T)
    for kc in range(KC):
        P.op("scalar", lambda sc, kc=kc: sc.activation(
            out=lndump[:, 0:T], in_=G1[:, kc * T:(kc + 1) * T], func=AF.Ln,
            accum_out=sumlog[:, kc:kc + 1]),
            waits=[("S_", P.now("S_"))], incs=[("S_", 1)])
    P.op("scalar", lambda sc: sc.activation(
        out=summ[:, 0:8], in_=sumlog, func=AF.Exp),
        waits=[("S_", P.now("S_"))], incs=[("S_", 1)])
    s_atot = P.now("S_")
    for kc in range(KC):
        P.op("vector", lambda v, kc=kc: v.tensor_tensor(
            out=G3[:, kc * T:(kc + 1) * T], in0=G3[:, kc * T:(kc + 1) * T],
            in1=G2[:, kc * T:(kc + 1) * T], op=OP.mult),
            waits=[("S_", evA[16 + kc]), ("V_", P.now("V_"))], incs=[("V_", 1)])
        uV.append(P.now("V_"))
        P.op("vector", lambda v, kc=kc: v.tensor_tensor_scan(
            out=scr, data0=G1[:, kc * T:(kc + 1) * T],
            data1=G3[:, kc * T:(kc + 1) * T], initial=0.0,
            op0=OP.mult, op1=OP.add),
            waits=[("S_", evA[kc]), ("V_", P.now("V_"))], incs=[("V_", 1)])
        P.op("vector", lambda v, kc=kc: v.tensor_copy(
            out=summ[:, 8 + kc:9 + kc], in_=scr[:, T - 1:T]),
            waits=[("V_", P.now("V_"))], incs=[("V_", 1)])
    v_summ = P.now("V_")
    plan_proj(3)


    P.op("sync", lambda s: s.dma_start(out=cc_in, in_=summ),
         waits=[("V_", v_summ), ("S_", s_atot)], incs=[("s_cc", 16)])
    P.op("gpsimd", lambda g: g.collective_compute(
        "AllGather", OP.bypass, replica_groups=[[0, 1, 2, 3], [4, 5, 6, 7]],
        ins=[cc_in], outs=[cc_out]),
        waits=[("s_cc", 16)], incs=[("G_", 1)])
    P.op("gpsimd", lambda g: g.dma_start(
        out=gath[:, 0:64].rearrange("p (r c) -> p r c", r=4),
        in_=cc_out.rearrange("r p c -> p r c")),
        waits=[("G_", P.now("G_"))], incs=[("g_gb", 16)])

    P.op("vector", lambda v: v.memset(Hrun[:], 0.0),
         waits=[("g_gb", 16), ("g_small", 96)], incs=[("V_", 1)])
    P.op("vector", lambda v: v.memset(HselA[:], 0.0),
         waits=[("V_", P.now("V_"))], incs=[("V_", 1)])
    hsel_cur, hsel_alt = HselA, HselB
    for r in range(4):
        P.op("vector", lambda v, r=r, a=hsel_alt, c=hsel_cur: v.scalar_tensor_tensor(
            out=a, in0=Hrun[:], scalar=selv[:, r:r + 1], in1=c,
            op0=OP.mult, op1=OP.add),
            waits=[("V_", P.now("V_"))], incs=[("V_", 1)])
        hsel_cur, hsel_alt = hsel_alt, hsel_cur
        if r < 3:
            P.op("vector", lambda v, r=r: v.tensor_tensor(
                out=Hrun[:], in0=Hrun[:], in1=gath[:, r * 16:r * 16 + 8], op=OP.mult),
                waits=[("V_", P.now("V_"))], incs=[("V_", 1)])
            P.op("vector", lambda v, r=r: v.tensor_tensor(
                out=Hrun[:], in0=Hrun[:], in1=gath[:, r * 16 + 8:r * 16 + 16], op=OP.add),
                waits=[("V_", P.now("V_"))], incs=[("V_", 1)])
    Hfin = hsel_cur

    yV = []
    for kc in range(KC):
        P.op("vector", lambda v, kc=kc: v.tensor_tensor_scan(
            out=scr, data0=G1[:, kc * T:(kc + 1) * T],
            data1=G3[:, kc * T:(kc + 1) * T], initial=Hfin[:, kc:kc + 1],
            op0=OP.mult, op1=OP.add),
            waits=[("V_", P.now("V_"))], incs=[("V_", 1)])
        P.op("vector", lambda v, kc=kc: v.tensor_tensor(
            out=G1[:, kc * T:(kc + 1) * T], in0=G2[:, kc * T:(kc + 1) * T],
            in1=scr, op=OP.mult),
            waits=[("V_", P.now("V_")), ("S_", evA[24 + kc])], incs=[("V_", 1)])
        yV.append(P.now("V_"))

    scr_words = max(T // 2, 1024)
    n_scr_ch = scr_words // 512
    wd_cnt = [0]

    def plan_wd(mlist, waits):
        for mi in mlist:
            P.op("gpsimd", lambda g, mi=mi: g.dma_start(
                out=wd_sb[:, mi * 1024:(mi + 1) * 1024],
                in_=wdT[mi * 128:(mi + 1) * 128, :]),
                waits=waits if mi == mlist[0] else [], incs=[("g_wd", 16)])
            wd_cnt[0] += 1

    plan_wd(list(range(0, 16)), [("T_", projT[3])])
    wd_rest = list(range(16, 24))

    colnames = ("s2", "ssum", "mu", "var", "rstd", "nb")
    colix = {}
    for stg in range(3):
        for ni, name in enumerate(colnames):
            for pr in range(8):
                colix[(stg, pr, name)] = stg * 48 + ni * 8 + pr
    lnV_last = {}
    lnS_last = {}
    tyT = [0] * TC
    v_z = [0] * TC
    v_w = [0] * TC
    v_xb_m = [0] * TC
    x1bf_S = [0] * TC
    ap32_S = [0] * TC
    x1T_quad = [0] * TC
    evC = []
    sq_mark = {}

    def lncol(stage, tc, name):
        return col(colix[(stage, tc % 8, name)])

    def ln_sq(stage, tc, buf, vwait, dump):
        key = (stage, tc % 8)
        c_s2 = lncol(stage, tc, "s2")
        wsq = [("V_", vwait), ("S_", P.now("S_"))]
        if stage < 2:
            wsq.append(("V_", yV[-1]))
        if key in lnV_last:
            wsq.append(("V_", lnV_last[key]))
        P.op("scalar", lambda sc, buf=buf, c_s2=c_s2, dump=dump: sc.activation(
            out=dump, in_=buf, func=AF.Square, accum_out=c_s2),
            waits=wsq, incs=[("S_", 1)])
        sq_mark[key] = P.now("S_")

    cm_gp = {}

    def ln_cm_a(stage, vwait, pr0=0, nw=4):
        def c4(name):
            ni = colnames.index(name)
            return f32v(P_COLS + stage * 48 + ni * 8 + pr0, nw)
        ss4, mu4, va4, rs4 = c4("ssum"), c4("mu"), c4("var"), c4("rstd")
        s24 = c4("s2")
        P.op("vector", lambda v: v.tensor_tensor(
            out=mu4, in0=ss4, in1=ss4, op=OP.mult),
            waits=[("V_", vwait)], incs=[("V_", 1)])
        P.op("vector", lambda v: v.scalar_tensor_tensor(
            out=va4, in0=mu4, scalar=-1.0 / D, in1=s24, op0=OP.mult, op1=OP.add),
            waits=[("V_", P.now("V_")), ("S_", sq_mark[(stage, pr0 + nw - 1)])],
            incs=[("V_", 1)])
        P.op("vector", lambda v: v.tensor_scalar(
            out=va4, in0=va4, scalar1=1.0 / D, scalar2=LN_EPS,
            op0=OP.mult, op1=OP.add),
            waits=[("V_", P.now("V_"))], incs=[("V_", 1)])
        vv = P.now("V_")
        if stage < 2:
            P.op("vector", lambda v: v.reciprocal(out=rs4, in_=va4),
                 waits=[("V_", vv)], incs=[("V_", 1)])
            P.op("scalar", lambda sc: sc.activation(
                out=rs4, in_=rs4, func=AF.Sqrt),
                waits=[("V_", P.now("V_")), ("S_", P.now("S_"))], incs=[("S_", 1)])
            cm_gp[(stage, pr0)] = ("S_", P.now("S_"))
        else:
            P.op("gpsimd", lambda g: g.tensor_tensor(
                out=rs4, in0=va4, in1=nh4[:, 0:nw], op=OP.pow),
                waits=[("V_", vv), ("G_", 1)], incs=[("G_", 1)])
            cm_gp[(stage, pr0)] = ("G_", P.now("G_"))

    def ln_cm_b(stage, pr0=0, nw=4):
        def c4(name):
            ni = colnames.index(name)
            return f32v(P_COLS + stage * 48 + ni * 8 + pr0, nw)
        ss4, rs4, nb4 = c4("ssum"), c4("rstd"), c4("nb")
        P.op("vector", lambda v: v.scalar_tensor_tensor(
            out=nb4, in0=ss4, scalar=-1.0 / D, in1=rs4, op0=OP.mult, op1=OP.mult),
            waits=[cm_gp[(stage, pr0)], ("V_", P.now("V_"))],
            incs=[("V_", 1)])
        for pr in range(pr0, pr0 + nw):
            lnV_last[(stage, pr)] = P.now("V_")

    def ln_back(stage, tc):
        return (lncol(stage, tc, "rstd"), lncol(stage, tc, "nb"),
                lnV_last[(stage, tc % 8)])

    def b_step_A(tc):
        pb = tc % 4
        for kc in range(KC):
            w = []
            if tc < 4:
                w.append(("V_", yV[kc]))
                if kc == 0:
                    w.append(("S_", evA[31]))
                    if tc == 0:
                        w.append(("g_small", 96))
            elif kc == 0:
                w.append(("V_", v_z[tc - 4]))
            P.op("tensor", lambda t, kc=kc, pb=pb, tc=tc: t.transpose(
                out=psB[pb][:, kc * 128:(kc + 1) * 128],
                in_=G1[:, kc * T + tc * 128: kc * T + (tc + 1) * 128],
                identity=idf),
                waits=w, incs=[("T_", 1)] if kc == KC - 1 else [])
        tyT[tc] = P.now("T_")
        q = tc % 8
        w = [("T_", tyT[tc]), (f"s_x{tc % 4}", 16 * (tc // 4 + 1))]
        if tc >= 8:
            w.append(("V_", ap32_S[tc - 8]))
            w.append(("T_", x1T_quad[tc - 8]))
            w.append((f"s_x1w{(tc - 8) % 4}", 16 * (tc // 4 - 1)))
        if tc in (2, 3):
            w.append(("V_", yV[-1]))
        if (0, q) in lnV_last:
            w.append(("V_", lnV_last[(0, q)]))
        c_ss1 = lncol(0, tc, "ssum")
        P.op("vector", lambda v, pb=pb, q=q, c_ss1=c_ss1: v.scalar_tensor_tensor(
            out=zb[q], in0=psB[pb], scalar=1.0, in1=xt[tc % 4],
            op0=OP.mult, op1=OP.add, accum_out=c_ss1),
            waits=w, incs=[("V_", 1)])
        v_z[tc] = P.now("V_")
        P.op("vector", lambda v, q=tc % 4: v.tensor_tensor(
            out=xt[q], in0=xt[q], in1=bet, op=OP.add),
            waits=[("V_", v_z[tc])], incs=[("V_", 1)])
        v_xb_m[tc] = ("V_", P.now("V_"))

    def b_step_D(tc):
        par = tc % 2
        q = tc % 8
        rs1, nb1, vnb1 = ln_back(0, tc)
        wn1 = [("V_", vnb1)]
        if tc >= 2:
            wn1.append(("V_", v_w[tc - 2]))
        P.op("vector", lambda v, rs1=rs1, nb1=nb1, par=par, q=q: v.tensor_scalar(
            out=ab[par], in0=zb[q], scalar1=rs1, scalar2=nb1,
            op0=OP.mult, op1=OP.add),
            waits=wn1, incs=[("V_", 1)])
        P.op("vector", lambda v, par=par, q=q: v.tensor_tensor(
            out=zb[q], in0=ab[par], in1=gam, op=OP.mult),
            waits=[("V_", P.now("V_"))], incs=[("V_", 1)])
        wsum = [("V_", P.now("V_")), v_xb_m[tc]]
        if (1, tc % 8) in lnV_last:
            wsum.append(("V_", lnV_last[(1, tc % 8)]))
        c_ss2 = lncol(1, tc, "ssum")
        P.op("vector", lambda v, q=q, xq=tc % 4, c_ss2=c_ss2: v.scalar_tensor_tensor(
            out=zb[q], in0=zb[q], in1=xt[xq], scalar=1.0,
            op0=OP.mult, op1=OP.add, accum_out=c_ss2),
            waits=wsum, incs=[("V_", 1)])
        v_w[tc] = P.now("V_")
        if tc + 4 < TC:
            P.op("gpsimd", lambda g, tc=tc, xq=tc % 4: g.dma_start(
                out=xt[xq], in_=xN[(tc + 4) * 128:(tc + 5) * 128, :]),
                waits=[("V_", v_w[tc])], incs=[(f"s_x{tc % 4}", 16)])
            sx_cnt[tc % 4] += 16

    def b_step_G(tc):
        par = tc % 2
        q = tc % 8
        rs2, nb2, vnb2 = ln_back(1, tc)
        wap = [("V_", vnb2)]
        P.op("vector", lambda v, rs2=rs2, nb2=nb2, q=q: v.tensor_scalar(
            out=zb[q], in0=zb[q], scalar1=rs2, scalar2=nb2,
            op0=OP.mult, op1=OP.add),
            waits=wap, incs=[("V_", 1)])
        x1bf_S[tc] = P.now("V_")
        ap32_S[tc] = P.now("V_")
        P.op("sync", lambda s, tc=tc, q=q: s.dma_start(
            out=x1s[tc * 128:(tc + 1) * 128, :], in_=zb[q]),
            waits=[("V_", x1bf_S[tc])], incs=[(f"s_x1w{tc % 4}", 16)])
        for kc in range(KC):
            w = [("V_", x1bf_S[tc])] if kc == 0 else []
            if tc % 4 == 0 and len(evC) > 0 and kc == 0:
                w.append(("S_", evC[-1]))
            P.op("tensor", lambda t, kc=kc, tc=tc: t.transpose(
                out=psC[kc // 2][:, (kc % 2) * 512 + (tc % 4) * 128:
                                 (kc % 2) * 512 + (tc % 4) * 128 + 128],
                in_=zb[tc % 8][:, kc * 128:(kc + 1) * 128],
                identity=idf),
                waits=w, incs=[("T_", 1)] if kc == KC - 1 else [])
        x1T_quad[tc] = P.now("T_")
        if tc % 4 == 3:
            for kc in range(KC):
                wev = [("T_", x1T_quad[tc])] if kc == 0 else []
                if kc == 0 and len(evC) == 0:
                    wev.append(("V_", yV[-1]))
                P.op("scalar", lambda sc, kc=kc, tc=tc: sc.copy(
                    out=x1fm[:, kc * T + (tc - 3) * 128: kc * T + (tc + 1) * 128],
                    in_=psC[kc // 2][:, (kc % 2) * 512:(kc % 2) * 512 + 512]),
                    waits=wev, incs=[("S_", 1)])
            evC.append(P.now("S_"))

    assert TC % 4 == 0
    sq2_mark = [0] * TC
    wsum2_mark = [0] * TC
    NQ = TC // 4

    def bH1(qq):
        tcs = [4 * qq + j for j in range(4)]
        for tc in tcs:
            b_step_A(tc)
        for tc in tcs:
            ln_sq(0, tc, zb[tc % 8], v_z[tc], dumpB)
        ln_cm_a(0, v_z[tcs[-1]], pr0=(4 * qq) % 8, nw=4)

    def bH2(qq):
        tcs = [4 * qq + j for j in range(4)]
        ln_cm_b(0, pr0=(4 * qq) % 8, nw=4)
        for tc in tcs:
            b_step_D(tc)
        for tc in tcs:
            ln_sq(1, tc, zb[tc % 8], v_w[tc], dumpB)
            sq2_mark[tc] = P.now("S_")
        ln_cm_a(1, v_w[tcs[-1]], pr0=(4 * qq) % 8, nw=4)

    def bH3(qq):
        tcs = [4 * qq + j for j in range(4)]
        ln_cm_b(1, pr0=(4 * qq) % 8, nw=4)
        for tc in tcs:
            b_step_G(tc)

    for qq in range(NQ + 1):
        if qq < NQ:
            bH1(qq)
        if qq >= 1:
            bH3(qq - 1)
        if qq < NQ:
            bH2(qq)

    vB_done = P.now("V_")
    sB_done = P.now("S_")
    plan_wd(wd_rest, [("S_", sB_done), ("V_", vB_done), ("T_", x1T_quad[-1]),
                      ("s_x1w0", P.now("s_x1w0")), ("s_x1w1", P.now("s_x1w1")),
                      ("s_x1w2", P.now("s_x1w2")), ("s_x1w3", P.now("s_x1w3"))])

    def plan_wgu(m, waits):
        par = m % 2
        P.op("gpsimd", lambda g, m=m, par=par: g.dma_start(
            out=wg_b[par].rearrange("p (kc h) -> p kc h", kc=KC),
            in_=wgT[:, m * 128:(m + 1) * 128].rearrange("(kc p) h -> p kc h", p=128)),
            waits=waits, incs=[(f"g_wgu{par}", 16)])
        P.op("gpsimd", lambda g, m=m, par=par: g.dma_start(
            out=wu_b[par].rearrange("p (kc h) -> p kc h", kc=KC),
            in_=wuT[:, m * 128:(m + 1) * 128].rearrange("(kc p) h -> p kc h", p=128)),
            waits=[], incs=[(f"g_wgu{par}", 16)])

    plan_wgu(0, [])
    plan_wgu(1, [])
    gwgu_cnt = [32, 32]
    siluS = []
    hhV = []
    tG = []
    tU = []
    for m in range(M):
        par = m % 2
        w = [(f"g_wgu{par}", gwgu_cnt[par])]
        if m == 0:
            w += [("S_", evC[-1]), ("V_", vB_done), ("S_", sB_done)]
        if m >= 1:
            w.append(("S_", siluS[m - 1]))
        for k in range(KC):
            for ts in range(NTS):
                def mmg(t, par=par, k=k, ts=ts, m=m):
                    return t.matmul(
                        out=psG[:, ts * MV:(ts + 1) * MV],
                        lhsT=wg_b[par][:, k * 128:(k + 1) * 128],
                        rhs=x1fm[:, k * T + ts * MV: k * T + ts * MV + MV],
                        start=(k == 0), stop=(k == KC - 1))
                last = (k == KC - 1 and ts == NTS - 1)
                P.op("tensor", mmg, waits=w, incs=[("T_", 1)] if last else [])
                w = []
        tG.append(P.now("T_"))
        w = []
        if m >= 1:
            w.append(("V_", hhV[m - 1]))
        elif m == 0:
            pass
        for k in range(KC):
            for ts in range(NTS):
                def mmu(t, par=par, k=k, ts=ts, m=m):
                    return t.matmul(
                        out=psU[:, ts * MV:(ts + 1) * MV],
                        lhsT=wu_b[par][:, k * 128:(k + 1) * 128],
                        rhs=x1fm[:, k * T + ts * MV: k * T + ts * MV + MV],
                        start=(k == 0), stop=(k == KC - 1))
                last = (k == KC - 1 and ts == NTS - 1)
                P.op("tensor", mmu, waits=w, incs=[("T_", 1)] if last else [])
                w = []
        tU.append(P.now("T_"))
        if m + 2 < M:
            plan_wgu(m + 2, [("T_", tU[m])])
            gwgu_cnt[par] += 32
        w = [("T_", tG[m])]
        if m >= 2:
            w.append(("V_", hhV[m - 2]))
        if m == 0:
            w.append(("V_", vB_done))
        P.op("scalar", lambda sc, par=par: sc.activation(
            out=sg[par][:, 0:T], in_=psG[:, 0:T], func=AF.Silu, scale=QS),
            waits=w, incs=[("S_", 1)])
        siluS.append(P.now("S_"))
        w = [("T_", tU[m]), ("S_", siluS[m])]
        if m == 0:
            w.append(("T_", projT[3]))
        if m == 8:
            w.append(("T_", tyT[-1]))
        if m == 16:
            w.append(("V_", yV[-1]))
            w.append(("S_", sB_done))
        P.op("vector", lambda v, par=par, m=m: v.scalar_tensor_tensor(
            out=hh[:, m * T:(m + 1) * T], in0=psU[:, 0:T], scalar=QS,
            in1=sg[par][:, 0:T], op0=OP.mult, op1=OP.mult),
            waits=w, incs=[("V_", 1)])
        hhV.append(P.now("V_"))

    vv_marks = [0] * TC
    sx_cnt_at = {}
    ln3S = [0] * TC

    def d_step_A(tc):
        par = tc % 2
        q = tc % 4
        w = [("s_x1w0", P.now("s_x1w0")), ("s_x1w1", P.now("s_x1w1")),
             ("s_x1w2", P.now("s_x1w2")), ("s_x1w3", P.now("s_x1w3"))]
        if tc < 4:
            w.append(("V_", hhV[-1]))
        else:
            w.append(("V_", vv_marks[tc - 4]))
        P.op("gpsimd", lambda g, tc=tc, q=q: g.dma_start(
            out=xt[q], in_=x1s[tc * 128:(tc + 1) * 128, :]),
            waits=w, incs=[(f"s_x{q}", 16)])
        sx_cnt[q] += 16
        sx_cnt_at[tc] = sx_cnt[q]
        w = []
        if tc == 0:
            w += [("g_wd", 16 * 24), ("V_", hhV[-1]), ("S_", siluS[-1])]
        if tc == 1:
            w.append(("S_", siluS[-1]))
        if tc >= 2:
            w.append(("V_", vv_marks[tc - 2]))
        for m in range(M):
            for half in range(2):
                def mmd(t, m=m, half=half, par=par, tc=tc):
                    return t.matmul(
                        out=psD[par][:, half * 512:(half + 1) * 512],
                        lhsT=hh[:, m * T + tc * 128: m * T + (tc + 1) * 128],
                        rhs=wd_sb[:, m * 1024 + half * 512: m * 1024 + (half + 1) * 512],
                        start=(m == 0), stop=(m == M - 1))
                last = (m == M - 1 and half == 1)
                P.op("tensor", mmd, waits=w, incs=[("T_", 1)] if last else [])
                w = []
        t_d = P.now("T_")
        w = [("T_", t_d), (f"s_x{tc % 4}", sx_cnt_at[tc])]
        if tc >= 2:
            w.append((f"s_out{par}", 16 * (tc // 2)))
        if (2, tc % 4) in lnV_last:
            w.append(("V_", lnV_last[(2, tc % 4)]))
        c_ss3 = lncol(2, tc, "ssum")
        P.op("vector", lambda v, par=par, q=tc % 4, c_ss3=c_ss3: v.scalar_tensor_tensor(
            out=outst[par], in0=psD[par], scalar=QS, in1=xt[q],
            op0=OP.mult, op1=OP.add, accum_out=c_ss3),
            waits=w, incs=[("V_", 1)])
        vv_marks[tc] = P.now("V_")

    def d_step_G(tc):
        par = tc % 2
        rs3, nb3, vnb3 = ln_back(2, tc)
        P.op("scalar", lambda sc, rs3=rs3, nb3=nb3, par=par: sc.activation(
            out=outst[par], in_=outst[par], func=AF.Identity, bias=nb3, scale=rs3),
            waits=[("V_", vnb3)], incs=[("S_", 1)])
        ln3S[tc] = P.now("S_")
        lnS_last[(2, tc % 4)] = P.now("S_")
        P.op("sync", lambda s, tc=tc, par=par: s.dma_start(
            out=out_d[tc * 128:(tc + 1) * 128, :], in_=outst[par]),
            waits=[("S_", ln3S[tc])], incs=[(f"s_out{par}", 16)])

    for pp in range(TC // 2):
        tcs = [2 * pp, 2 * pp + 1]
        for tc in tcs:
            d_step_A(tc)
        for tc in tcs:
            ln_sq(2, tc, outst[tc % 2], vv_marks[tc], dumpC)
        ln_cm_a(2, vv_marks[tcs[-1]], pr0=(2 * pp) % 8, nw=2)
        ln_cm_b(2, pr0=(2 * pp) % 8, nw=2)
        for tc in tcs:
            d_step_G(tc)

    P.op("sync", lambda s: s.wait_ge(sems["s_out0"], 16 * ((TC + 1) // 2)),
         waits=[], incs=[])
    P.op("sync", lambda s: s.wait_ge(sems["s_out1"], 16 * (TC // 2)),
         waits=[], incs=[])

    import contextlib
    es = contextlib.ExitStack()
    sems = {}
    for nm in semnames:
        sems[nm] = es.enter_context(nc.semaphore(nm))
    block = es.enter_context(nc.Block())

    def emit(eng, name):
        for fn, waits, incs in P.items[name]:
            for s, v in waits:
                eng.wait_ge(sems[s], v)
            instr = fn(eng)
            assert instr is not None or not incs, (name, waits, incs)
            if instr is not None:
                for s, k in incs:
                    instr.then_inc(sems[s], k)

    block.sync(lambda e: emit(e, "sync"))
    block.gpsimd(lambda e: emit(e, "gpsimd"))
    block.tensor(lambda e: emit(e, "tensor"))
    block.vector(lambda e: emit(e, "vector"))
    block.scalar(lambda e: emit(e, "scalar"))
    es.close()
    arena_ctx.__exit__(None, None, None)
    ps_ctx.__exit__(None, None, None)
    return nc


def _shards(L0_attn_x):
    B, S, _ = L0_attn_x.shape
    tok = (B * S) // NCORES
    core_slices = []
    for c in range(NCORES):
        seq = c // (NCORES // B)
        off = (c % (NCORES // B)) * tok
        core_slices.append((seq, off, tok))
    return core_slices


def kernel(L0_attn_x, L0_attn_Wi, L0_attn_bi, L0_attn_Wf, L0_attn_bf,
           L0_attn_Wo, L0_attn_bo, L0_attn_Ws, L0_attn_bs,
           L0_attn_gamma, L0_attn_beta, L0_ffn_w_gate, L0_ffn_w_up,
           L0_ffn_w_down):
    B, S, Dm = L0_attn_x.shape
    assert Dm == D
    T = (B * S) // NCORES
    x = np.ascontiguousarray(L0_attn_x, dtype=np.float32)
    bias_pack = np.concatenate(
        [np.ascontiguousarray(np.asarray(b, np.float32)).reshape(8, 128).T
         for b in (L0_attn_bf, L0_attn_bi, L0_attn_bs, L0_attn_bo)], axis=1)
    bias_pack = np.ascontiguousarray(bias_pack, np.float32)
    ident = np.eye(128, dtype=np.float32)
    wgT = np.ascontiguousarray(np.asarray(L0_ffn_w_gate).T, np.int32)
    wuT = np.ascontiguousarray(np.asarray(L0_ffn_w_up).T, np.int32)
    wdT = np.ascontiguousarray(np.asarray(L0_ffn_w_down).T, np.int32)
    Wi = np.ascontiguousarray(np.asarray(L0_attn_Wi), np.float32)
    Wf = np.ascontiguousarray(np.asarray(L0_attn_Wf), np.float32)
    Wo = np.ascontiguousarray(np.asarray(L0_attn_Wo), np.float32)
    Ws = np.ascontiguousarray(np.asarray(L0_attn_Ws), np.float32)
    gam = np.ascontiguousarray(np.asarray(L0_attn_gamma), np.float32)
    bet = np.ascontiguousarray(np.asarray(L0_attn_beta), np.float32)

    nc = build_nc(T)
    in_maps = []
    for c, (seq, off, tok) in enumerate(_shards(x)):
        xs = x[seq, off:off + tok, :]
        sel = np.zeros((128, 8), np.float32)
        sel[:, c % 4] = 1.0
        in_maps.append({
            "xT": np.ascontiguousarray(xs.T),
            "xN": np.ascontiguousarray(xs),
            "Wi": Wi, "Wf": Wf, "Wo": Wo, "Ws": Ws,
            "biases": bias_pack, "gamma": gam, "beta": bet,
            "sel": sel, "ident": ident,
            "wgT": wgT, "wuT": wuT, "wdT": wdT,
        })
    res = run_bass_kernel_spmd(nc, in_maps, core_ids=list(range(NCORES)))
    out = np.empty((B, S, D), np.float32)
    for c, (seq, off, tok) in enumerate(_shards(x)):
        out[seq, off:off + tok, :] = res.results[c]["out"]
    return out


# version 28
# speedup vs baseline: 1.0144x; 1.0144x over previous
import numpy as np
import concourse.bass as bass
import concourse.mybir as mybir
from concourse.bass_utils import run_bass_kernel_spmd

F32 = mybir.dt.float32
F16 = mybir.dt.float16
BF16 = mybir.dt.bfloat16
I32 = mybir.dt.int32
AF = mybir.ActivationFunctionType
OP = mybir.AluOpType

D = 1024
HID = 3072
KC = 8
M = 24
NCORES = 8
LN_EPS = 1e-5
QS = 1.0 / 127.0

ENGINES = ("sync", "gpsimd", "tensor", "vector", "scalar")


class Planner:

    def __init__(self):
        self.items = {e: [] for e in ENGINES}
        self.cnt = {}

    def op(self, eng, fn, waits=(), incs=()):
        for s, k in incs:
            self.cnt[s] = self.cnt.get(s, 0) + k
        self.items[eng].append((fn, list(waits), list(incs)))

    def now(self, sem):
        return self.cnt.get(sem, 0)


def build_nc(T):
    assert T % 128 == 0
    TC = T // 128
    MV = min(512, T)
    NTS = T // MV
    nc = bass.Bass("TRN2", target_bir_lowering=False)

    xT = nc.dram_tensor("xT", [D, T], F32, kind="ExternalInput").ap()
    xN = nc.dram_tensor("xN", [T, D], F32, kind="ExternalInput").ap()
    Wi = nc.dram_tensor("Wi", [D, D], F32, kind="ExternalInput").ap()
    Wf = nc.dram_tensor("Wf", [D, D], F32, kind="ExternalInput").ap()
    Wo = nc.dram_tensor("Wo", [D, D], F32, kind="ExternalInput").ap()
    Ws = nc.dram_tensor("Ws", [D, D], F32, kind="ExternalInput").ap()
    biases = nc.dram_tensor("biases", [128, 32], F32, kind="ExternalInput").ap()
    gamma = nc.dram_tensor("gamma", [D], F32, kind="ExternalInput").ap()
    beta = nc.dram_tensor("beta", [D], F32, kind="ExternalInput").ap()
    sel = nc.dram_tensor("sel", [128, 8], F32, kind="ExternalInput").ap()
    ident = nc.dram_tensor("ident", [128, 128], F32, kind="ExternalInput").ap()
    wgT = nc.dram_tensor("wgT", [D, HID], I32, kind="ExternalInput").ap()
    wuT = nc.dram_tensor("wuT", [D, HID], I32, kind="ExternalInput").ap()
    wdT = nc.dram_tensor("wdT", [HID, D], I32, kind="ExternalInput").ap()
    out_d = nc.dram_tensor("out", [T, D], F32, kind="ExternalOutput").ap()
    cc_in = nc.dram_tensor("cc_in", [128, 16], F32, kind="Internal").ap()
    cc_out = nc.dram_tensor("cc_out", [4, 128, 16], F32, kind="Internal",
                            addr_space="Local").ap()
    x1s = nc.dram_tensor("x1s", [T, D], F16, kind="Internal").ap()

    GW = 4 * T
    O_XFM = 0
    O_G1 = GW
    O_G2 = 2 * GW
    O_G3 = 3 * GW
    O_Q = 4 * GW
    QW_W0 = O_Q
    QW_W1 = O_Q + 4096
    QW_SCR = O_Q + 8192
    QW_Z = QW_SCR + max(T // 2, 1024)
    QW_N1 = QW_Z + 1024
    Q_END = QW_N1 + 1024
    O_WD = O_Q
    O_X1BF = Q_END
    assert Q_END + 1024 - O_Q <= 12288
    O_P = O_Q + 12288
    P_GAM = O_P
    P_BET = O_P + 512
    P_BIA = O_P + 1024
    P_SEL = P_BIA + 32
    P_IDF = P_SEL + 8
    P_IDB = P_IDF + 64
    P_SUM = P_IDB + 64
    P_GATH = P_SUM + 16
    P_HRUN = P_GATH + 128
    P_HSELA = P_HRUN + 8
    P_HSELB = P_HSELA + 8
    P_COLS = P_HSELB + 8
    P_NH = P_COLS + 144
    P_SL = P_NH + 8
    P_XT = P_SL + 8
    P_OUTST = P_XT + 2048
    P_WGU = P_OUTST + 2048
    P_END = P_WGU + 2048
    NWORDS = P_END
    assert NWORDS <= 53000, NWORDS

    P = Planner()
    semnames = ([f"g_xf{k}" for k in range(KC)]
                + [f"g_wA{m}" for m in range(8)] + [f"g_wB{m}" for m in range(8)]
                + ["g_w0", "g_w1", "g_small", "g_gb", "g_wd", "r_bc", "l_bc", "g_wgu0",
                "g_wgu1", "s_x0", "s_x1", "s_x2", "s_x3", "s_cc", "s_x1w0", "s_x1w1", "s_x1w2", "s_x1w3", "s_wf0", "s_wf1", "s_out0", "s_out1",
                "T_", "V_", "S_", "G_"])

    arena_ctx = nc.sbuf_tensor("AR", [128, NWORDS], F32)
    AR = arena_ctx.__enter__()
    ps_ctx = nc.psum_tensor("PS", [128, 4096], F32)
    PS = ps_ctx.__enter__()
    ar = AR[:]
    ar16 = ar.bitcast(F16)
    arbf = ar.bitcast(BF16)
    ps = PS[:]
    ps16 = ps.bitcast(F16)
    psbf = ps.bitcast(BF16)

    def f32v(off, n):
        return ar[:, off:off + n]

    def f16v(off, n):
        return ar16[:, 2 * off: 2 * off + n]

    def bf16v(off, n):
        return arbf[:, 2 * off: 2 * off + n]

    xfm = bf16v(O_XFM, 8 * T)
    G1 = f16v(O_G1, 8 * T)
    G2 = f16v(O_G2, 8 * T)
    G3 = f16v(O_G3, 8 * T)
    x1fm = f16v(O_G3, 8 * T)
    hh = bf16v(O_XFM, 24 * T)
    Wb = [bf16v(QW_W0, 8 * 1024), bf16v(QW_W1, 8 * 1024)]
    scr = f16v(QW_SCR, T)
    zb = [f16v(QW_Z, 1024), f16v(QW_Z + 512, 1024),
          f16v(QW_SCR, 1024), f16v(QW_SCR + 512, 1024),
          f16v(P_OUTST, 1024), f16v(P_OUTST + 512, 1024),
          f16v(P_OUTST + 1024, 1024), f16v(P_OUTST + 1536, 1024)]
    ab = [f16v(QW_N1, 1024), f16v(QW_N1 + 512, 1024)]
    wd_sb = bf16v(O_WD, 24 * 1024)
    x1bf = [f16v(O_X1BF, 1024), f16v(O_X1BF + 512, 1024)]
    gam = f16v(P_GAM, 1024)
    bet = f16v(P_BET, 1024)
    bia = f32v(P_BIA, 32)
    selv = f32v(P_SEL, 8)
    idf = f16v(P_IDF, 128)
    idb = bf16v(P_IDB, 128)
    summ = f32v(P_SUM, 16)
    gath = f32v(P_GATH, 128)
    Hrun = f32v(P_HRUN, 8)
    HselA = f32v(P_HSELA, 8)
    HselB = f32v(P_HSELB, 8)
    nh = f32v(P_NH, 1)
    nh4 = f32v(P_NH, 4)
    sumlog = f32v(P_SL, 8)

    def col(i):
        return f32v(P_COLS + i, 1)

    xt = [f16v(P_XT, 1024), f16v(P_XT + 512, 1024),
          f16v(P_XT + 1024, 1024), f16v(P_XT + 1536, 1024)]
    sg = [f16v(P_XT, T), f16v(P_XT + 1024, T)]
    outst = [f32v(P_OUTST, 1024), f32v(P_OUTST + 1024, 1024)]
    wg_b = [bf16v(P_WGU, 1024), bf16v(P_WGU + 512, 1024)]
    wu_b = [bf16v(P_WGU + 1024, 1024), bf16v(P_WGU + 1536, 1024)]

    def grpA(par):
        return ps[:, par * 2048: par * 2048 + T]

    psB = [ps16[:, 0:1024], ps16[:, 1024:2048],
           ps16[:, 6144:7168], ps16[:, 7168:8192]]
    dumpB = G2[:, 0:1024]
    psC = [ps16[:, (2 + j) * 1024:(3 + j) * 1024] for j in range(4)]
    psG = ps[:, 0:T]
    psU = ps[:, 2048:2048 + T]
    psD = [ps[:, 0:1024], ps[:, 1024:2048]]
    dumpC = ps[:, 2048:3072]

    for fn in [
        lambda g: g.dma_start(out=bia, in_=biases),
        lambda g: g.dma_start(out=selv, in_=sel),
        lambda g: g.dma_start(out=idf, in_=ident),
        lambda g: g.dma_start(out=idb, in_=ident),
        lambda g: g.dma_start(out=gam, in_=gamma.partition_broadcast(128)),
        lambda g: g.dma_start(out=bet, in_=beta.partition_broadcast(128)),
    ]:
        P.op("gpsimd", fn, incs=[("g_small", 16)])
    staged_wf = (4 * T >= 8192)
    if staged_wf:
        for h in range(2):
            P.op("sync", lambda s, h=h: s.dma_start(
                out=f32v(O_G3 + h * 4096, 4096).rearrange("p (kc e) -> p kc e", kc=4),
                in_=Wf[h * 512:(h + 1) * 512, :].rearrange("(kc p) e -> p kc e", p=128)),
                incs=[(f"s_wf{h}", 16)])
        for h in range(2):
            P.op("vector", lambda v, h=h: v.tensor_copy(
                out=Wb[0][:, h * 4096:(h + 1) * 4096],
                in_=f32v(O_G3 + h * 4096, 4096)),
                waits=[(f"s_wf{h}", 16)], incs=[("V_", 1)])
    else:
        P.op("gpsimd", lambda g: g.dma_start(
            out=Wb[0].rearrange("p (kc e) -> p kc e", kc=KC),
            in_=Wf.rearrange("(kc p) e -> p kc e", p=128)), incs=[("g_w0", 16)])
    v_wf = P.now("V_")
    for k in range(KC):
        P.op("gpsimd", lambda g, k=k: g.dma_start(
            out=xfm[:, k * T:(k + 1) * T],
            in_=xT[k * 128:(k + 1) * 128, :]), incs=[(f"g_xf{k}", 16)])
    P.op("gpsimd", lambda g: g.dma_start(
        out=Wb[1].rearrange("p (kc e) -> p kc e", kc=KC),
        in_=Wi.rearrange("(kc p) e -> p kc e", p=128)), incs=[("g_w1", 16)])
    P.op("gpsimd", lambda g: g.memset(nh4[:], -0.5), incs=[("G_", 1)])

    for j in range(min(4, TC)):
        P.op("gpsimd", lambda g, j=j: g.dma_start(
            out=xt[j], in_=xN[j * 128:(j + 1) * 128, :]),
            incs=[(f"s_x{j}", 16)])
    sx_cnt = [16, 16, 16, 16]

    projs = [
        (0, AF.Sigmoid, G1),
        (1, AF.Sigmoid, G3),
        (0, AF.Tanh, G2),
        (1, AF.Sigmoid, G2),
    ]
    evA = []
    uV = []
    projT = []
    def plan_proj(p):
        slot, act, dst = projs[p]
        waits = []
        if p == 0:
            waits = [("V_", v_wf)] if staged_wf else [("g_w0", 16)]
        elif p == 1:
            waits = [("g_w1", 16)]
        elif p == 2:
            waits = [("g_w0", 16 if staged_wf else 32)]
        elif p == 3:
            waits = [("g_w1", 32)]
        for m in range(8):
            gidx = p * 8 + m
            par = gidx % 2
            w = list(waits) if m == 0 else []
            if gidx >= 2:
                w.append(("S_", evA[gidx - 2]))
            if p == 3:
                pass
            for k in range(KC):
                if p == 0 and m == 0:
                    w.append((f"g_xf{k}", 16))
                for ts in range(NTS):
                    def mm(t, par=par, k=k, ts=ts, slot=slot, m=m):
                        return t.matmul(
                            out=grpA(par)[:, ts * MV:(ts + 1) * MV],
                            lhsT=Wb[slot][:, k * 1024 + m * 128: k * 1024 + (m + 1) * 128],
                            rhs=xfm[:, k * T + ts * MV: k * T + ts * MV + MV],
                            start=(k == 0), stop=(k == KC - 1))
                    last = (k == KC - 1 and ts == NTS - 1)
                    P.op("tensor", mm, waits=w, incs=[("T_", 1)] if last else [])
                    w = []
            tdone = P.now("T_")
            ew = [("T_", tdone)]
            if p == 3:
                ew.append(("V_", uV[m]))

            def ev(sc, par=par, act=act, dst=dst, m=m, p=p):
                return sc.activation(out=dst[:, m * T:(m + 1) * T],
                                     in_=grpA(par)[:, 0:T], func=act,
                                     bias=bia[:, p * 8 + m: p * 8 + m + 1])
            if p == 0 and m == 0:
                ew.append(("g_small", 96))
            if p == 1 and m == 0 and staged_wf:
                ew.append(("V_", v_wf))
            P.op("scalar", ev, waits=ew, incs=[("S_", 1)])
            evA.append(P.now("S_"))
        projT.append(P.now("T_"))
        if p == 0:
            P.op("gpsimd", lambda g: g.dma_start(
                out=Wb[0].rearrange("p (kc e) -> p kc e", kc=KC),
                in_=Ws.rearrange("(kc p) e -> p kc e", p=128)),
                waits=[("T_", projT[0])], incs=[("g_w0", 16)])
        if p == 1:
            P.op("gpsimd", lambda g: g.dma_start(
                out=Wb[1].rearrange("p (kc e) -> p kc e", kc=KC),
                in_=Wo.rearrange("(kc p) e -> p kc e", p=128)),
                waits=[("T_", projT[1])], incs=[("g_w1", 16)])


    plan_proj(0)
    plan_proj(1)
    plan_proj(2)

    lndump = f16v(O_X1BF, 2 * T)
    for kc in range(KC):
        P.op("scalar", lambda sc, kc=kc: sc.activation(
            out=lndump[:, 0:T], in_=G1[:, kc * T:(kc + 1) * T], func=AF.Ln,
            accum_out=sumlog[:, kc:kc + 1]),
            waits=[("S_", P.now("S_"))], incs=[("S_", 1)])
    P.op("scalar", lambda sc: sc.activation(
        out=summ[:, 0:8], in_=sumlog, func=AF.Exp),
        waits=[("S_", P.now("S_"))], incs=[("S_", 1)])
    s_atot = P.now("S_")
    for kc in range(KC):
        P.op("vector", lambda v, kc=kc: v.tensor_tensor(
            out=G3[:, kc * T:(kc + 1) * T], in0=G3[:, kc * T:(kc + 1) * T],
            in1=G2[:, kc * T:(kc + 1) * T], op=OP.mult),
            waits=[("S_", evA[16 + kc]), ("V_", P.now("V_"))], incs=[("V_", 1)])
        uV.append(P.now("V_"))
        P.op("vector", lambda v, kc=kc: v.tensor_tensor_scan(
            out=scr, data0=G1[:, kc * T:(kc + 1) * T],
            data1=G3[:, kc * T:(kc + 1) * T], initial=0.0,
            op0=OP.mult, op1=OP.add),
            waits=[("S_", evA[kc]), ("V_", P.now("V_"))], incs=[("V_", 1)])
        P.op("vector", lambda v, kc=kc: v.tensor_copy(
            out=summ[:, 8 + kc:9 + kc], in_=scr[:, T - 1:T]),
            waits=[("V_", P.now("V_"))], incs=[("V_", 1)])
    v_summ = P.now("V_")
    plan_proj(3)


    P.op("sync", lambda s: s.dma_start(out=cc_in, in_=summ),
         waits=[("V_", v_summ), ("S_", s_atot)], incs=[("s_cc", 16)])
    P.op("gpsimd", lambda g: g.collective_compute(
        "AllGather", OP.bypass, replica_groups=[[0, 1, 2, 3], [4, 5, 6, 7]],
        ins=[cc_in], outs=[cc_out]),
        waits=[("s_cc", 16)], incs=[("G_", 1)])
    P.op("gpsimd", lambda g: g.dma_start(
        out=gath[:, 0:64].rearrange("p (r c) -> p r c", r=4),
        in_=cc_out.rearrange("r p c -> p r c")),
        waits=[("G_", P.now("G_"))], incs=[("g_gb", 16)])

    P.op("vector", lambda v: v.memset(Hrun[:], 0.0),
         waits=[("g_gb", 16), ("g_small", 96)], incs=[("V_", 1)])
    P.op("vector", lambda v: v.memset(HselA[:], 0.0),
         waits=[("V_", P.now("V_"))], incs=[("V_", 1)])
    hsel_cur, hsel_alt = HselA, HselB
    for r in range(4):
        P.op("vector", lambda v, r=r, a=hsel_alt, c=hsel_cur: v.scalar_tensor_tensor(
            out=a, in0=Hrun[:], scalar=selv[:, r:r + 1], in1=c,
            op0=OP.mult, op1=OP.add),
            waits=[("V_", P.now("V_"))], incs=[("V_", 1)])
        hsel_cur, hsel_alt = hsel_alt, hsel_cur
        if r < 3:
            P.op("vector", lambda v, r=r: v.tensor_tensor(
                out=Hrun[:], in0=Hrun[:], in1=gath[:, r * 16:r * 16 + 8], op=OP.mult),
                waits=[("V_", P.now("V_"))], incs=[("V_", 1)])
            P.op("vector", lambda v, r=r: v.tensor_tensor(
                out=Hrun[:], in0=Hrun[:], in1=gath[:, r * 16 + 8:r * 16 + 16], op=OP.add),
                waits=[("V_", P.now("V_"))], incs=[("V_", 1)])
    Hfin = hsel_cur

    yV = []
    for kc in range(KC):
        P.op("vector", lambda v, kc=kc: v.tensor_tensor_scan(
            out=scr, data0=G1[:, kc * T:(kc + 1) * T],
            data1=G3[:, kc * T:(kc + 1) * T], initial=Hfin[:, kc:kc + 1],
            op0=OP.mult, op1=OP.add),
            waits=[("V_", P.now("V_"))], incs=[("V_", 1)])
        P.op("vector", lambda v, kc=kc: v.tensor_tensor(
            out=G1[:, kc * T:(kc + 1) * T], in0=G2[:, kc * T:(kc + 1) * T],
            in1=scr, op=OP.mult),
            waits=[("V_", P.now("V_")), ("S_", evA[24 + kc])], incs=[("V_", 1)])
        yV.append(P.now("V_"))

    scr_words = max(T // 2, 1024)
    n_scr_ch = scr_words // 512
    wd_cnt = [0]

    def plan_wd(mlist, waits):
        for mi in mlist:
            P.op("gpsimd", lambda g, mi=mi: g.dma_start(
                out=wd_sb[:, mi * 1024:(mi + 1) * 1024],
                in_=wdT[mi * 128:(mi + 1) * 128, :]),
                waits=waits if mi == mlist[0] else [], incs=[("g_wd", 16)])
            wd_cnt[0] += 1

    plan_wd(list(range(0, 16)), [("T_", projT[3])])
    wd_rest = list(range(16, 24))

    colnames = ("s2", "ssum", "mu", "var", "rstd", "nb")
    colix = {}
    for stg in range(3):
        for ni, name in enumerate(colnames):
            for pr in range(8):
                colix[(stg, pr, name)] = stg * 48 + ni * 8 + pr
    lnV_last = {}
    lnS_last = {}
    tyT = [0] * TC
    v_z = [0] * TC
    v_w = [0] * TC
    v_xb_m = [0] * TC
    x1bf_S = [0] * TC
    ap32_S = [0] * TC
    x1T_quad = [0] * TC
    evC = []
    sq_mark = {}

    def lncol(stage, tc, name):
        return col(colix[(stage, tc % 8, name)])

    def ln_sq(stage, tc, buf, vwait, dump):
        key = (stage, tc % 8)
        c_s2 = lncol(stage, tc, "s2")
        wsq = [("V_", vwait), ("S_", P.now("S_"))]
        if stage < 2:
            wsq.append(("V_", yV[-1]))
        if key in lnV_last:
            wsq.append(("V_", lnV_last[key]))
        P.op("scalar", lambda sc, buf=buf, c_s2=c_s2, dump=dump: sc.activation(
            out=dump, in_=buf, func=AF.Square, accum_out=c_s2),
            waits=wsq, incs=[("S_", 1)])
        sq_mark[key] = P.now("S_")

    cm_gp = {}

    def ln_cm_a(stage, vwait, pr0=0, nw=4):
        def c4(name):
            ni = colnames.index(name)
            return f32v(P_COLS + stage * 48 + ni * 8 + pr0, nw)
        ss4, mu4, va4, rs4 = c4("ssum"), c4("mu"), c4("var"), c4("rstd")
        s24 = c4("s2")
        P.op("vector", lambda v: v.tensor_tensor(
            out=mu4, in0=ss4, in1=ss4, op=OP.mult),
            waits=[("V_", vwait)], incs=[("V_", 1)])
        P.op("vector", lambda v: v.scalar_tensor_tensor(
            out=va4, in0=mu4, scalar=-1.0 / D, in1=s24, op0=OP.mult, op1=OP.add),
            waits=[("V_", P.now("V_")), ("S_", sq_mark[(stage, pr0 + nw - 1)])],
            incs=[("V_", 1)])
        P.op("vector", lambda v: v.tensor_scalar(
            out=va4, in0=va4, scalar1=1.0 / D, scalar2=LN_EPS,
            op0=OP.mult, op1=OP.add),
            waits=[("V_", P.now("V_"))], incs=[("V_", 1)])
        vv = P.now("V_")
        if stage < 2:
            P.op("vector", lambda v: v.reciprocal(out=rs4, in_=va4),
                 waits=[("V_", vv)], incs=[("V_", 1)])
            P.op("scalar", lambda sc: sc.activation(
                out=rs4, in_=rs4, func=AF.Sqrt),
                waits=[("V_", P.now("V_")), ("S_", P.now("S_"))], incs=[("S_", 1)])
            cm_gp[(stage, pr0)] = ("S_", P.now("S_"))
        else:
            P.op("gpsimd", lambda g: g.tensor_tensor(
                out=rs4, in0=va4, in1=nh4[:, 0:nw], op=OP.pow),
                waits=[("V_", vv), ("G_", 1)], incs=[("G_", 1)])
            cm_gp[(stage, pr0)] = ("G_", P.now("G_"))

    def ln_cm_b(stage, pr0=0, nw=4):
        def c4(name):
            ni = colnames.index(name)
            return f32v(P_COLS + stage * 48 + ni * 8 + pr0, nw)
        ss4, rs4, nb4 = c4("ssum"), c4("rstd"), c4("nb")
        P.op("vector", lambda v: v.scalar_tensor_tensor(
            out=nb4, in0=ss4, scalar=-1.0 / D, in1=rs4, op0=OP.mult, op1=OP.mult),
            waits=[cm_gp[(stage, pr0)], ("V_", P.now("V_"))],
            incs=[("V_", 1)])
        for pr in range(pr0, pr0 + nw):
            lnV_last[(stage, pr)] = P.now("V_")

    def ln_back(stage, tc):
        return (lncol(stage, tc, "rstd"), lncol(stage, tc, "nb"),
                lnV_last[(stage, tc % 8)])

    def b_step_A(tc):
        pb = tc % 4
        for kc in range(KC):
            w = []
            if tc < 4:
                w.append(("V_", yV[kc]))
                if kc == 0:
                    w.append(("S_", evA[31]))
                    if tc == 0:
                        w.append(("g_small", 96))
            elif kc == 0:
                w.append(("V_", v_z[tc - 4]))
            P.op("tensor", lambda t, kc=kc, pb=pb, tc=tc: t.transpose(
                out=psB[pb][:, kc * 128:(kc + 1) * 128],
                in_=G1[:, kc * T + tc * 128: kc * T + (tc + 1) * 128],
                identity=idf),
                waits=w, incs=[("T_", 1)] if kc == KC - 1 else [])
        tyT[tc] = P.now("T_")
        q = tc % 8
        w = [("T_", tyT[tc]), (f"s_x{tc % 4}", 16 * (tc // 4 + 1))]
        if tc >= 8:
            w.append(("V_", ap32_S[tc - 8]))
            w.append(("T_", x1T_quad[tc - 8]))
            w.append((f"s_x1w{(tc - 8) % 4}", 16 * (tc // 4 - 1)))
        if tc in (2, 3):
            w.append(("V_", yV[-1]))
        if (0, q) in lnV_last:
            w.append(("V_", lnV_last[(0, q)]))
        c_ss1 = lncol(0, tc, "ssum")
        P.op("vector", lambda v, pb=pb, q=q, c_ss1=c_ss1: v.scalar_tensor_tensor(
            out=zb[q], in0=psB[pb], scalar=1.0, in1=xt[tc % 4],
            op0=OP.mult, op1=OP.add, accum_out=c_ss1),
            waits=w, incs=[("V_", 1)])
        v_z[tc] = P.now("V_")
        P.op("vector", lambda v, q=tc % 4: v.tensor_tensor(
            out=xt[q], in0=xt[q], in1=bet, op=OP.add),
            waits=[("V_", v_z[tc])], incs=[("V_", 1)])
        v_xb_m[tc] = ("V_", P.now("V_"))

    def b_step_D(tc):
        par = tc % 2
        q = tc % 8
        rs1, nb1, vnb1 = ln_back(0, tc)
        wn1 = [("V_", vnb1)]
        if tc >= 2:
            wn1.append(("V_", v_w[tc - 2]))
        P.op("vector", lambda v, rs1=rs1, nb1=nb1, par=par, q=q: v.tensor_scalar(
            out=ab[par], in0=zb[q], scalar1=rs1, scalar2=nb1,
            op0=OP.mult, op1=OP.add),
            waits=wn1, incs=[("V_", 1)])
        P.op("vector", lambda v, par=par, q=q: v.tensor_tensor(
            out=zb[q], in0=ab[par], in1=gam, op=OP.mult),
            waits=[("V_", P.now("V_"))], incs=[("V_", 1)])
        wsum = [("V_", P.now("V_")), v_xb_m[tc]]
        if (1, tc % 8) in lnV_last:
            wsum.append(("V_", lnV_last[(1, tc % 8)]))
        c_ss2 = lncol(1, tc, "ssum")
        P.op("vector", lambda v, q=q, xq=tc % 4, c_ss2=c_ss2: v.scalar_tensor_tensor(
            out=zb[q], in0=zb[q], in1=xt[xq], scalar=1.0,
            op0=OP.mult, op1=OP.add, accum_out=c_ss2),
            waits=wsum, incs=[("V_", 1)])
        v_w[tc] = P.now("V_")
        if tc + 4 < TC:
            P.op("gpsimd", lambda g, tc=tc, xq=tc % 4: g.dma_start(
                out=xt[xq], in_=xN[(tc + 4) * 128:(tc + 5) * 128, :]),
                waits=[("V_", v_w[tc])], incs=[(f"s_x{tc % 4}", 16)])
            sx_cnt[tc % 4] += 16

    def b_step_G(tc):
        par = tc % 2
        q = tc % 8
        rs2, nb2, vnb2 = ln_back(1, tc)
        wap = [("V_", vnb2)]
        P.op("vector", lambda v, rs2=rs2, nb2=nb2, q=q: v.tensor_scalar(
            out=zb[q], in0=zb[q], scalar1=rs2, scalar2=nb2,
            op0=OP.mult, op1=OP.add),
            waits=wap, incs=[("V_", 1)])
        x1bf_S[tc] = P.now("V_")
        ap32_S[tc] = P.now("V_")
        P.op("sync", lambda s, tc=tc, q=q: s.dma_start(
            out=x1s[tc * 128:(tc + 1) * 128, :], in_=zb[q]),
            waits=[("V_", x1bf_S[tc])], incs=[(f"s_x1w{tc % 4}", 16)])
        for kc in range(KC):
            w = [("V_", x1bf_S[tc])] if kc == 0 else []
            if tc % 4 == 0 and len(evC) > 0 and kc == 0:
                w.append(("S_", evC[-1]))
            P.op("tensor", lambda t, kc=kc, tc=tc: t.transpose(
                out=psC[kc // 2][:, (kc % 2) * 512 + (tc % 4) * 128:
                                 (kc % 2) * 512 + (tc % 4) * 128 + 128],
                in_=zb[tc % 8][:, kc * 128:(kc + 1) * 128],
                identity=idf),
                waits=w, incs=[("T_", 1)] if kc == KC - 1 else [])
        x1T_quad[tc] = P.now("T_")
        if tc % 4 == 3:
            for kc in range(KC):
                wev = [("T_", x1T_quad[tc])] if kc == 0 else []
                if kc == 0 and len(evC) == 0:
                    wev.append(("V_", yV[-1]))
                P.op("scalar", lambda sc, kc=kc, tc=tc: sc.copy(
                    out=x1fm[:, kc * T + (tc - 3) * 128: kc * T + (tc + 1) * 128],
                    in_=psC[kc // 2][:, (kc % 2) * 512:(kc % 2) * 512 + 512]),
                    waits=wev, incs=[("S_", 1)])
            evC.append(P.now("S_"))

    assert TC % 4 == 0
    sq2_mark = [0] * TC
    wsum2_mark = [0] * TC
    NQ = TC // 4

    def bH1(qq):
        tcs = [4 * qq + j for j in range(4)]
        for tc in tcs:
            b_step_A(tc)
        for tc in tcs:
            ln_sq(0, tc, zb[tc % 8], v_z[tc], dumpB)
        ln_cm_a(0, v_z[tcs[-1]], pr0=(4 * qq) % 8, nw=4)

    def bH2(qq):
        tcs = [4 * qq + j for j in range(4)]
        ln_cm_b(0, pr0=(4 * qq) % 8, nw=4)
        for tc in tcs:
            b_step_D(tc)
        for tc in tcs:
            ln_sq(1, tc, zb[tc % 8], v_w[tc], dumpB)
            sq2_mark[tc] = P.now("S_")
        ln_cm_a(1, v_w[tcs[-1]], pr0=(4 * qq) % 8, nw=4)

    def bH3(qq):
        tcs = [4 * qq + j for j in range(4)]
        ln_cm_b(1, pr0=(4 * qq) % 8, nw=4)
        for tc in tcs:
            b_step_G(tc)

    for qq in range(NQ + 1):
        if qq < NQ:
            bH1(qq)
        if qq >= 1:
            bH3(qq - 1)
        if qq < NQ:
            bH2(qq)

    vB_done = P.now("V_")
    sB_done = P.now("S_")
    plan_wd(wd_rest, [("S_", sB_done), ("V_", vB_done), ("T_", x1T_quad[-1]),
                      ("s_x1w0", P.now("s_x1w0")), ("s_x1w1", P.now("s_x1w1")),
                      ("s_x1w2", P.now("s_x1w2")), ("s_x1w3", P.now("s_x1w3"))])

    def plan_wgu(m, waits):
        par = m % 2
        P.op("gpsimd", lambda g, m=m, par=par: g.dma_start(
            out=wg_b[par].rearrange("p (kc h) -> p kc h", kc=KC),
            in_=wgT[:, m * 128:(m + 1) * 128].rearrange("(kc p) h -> p kc h", p=128)),
            waits=waits, incs=[(f"g_wgu{par}", 16)])
        P.op("gpsimd", lambda g, m=m, par=par: g.dma_start(
            out=wu_b[par].rearrange("p (kc h) -> p kc h", kc=KC),
            in_=wuT[:, m * 128:(m + 1) * 128].rearrange("(kc p) h -> p kc h", p=128)),
            waits=[], incs=[(f"g_wgu{par}", 16)])

    plan_wgu(0, [])
    plan_wgu(1, [])
    gwgu_cnt = [32, 32]
    siluS = []
    hhV = []
    tG = []
    tU = []
    for m in range(M):
        par = m % 2
        w = [(f"g_wgu{par}", gwgu_cnt[par])]
        if m == 0:
            w += [("S_", evC[-1]), ("V_", vB_done), ("S_", sB_done)]
        if m >= 1:
            w.append(("S_", siluS[m - 1]))
        for k in range(KC):
            for ts in range(NTS):
                def mmg(t, par=par, k=k, ts=ts, m=m):
                    return t.matmul(
                        out=psG[:, ts * MV:(ts + 1) * MV],
                        lhsT=wg_b[par][:, k * 128:(k + 1) * 128],
                        rhs=x1fm[:, k * T + ts * MV: k * T + ts * MV + MV],
                        start=(k == 0), stop=(k == KC - 1))
                last = (k == KC - 1 and ts == NTS - 1)
                P.op("tensor", mmg, waits=w, incs=[("T_", 1)] if last else [])
                w = []
        tG.append(P.now("T_"))
        w = []
        if m >= 1:
            w.append(("V_", hhV[m - 1]))
        elif m == 0:
            pass
        for k in range(KC):
            for ts in range(NTS):
                def mmu(t, par=par, k=k, ts=ts, m=m):
                    return t.matmul(
                        out=psU[:, ts * MV:(ts + 1) * MV],
                        lhsT=wu_b[par][:, k * 128:(k + 1) * 128],
                        rhs=x1fm[:, k * T + ts * MV: k * T + ts * MV + MV],
                        start=(k == 0), stop=(k == KC - 1))
                last = (k == KC - 1 and ts == NTS - 1)
                P.op("tensor", mmu, waits=w, incs=[("T_", 1)] if last else [])
                w = []
        tU.append(P.now("T_"))
        if m + 2 < M:
            plan_wgu(m + 2, [("T_", tU[m])])
            gwgu_cnt[par] += 32
        w = [("T_", tG[m])]
        if m >= 2:
            w.append(("V_", hhV[m - 2]))
        if m == 0:
            w.append(("V_", vB_done))
        P.op("scalar", lambda sc, par=par: sc.activation(
            out=sg[par][:, 0:T], in_=psG[:, 0:T], func=AF.Silu, scale=QS),
            waits=w, incs=[("S_", 1)])
        siluS.append(P.now("S_"))
        w = [("T_", tU[m]), ("S_", siluS[m])]
        if m == 0:
            w.append(("T_", projT[3]))
        if m == 8:
            w.append(("T_", tyT[-1]))
        if m == 16:
            w.append(("V_", yV[-1]))
            w.append(("S_", sB_done))
        P.op("vector", lambda v, par=par, m=m: v.scalar_tensor_tensor(
            out=hh[:, m * T:(m + 1) * T], in0=psU[:, 0:T], scalar=QS,
            in1=sg[par][:, 0:T], op0=OP.mult, op1=OP.mult),
            waits=w, incs=[("V_", 1)])
        hhV.append(P.now("V_"))

    vv_marks = [0] * TC
    sx_cnt_at = {}
    ln3S = [0] * TC

    def d_step_A(tc):
        par = tc % 2
        q = tc % 4
        w = [("s_x1w0", P.now("s_x1w0")), ("s_x1w1", P.now("s_x1w1")),
             ("s_x1w2", P.now("s_x1w2")), ("s_x1w3", P.now("s_x1w3"))]
        if tc < 4:
            w.append(("V_", hhV[-1]))
        else:
            w.append(("V_", vv_marks[tc - 4]))
        P.op("gpsimd", lambda g, tc=tc, q=q: g.dma_start(
            out=xt[q], in_=x1s[tc * 128:(tc + 1) * 128, :]),
            waits=w, incs=[(f"s_x{q}", 16)])
        sx_cnt[q] += 16
        sx_cnt_at[tc] = sx_cnt[q]
        w = []
        if tc == 0:
            w += [("g_wd", 16 * 24), ("V_", hhV[-1]), ("S_", siluS[-1])]
        if tc == 1:
            w.append(("S_", siluS[-1]))
        if tc >= 2:
            w.append(("V_", vv_marks[tc - 2]))
        for m in range(M):
            for half in range(2):
                def mmd(t, m=m, half=half, par=par, tc=tc):
                    return t.matmul(
                        out=psD[par][:, half * 512:(half + 1) * 512],
                        lhsT=hh[:, m * T + tc * 128: m * T + (tc + 1) * 128],
                        rhs=wd_sb[:, m * 1024 + half * 512: m * 1024 + (half + 1) * 512],
                        start=(m == 0), stop=(m == M - 1))
                last = (m == M - 1 and half == 1)
                P.op("tensor", mmd, waits=w, incs=[("T_", 1)] if last else [])
                w = []
        t_d = P.now("T_")
        w = [("T_", t_d), (f"s_x{tc % 4}", sx_cnt_at[tc])]
        if tc >= 2:
            w.append((f"s_out{par}", 16 * (tc // 2)))
        if (2, tc % 4) in lnV_last:
            w.append(("V_", lnV_last[(2, tc % 4)]))
        c_ss3 = lncol(2, tc, "ssum")
        P.op("vector", lambda v, par=par, q=tc % 4, c_ss3=c_ss3: v.scalar_tensor_tensor(
            out=outst[par], in0=psD[par], scalar=QS, in1=xt[q],
            op0=OP.mult, op1=OP.add, accum_out=c_ss3),
            waits=w, incs=[("V_", 1)])
        vv_marks[tc] = P.now("V_")

    def d_step_G(tc):
        par = tc % 2
        rs3, nb3, vnb3 = ln_back(2, tc)
        P.op("scalar", lambda sc, rs3=rs3, nb3=nb3, par=par: sc.activation(
            out=outst[par], in_=outst[par], func=AF.Identity, bias=nb3, scale=rs3),
            waits=[("V_", vnb3)], incs=[("S_", 1)])
        ln3S[tc] = P.now("S_")
        lnS_last[(2, tc % 4)] = P.now("S_")
        P.op("sync", lambda s, tc=tc, par=par: s.dma_start(
            out=out_d[tc * 128:(tc + 1) * 128, :], in_=outst[par]),
            waits=[("S_", ln3S[tc])], incs=[(f"s_out{par}", 16)])

    for pp in range(TC // 2):
        tcs = [2 * pp, 2 * pp + 1]
        for tc in tcs:
            d_step_A(tc)
        for tc in tcs:
            ln_sq(2, tc, outst[tc % 2], vv_marks[tc], dumpC)
        ln_cm_a(2, vv_marks[tcs[-1]], pr0=(2 * pp) % 8, nw=2)
        ln_cm_b(2, pr0=(2 * pp) % 8, nw=2)
        for tc in tcs:
            d_step_G(tc)

    P.op("sync", lambda s: s.wait_ge(sems["s_out0"], 16 * ((TC + 1) // 2)),
         waits=[], incs=[])
    P.op("sync", lambda s: s.wait_ge(sems["s_out1"], 16 * (TC // 2)),
         waits=[], incs=[])

    import contextlib
    es = contextlib.ExitStack()
    sems = {}
    for nm in semnames:
        sems[nm] = es.enter_context(nc.semaphore(nm))
    block = es.enter_context(nc.Block())

    def emit(eng, name):
        for fn, waits, incs in P.items[name]:
            for s, v in waits:
                eng.wait_ge(sems[s], v)
            instr = fn(eng)
            assert instr is not None or not incs, (name, waits, incs)
            if instr is not None:
                for s, k in incs:
                    instr.then_inc(sems[s], k)

    block.sync(lambda e: emit(e, "sync"))
    block.gpsimd(lambda e: emit(e, "gpsimd"))
    block.tensor(lambda e: emit(e, "tensor"))
    block.vector(lambda e: emit(e, "vector"))
    block.scalar(lambda e: emit(e, "scalar"))
    es.close()
    arena_ctx.__exit__(None, None, None)
    ps_ctx.__exit__(None, None, None)
    return nc


def _shards(L0_attn_x):
    B, S, _ = L0_attn_x.shape
    tok = (B * S) // NCORES
    core_slices = []
    for c in range(NCORES):
        seq = c // (NCORES // B)
        off = (c % (NCORES // B)) * tok
        core_slices.append((seq, off, tok))
    return core_slices


def kernel(L0_attn_x, L0_attn_Wi, L0_attn_bi, L0_attn_Wf, L0_attn_bf,
           L0_attn_Wo, L0_attn_bo, L0_attn_Ws, L0_attn_bs,
           L0_attn_gamma, L0_attn_beta, L0_ffn_w_gate, L0_ffn_w_up,
           L0_ffn_w_down):
    B, S, Dm = L0_attn_x.shape
    assert Dm == D
    T = (B * S) // NCORES
    x = np.ascontiguousarray(L0_attn_x, dtype=np.float32)
    bias_pack = np.concatenate(
        [np.ascontiguousarray(np.asarray(b, np.float32)).reshape(8, 128).T
         for b in (L0_attn_bf, L0_attn_bi, L0_attn_bs, L0_attn_bo)], axis=1)
    bias_pack = np.ascontiguousarray(bias_pack, np.float32)
    ident = np.eye(128, dtype=np.float32)
    wgT = np.ascontiguousarray(np.asarray(L0_ffn_w_gate).T, np.int32)
    wuT = np.ascontiguousarray(np.asarray(L0_ffn_w_up).T, np.int32)
    wdT = np.ascontiguousarray(np.asarray(L0_ffn_w_down).T, np.int32)
    Wi = np.ascontiguousarray(np.asarray(L0_attn_Wi), np.float32)
    Wf = np.ascontiguousarray(np.asarray(L0_attn_Wf), np.float32)
    Wo = np.ascontiguousarray(np.asarray(L0_attn_Wo), np.float32)
    Ws = np.ascontiguousarray(np.asarray(L0_attn_Ws), np.float32)
    gam = np.ascontiguousarray(np.asarray(L0_attn_gamma), np.float32)
    bet = np.ascontiguousarray(np.asarray(L0_attn_beta), np.float32)

    nc = build_nc(T)
    in_maps = []
    for c, (seq, off, tok) in enumerate(_shards(x)):
        xs = x[seq, off:off + tok, :]
        sel = np.zeros((128, 8), np.float32)
        sel[:, c % 4] = 1.0
        in_maps.append({
            "xT": np.ascontiguousarray(xs.T),
            "xN": np.ascontiguousarray(xs),
            "Wi": Wi, "Wf": Wf, "Wo": Wo, "Ws": Ws,
            "biases": bias_pack, "gamma": gam, "beta": bet,
            "sel": sel, "ident": ident,
            "wgT": wgT, "wuT": wuT, "wdT": wdT,
        })
    res = run_bass_kernel_spmd(nc, in_maps, core_ids=list(range(NCORES)))
    out = np.empty((B, S, D), np.float32)
    for c, (seq, off, tok) in enumerate(_shards(x)):
        out[seq, off:off + tok, :] = res.results[c]["out"]
    return out


# version 29
# speedup vs baseline: 1.0252x; 1.0106x over previous
import numpy as np
import concourse.bass as bass
import concourse.mybir as mybir
from concourse.bass_utils import run_bass_kernel_spmd

F32 = mybir.dt.float32
F16 = mybir.dt.float16
BF16 = mybir.dt.bfloat16
I32 = mybir.dt.int32
AF = mybir.ActivationFunctionType
OP = mybir.AluOpType

D = 1024
HID = 3072
KC = 8
M = 24
NCORES = 8
LN_EPS = 1e-5
QS = 1.0 / 127.0

ENGINES = ("sync", "gpsimd", "tensor", "vector", "scalar")


class Planner:

    def __init__(self):
        self.items = {e: [] for e in ENGINES}
        self.cnt = {}

    def op(self, eng, fn, waits=(), incs=()):
        for s, k in incs:
            self.cnt[s] = self.cnt.get(s, 0) + k
        self.items[eng].append((fn, list(waits), list(incs)))

    def now(self, sem):
        return self.cnt.get(sem, 0)


def build_nc(T):
    assert T % 128 == 0
    TC = T // 128
    MV = min(512, T)
    NTS = T // MV
    nc = bass.Bass("TRN2", target_bir_lowering=False)

    xT = nc.dram_tensor("xT", [D, T], F32, kind="ExternalInput").ap()
    xN = nc.dram_tensor("xN", [T, D], F32, kind="ExternalInput").ap()
    Wi = nc.dram_tensor("Wi", [D, D], F32, kind="ExternalInput").ap()
    Wf = nc.dram_tensor("Wf", [D, D], F32, kind="ExternalInput").ap()
    Wo = nc.dram_tensor("Wo", [D, D], F32, kind="ExternalInput").ap()
    Ws = nc.dram_tensor("Ws", [D, D], F32, kind="ExternalInput").ap()
    biases = nc.dram_tensor("biases", [128, 32], F32, kind="ExternalInput").ap()
    gamma = nc.dram_tensor("gamma", [D], F32, kind="ExternalInput").ap()
    beta = nc.dram_tensor("beta", [D], F32, kind="ExternalInput").ap()
    sel = nc.dram_tensor("sel", [128, 8], F32, kind="ExternalInput").ap()
    ident = nc.dram_tensor("ident", [128, 128], F32, kind="ExternalInput").ap()
    wgT = nc.dram_tensor("wgT", [D, HID], I32, kind="ExternalInput").ap()
    wuT = nc.dram_tensor("wuT", [D, HID], I32, kind="ExternalInput").ap()
    wdT = nc.dram_tensor("wdT", [HID, D], I32, kind="ExternalInput").ap()
    out_d = nc.dram_tensor("out", [T, D], F32, kind="ExternalOutput").ap()
    cc_in = nc.dram_tensor("cc_in", [128, 16], F32, kind="Internal").ap()
    cc_out = nc.dram_tensor("cc_out", [4, 128, 16], F32, kind="Internal",
                            addr_space="Local").ap()
    x1s = nc.dram_tensor("x1s", [T, D], F16, kind="Internal").ap()

    GW = 4 * T
    O_XFM = 0
    O_G1 = GW
    O_G2 = 2 * GW
    O_G3 = 3 * GW
    O_Q = 4 * GW
    QW_W0 = O_Q
    QW_W1 = O_Q + 4096
    QW_SCR = O_Q + 8192
    QW_Z = QW_SCR + max(T // 2, 1024)
    QW_N1 = QW_Z + 1024
    Q_END = QW_N1 + 1024
    O_WD = O_Q
    O_X1BF = Q_END
    assert Q_END + 1024 - O_Q <= 12288
    O_P = O_Q + 12288
    P_GAM = O_P
    P_BET = O_P + 512
    P_BIA = O_P + 1024
    P_SEL = P_BIA + 32
    P_IDF = P_SEL + 8
    P_IDB = P_IDF + 64
    P_SUM = P_IDB + 64
    P_GATH = P_SUM + 16
    P_HRUN = P_GATH + 128
    P_HSELA = P_HRUN + 8
    P_HSELB = P_HSELA + 8
    P_COLS = P_HSELB + 8
    P_NH = P_COLS + 144
    P_SL = P_NH + 8
    P_XT = P_SL + 8
    P_OUTST = P_XT + 2048
    P_WGU = P_OUTST + 2048
    P_END = P_WGU + 2048
    NWORDS = P_END
    assert NWORDS <= 53000, NWORDS

    P = Planner()
    semnames = ([f"g_xf{k}" for k in range(KC)]
                + [f"g_wA{m}" for m in range(8)] + [f"g_wB{m}" for m in range(8)]
                + ["g_w0", "g_w1", "g_small", "g_gb", "g_wd", "r_bc", "l_bc", "g_wgu0",
                "g_wgu1", "s_x0", "s_x1", "s_x2", "s_x3", "s_cc", "s_x1w0", "s_x1w1", "s_x1w2", "s_x1w3", "s_wf0", "s_wf1", "s_out0", "s_out1",
                "T_", "V_", "S_", "G_"])

    arena_ctx = nc.sbuf_tensor("AR", [128, NWORDS], F32)
    AR = arena_ctx.__enter__()
    ps_ctx = nc.psum_tensor("PS", [128, 4096], F32)
    PS = ps_ctx.__enter__()
    ar = AR[:]
    ar16 = ar.bitcast(F16)
    arbf = ar.bitcast(BF16)
    ps = PS[:]
    ps16 = ps.bitcast(F16)
    psbf = ps.bitcast(BF16)

    def f32v(off, n):
        return ar[:, off:off + n]

    def f16v(off, n):
        return ar16[:, 2 * off: 2 * off + n]

    def bf16v(off, n):
        return arbf[:, 2 * off: 2 * off + n]

    xfm = bf16v(O_XFM, 8 * T)
    G1 = f16v(O_G1, 8 * T)
    G2 = f16v(O_G2, 8 * T)
    G3 = f16v(O_G3, 8 * T)
    x1fm = f16v(O_G3, 8 * T)
    hh = bf16v(O_XFM, 24 * T)
    Wb = [bf16v(QW_W0, 8 * 1024), bf16v(QW_W1, 8 * 1024)]
    scr = f16v(QW_SCR, T)
    zb = [f16v(QW_Z, 1024), f16v(QW_Z + 512, 1024),
          f16v(QW_SCR, 1024), f16v(QW_SCR + 512, 1024),
          f16v(P_OUTST, 1024), f16v(P_OUTST + 512, 1024),
          f16v(P_OUTST + 1024, 1024), f16v(P_OUTST + 1536, 1024)]
    ab = [f16v(QW_N1, 1024), f16v(QW_N1 + 512, 1024)]
    wd_sb = bf16v(O_WD, 24 * 1024)
    x1bf = [f16v(O_X1BF, 1024), f16v(O_X1BF + 512, 1024)]
    gam = f16v(P_GAM, 1024)
    bet = f16v(P_BET, 1024)
    bia = f32v(P_BIA, 32)
    selv = f32v(P_SEL, 8)
    idf = f16v(P_IDF, 128)
    idb = bf16v(P_IDB, 128)
    summ = f32v(P_SUM, 16)
    gath = f32v(P_GATH, 128)
    Hrun = f32v(P_HRUN, 8)
    HselA = f32v(P_HSELA, 8)
    HselB = f32v(P_HSELB, 8)
    nh = f32v(P_NH, 1)
    nh4 = f32v(P_NH, 4)
    sumlog = f32v(P_SL, 8)

    def col(i):
        return f32v(P_COLS + i, 1)

    xt = [f16v(P_XT, 1024), f16v(P_XT + 512, 1024),
          f16v(P_XT + 1024, 1024), f16v(P_XT + 1536, 1024)]
    sg = [f16v(P_XT, T), f16v(P_XT + 1024, T)]
    outst = [f32v(P_OUTST, 1024), f32v(P_OUTST + 1024, 1024)]
    wg_b = [bf16v(P_WGU, 1024), bf16v(P_WGU + 512, 1024)]
    wu_b = [bf16v(P_WGU + 1024, 1024), bf16v(P_WGU + 1536, 1024)]

    def grpA(par):
        return ps[:, par * 2048: par * 2048 + T]

    psB = [ps16[:, 0:1024], ps16[:, 1024:2048],
           ps16[:, 6144:7168], ps16[:, 7168:8192]]
    dumpB = G2[:, 0:1024]
    psC = [ps16[:, (2 + j) * 1024:(3 + j) * 1024] for j in range(4)]
    psG = ps[:, 0:T]
    psU = ps[:, 2048:2048 + T]
    psD = [ps[:, 0:1024], ps[:, 1024:2048]]
    dumpC = ps[:, 2048:3072]

    for fn in [
        lambda g: g.dma_start(out=bia, in_=biases),
        lambda g: g.dma_start(out=selv, in_=sel),
        lambda g: g.dma_start(out=idf, in_=ident),
        lambda g: g.dma_start(out=idb, in_=ident),
        lambda g: g.dma_start(out=gam, in_=gamma.partition_broadcast(128)),
        lambda g: g.dma_start(out=bet, in_=beta.partition_broadcast(128)),
    ]:
        P.op("gpsimd", fn, incs=[("g_small", 16)])
    staged_wf = (4 * T >= 8192)
    if staged_wf:
        for h in range(2):
            P.op("sync", lambda s, h=h: s.dma_start(
                out=f32v(O_G3 + h * 4096, 4096).rearrange("p (kc e) -> p kc e", kc=4),
                in_=Wf[h * 512:(h + 1) * 512, :].rearrange("(kc p) e -> p kc e", p=128)),
                incs=[(f"s_wf{h}", 16)])
        for h in range(2):
            P.op("vector", lambda v, h=h: v.tensor_copy(
                out=Wb[0][:, h * 4096:(h + 1) * 4096],
                in_=f32v(O_G3 + h * 4096, 4096)),
                waits=[(f"s_wf{h}", 16)], incs=[("V_", 1)])
    else:
        P.op("gpsimd", lambda g: g.dma_start(
            out=Wb[0].rearrange("p (kc e) -> p kc e", kc=KC),
            in_=Wf.rearrange("(kc p) e -> p kc e", p=128)), incs=[("g_w0", 16)])
    v_wf = P.now("V_")
    for k in range(KC):
        P.op("gpsimd", lambda g, k=k: g.dma_start(
            out=xfm[:, k * T:(k + 1) * T],
            in_=xT[k * 128:(k + 1) * 128, :]), incs=[(f"g_xf{k}", 16)])
    P.op("gpsimd", lambda g: g.dma_start(
        out=Wb[1].rearrange("p (kc e) -> p kc e", kc=KC),
        in_=Wi.rearrange("(kc p) e -> p kc e", p=128)), incs=[("g_w1", 16)])
    P.op("gpsimd", lambda g: g.memset(nh4[:], -0.5), incs=[("G_", 1)])

    for j in range(min(4, TC)):
        P.op("gpsimd", lambda g, j=j: g.dma_start(
            out=xt[j], in_=xN[j * 128:(j + 1) * 128, :]),
            incs=[(f"s_x{j}", 16)])
    sx_cnt = [16, 16, 16, 16]

    projs = [
        (0, AF.Sigmoid, G1),
        (1, AF.Sigmoid, G3),
        (0, AF.Tanh, G2),
        (1, AF.Sigmoid, G2),
    ]
    evA = []
    uV = []
    projT = []
    def plan_proj(p):
        slot, act, dst = projs[p]
        waits = []
        if p == 0:
            waits = [("V_", v_wf)] if staged_wf else [("g_w0", 16)]
        elif p == 1:
            waits = [("g_w1", 16)]
        elif p == 2:
            waits = [("g_w0", 16 if staged_wf else 32)]
        elif p == 3:
            waits = [("g_w1", 32)]
        for m in range(8):
            gidx = p * 8 + m
            par = gidx % 2
            w = list(waits) if m == 0 else []
            if gidx >= 2:
                w.append(("S_", evA[gidx - 2]))
            if p == 3:
                pass
            for k in range(KC):
                if p == 0 and m == 0:
                    w.append((f"g_xf{k}", 16))
                for ts in range(NTS):
                    def mm(t, par=par, k=k, ts=ts, slot=slot, m=m):
                        return t.matmul(
                            out=grpA(par)[:, ts * MV:(ts + 1) * MV],
                            lhsT=Wb[slot][:, k * 1024 + m * 128: k * 1024 + (m + 1) * 128],
                            rhs=xfm[:, k * T + ts * MV: k * T + ts * MV + MV],
                            start=(k == 0), stop=(k == KC - 1))
                    last = (k == KC - 1 and ts == NTS - 1)
                    P.op("tensor", mm, waits=w, incs=[("T_", 1)] if last else [])
                    w = []
            tdone = P.now("T_")
            ew = [("T_", tdone)]
            if p == 3:
                ew.append(("V_", uV[m]))

            def ev(sc, par=par, act=act, dst=dst, m=m, p=p):
                return sc.activation(out=dst[:, m * T:(m + 1) * T],
                                     in_=grpA(par)[:, 0:T], func=act,
                                     bias=bia[:, p * 8 + m: p * 8 + m + 1])
            if p == 0 and m == 0:
                ew.append(("g_small", 96))
            if p == 1 and m == 0 and staged_wf:
                ew.append(("V_", v_wf))
            P.op("scalar", ev, waits=ew, incs=[("S_", 1)])
            evA.append(P.now("S_"))
        projT.append(P.now("T_"))
        if p == 0:
            P.op("gpsimd", lambda g: g.dma_start(
                out=Wb[0].rearrange("p (kc e) -> p kc e", kc=KC),
                in_=Ws.rearrange("(kc p) e -> p kc e", p=128)),
                waits=[("T_", projT[0])], incs=[("g_w0", 16)])
        if p == 1:
            P.op("gpsimd", lambda g: g.dma_start(
                out=Wb[1].rearrange("p (kc e) -> p kc e", kc=KC),
                in_=Wo.rearrange("(kc p) e -> p kc e", p=128)),
                waits=[("T_", projT[1])], incs=[("g_w1", 16)])


    plan_proj(0)
    plan_proj(1)
    plan_proj(2)

    lndump = f16v(O_X1BF, 2 * T)
    for kc in range(KC):
        P.op("scalar", lambda sc, kc=kc: sc.activation(
            out=lndump[:, 0:T], in_=G1[:, kc * T:(kc + 1) * T], func=AF.Ln,
            accum_out=sumlog[:, kc:kc + 1]),
            waits=[("S_", P.now("S_"))], incs=[("S_", 1)])
    P.op("scalar", lambda sc: sc.activation(
        out=summ[:, 0:8], in_=sumlog, func=AF.Exp),
        waits=[("S_", P.now("S_"))], incs=[("S_", 1)])
    s_atot = P.now("S_")
    for kc in range(KC):
        P.op("vector", lambda v, kc=kc: v.tensor_tensor(
            out=G3[:, kc * T:(kc + 1) * T], in0=G3[:, kc * T:(kc + 1) * T],
            in1=G2[:, kc * T:(kc + 1) * T], op=OP.mult),
            waits=[("S_", evA[16 + kc]), ("V_", P.now("V_"))], incs=[("V_", 1)])
        uV.append(P.now("V_"))
        P.op("vector", lambda v, kc=kc: v.tensor_tensor_scan(
            out=scr, data0=G1[:, kc * T:(kc + 1) * T],
            data1=G3[:, kc * T:(kc + 1) * T], initial=0.0,
            op0=OP.mult, op1=OP.add),
            waits=[("S_", evA[kc]), ("V_", P.now("V_"))], incs=[("V_", 1)])
        P.op("vector", lambda v, kc=kc: v.tensor_copy(
            out=summ[:, 8 + kc:9 + kc], in_=scr[:, T - 1:T]),
            waits=[("V_", P.now("V_"))], incs=[("V_", 1)])
    v_summ = P.now("V_")
    plan_proj(3)


    P.op("sync", lambda s: s.dma_start(out=cc_in, in_=summ),
         waits=[("V_", v_summ), ("S_", s_atot)], incs=[("s_cc", 16)])
    P.op("gpsimd", lambda g: g.collective_compute(
        "AllGather", OP.bypass, replica_groups=[[0, 1, 2, 3], [4, 5, 6, 7]],
        ins=[cc_in], outs=[cc_out]),
        waits=[("s_cc", 16)], incs=[("G_", 1)])
    P.op("gpsimd", lambda g: g.dma_start(
        out=gath[:, 0:64].rearrange("p (r c) -> p r c", r=4),
        in_=cc_out.rearrange("r p c -> p r c")),
        waits=[("G_", P.now("G_"))], incs=[("g_gb", 16)])

    P.op("vector", lambda v: v.memset(Hrun[:], 0.0),
         waits=[("g_gb", 16), ("g_small", 96)], incs=[("V_", 1)])
    P.op("vector", lambda v: v.memset(HselA[:], 0.0),
         waits=[("V_", P.now("V_"))], incs=[("V_", 1)])
    hsel_cur, hsel_alt = HselA, HselB
    for r in range(4):
        P.op("vector", lambda v, r=r, a=hsel_alt, c=hsel_cur: v.scalar_tensor_tensor(
            out=a, in0=Hrun[:], scalar=selv[:, r:r + 1], in1=c,
            op0=OP.mult, op1=OP.add),
            waits=[("V_", P.now("V_"))], incs=[("V_", 1)])
        hsel_cur, hsel_alt = hsel_alt, hsel_cur
        if r < 3:
            P.op("vector", lambda v, r=r: v.tensor_tensor(
                out=Hrun[:], in0=Hrun[:], in1=gath[:, r * 16:r * 16 + 8], op=OP.mult),
                waits=[("V_", P.now("V_"))], incs=[("V_", 1)])
            P.op("vector", lambda v, r=r: v.tensor_tensor(
                out=Hrun[:], in0=Hrun[:], in1=gath[:, r * 16 + 8:r * 16 + 16], op=OP.add),
                waits=[("V_", P.now("V_"))], incs=[("V_", 1)])
    Hfin = hsel_cur

    yV = []
    for kc in range(KC):
        P.op("vector", lambda v, kc=kc: v.tensor_tensor_scan(
            out=scr, data0=G1[:, kc * T:(kc + 1) * T],
            data1=G3[:, kc * T:(kc + 1) * T], initial=Hfin[:, kc:kc + 1],
            op0=OP.mult, op1=OP.add),
            waits=[("V_", P.now("V_"))], incs=[("V_", 1)])
        P.op("vector", lambda v, kc=kc: v.tensor_tensor(
            out=G1[:, kc * T:(kc + 1) * T], in0=G2[:, kc * T:(kc + 1) * T],
            in1=scr, op=OP.mult),
            waits=[("V_", P.now("V_")), ("S_", evA[24 + kc])], incs=[("V_", 1)])
        yV.append(P.now("V_"))

    scr_words = max(T // 2, 1024)
    n_scr_ch = scr_words // 512
    wd_cnt = [0]

    def plan_wd(mlist, waits):
        for mi in mlist:
            P.op("gpsimd", lambda g, mi=mi: g.dma_start(
                out=wd_sb[:, mi * 1024:(mi + 1) * 1024],
                in_=wdT[mi * 128:(mi + 1) * 128, :]),
                waits=waits if mi == mlist[0] else [], incs=[("g_wd", 16)])
            wd_cnt[0] += 1

    plan_wd(list(range(0, 16)), [("T_", projT[3])])
    wd_rest = list(range(16, 24))

    colnames = ("s2", "ssum", "mu", "var", "rstd", "nb")
    colix = {}
    for stg in range(3):
        for ni, name in enumerate(colnames):
            for pr in range(8):
                colix[(stg, pr, name)] = stg * 48 + ni * 8 + pr
    lnV_last = {}
    lnS_last = {}
    tyT = [0] * TC
    v_z = [0] * TC
    v_w = [0] * TC
    v_xb_m = [0] * TC
    x1bf_S = [0] * TC
    ap32_S = [0] * TC
    x1T_quad = [0] * TC
    evC = []
    sq_mark = {}

    def lncol(stage, tc, name):
        return col(colix[(stage, tc % 8, name)])

    def ln_sq(stage, tc, buf, vwait, dump):
        key = (stage, tc % 8)
        c_s2 = lncol(stage, tc, "s2")
        wsq = [("V_", vwait), ("S_", P.now("S_"))]
        if stage < 2:
            wsq.append(("V_", yV[-1]))
        if key in lnV_last:
            wsq.append(("V_", lnV_last[key]))
        P.op("scalar", lambda sc, buf=buf, c_s2=c_s2, dump=dump: sc.activation(
            out=dump, in_=buf, func=AF.Square, accum_out=c_s2),
            waits=wsq, incs=[("S_", 1)])
        sq_mark[key] = P.now("S_")

    cm_gp = {}

    def ln_cm_a(stage, vwait, pr0=0, nw=4):
        def c4(name):
            ni = colnames.index(name)
            return f32v(P_COLS + stage * 48 + ni * 8 + pr0, nw)
        ss4, mu4, va4, rs4 = c4("ssum"), c4("mu"), c4("var"), c4("rstd")
        s24 = c4("s2")
        P.op("vector", lambda v: v.tensor_tensor(
            out=mu4, in0=ss4, in1=ss4, op=OP.mult),
            waits=[("V_", vwait)], incs=[("V_", 1)])
        P.op("vector", lambda v: v.scalar_tensor_tensor(
            out=va4, in0=mu4, scalar=-1.0 / D, in1=s24, op0=OP.mult, op1=OP.add),
            waits=[("V_", P.now("V_")), ("S_", sq_mark[(stage, pr0 + nw - 1)])],
            incs=[("V_", 1)])
        P.op("vector", lambda v: v.tensor_scalar(
            out=va4, in0=va4, scalar1=1.0 / D, scalar2=LN_EPS,
            op0=OP.mult, op1=OP.add),
            waits=[("V_", P.now("V_"))], incs=[("V_", 1)])
        vv = P.now("V_")
        if stage < 2:
            P.op("vector", lambda v: v.reciprocal(out=rs4, in_=va4),
                 waits=[("V_", vv)], incs=[("V_", 1)])
            P.op("scalar", lambda sc: sc.activation(
                out=rs4, in_=rs4, func=AF.Sqrt),
                waits=[("V_", P.now("V_")), ("S_", P.now("S_"))], incs=[("S_", 1)])
            cm_gp[(stage, pr0)] = ("S_", P.now("S_"))
        else:
            P.op("gpsimd", lambda g: g.tensor_tensor(
                out=rs4, in0=va4, in1=nh4[:, 0:nw], op=OP.pow),
                waits=[("V_", vv), ("G_", 1)], incs=[("G_", 1)])
            cm_gp[(stage, pr0)] = ("G_", P.now("G_"))

    def ln_cm_b(stage, pr0=0, nw=4):
        def c4(name):
            ni = colnames.index(name)
            return f32v(P_COLS + stage * 48 + ni * 8 + pr0, nw)
        ss4, rs4, nb4 = c4("ssum"), c4("rstd"), c4("nb")
        P.op("vector", lambda v: v.scalar_tensor_tensor(
            out=nb4, in0=ss4, scalar=-1.0 / D, in1=rs4, op0=OP.mult, op1=OP.mult),
            waits=[cm_gp[(stage, pr0)], ("V_", P.now("V_"))],
            incs=[("V_", 1)])
        for pr in range(pr0, pr0 + nw):
            lnV_last[(stage, pr)] = P.now("V_")

    def ln_back(stage, tc):
        return (lncol(stage, tc, "rstd"), lncol(stage, tc, "nb"),
                lnV_last[(stage, tc % 8)])

    def b_step_A(tc):
        pb = tc % 4
        for kc in range(KC):
            w = []
            if tc < 4:
                w.append(("V_", yV[kc]))
                if kc == 0:
                    w.append(("S_", evA[31]))
                    if tc == 0:
                        w.append(("g_small", 96))
            elif kc == 0:
                w.append(("V_", v_z[tc - 4]))
            P.op("tensor", lambda t, kc=kc, pb=pb, tc=tc: t.transpose(
                out=psB[pb][:, kc * 128:(kc + 1) * 128],
                in_=G1[:, kc * T + tc * 128: kc * T + (tc + 1) * 128],
                identity=idf),
                waits=w, incs=[("T_", 1)] if kc == KC - 1 else [])
        tyT[tc] = P.now("T_")
        q = tc % 8
        w = [("T_", tyT[tc]), (f"s_x{tc % 4}", 16 * (tc // 4 + 1))]
        if tc >= 8:
            w.append(("V_", ap32_S[tc - 8]))
            w.append(("T_", x1T_quad[tc - 8]))
            w.append((f"s_x1w{(tc - 8) % 4}", 16 * (tc // 4 - 1)))
        if tc in (2, 3):
            w.append(("V_", yV[-1]))
        if (0, q) in lnV_last:
            w.append(("V_", lnV_last[(0, q)]))
        c_ss1 = lncol(0, tc, "ssum")
        P.op("vector", lambda v, pb=pb, q=q, c_ss1=c_ss1: v.scalar_tensor_tensor(
            out=zb[q], in0=psB[pb], scalar=1.0, in1=xt[tc % 4],
            op0=OP.mult, op1=OP.add, accum_out=c_ss1),
            waits=w, incs=[("V_", 1)])
        v_z[tc] = P.now("V_")
        P.op("vector", lambda v, q=tc % 4: v.tensor_tensor(
            out=xt[q], in0=xt[q], in1=bet, op=OP.add),
            waits=[("V_", v_z[tc])], incs=[("V_", 1)])
        v_xb_m[tc] = ("V_", P.now("V_"))

    def b_step_D(tc):
        par = tc % 2
        q = tc % 8
        rs1, nb1, vnb1 = ln_back(0, tc)
        wn1 = [("V_", vnb1)]
        if tc >= 2:
            wn1.append(("V_", v_w[tc - 2]))
        P.op("vector", lambda v, rs1=rs1, nb1=nb1, par=par, q=q: v.tensor_scalar(
            out=ab[par], in0=zb[q], scalar1=rs1, scalar2=nb1,
            op0=OP.mult, op1=OP.add),
            waits=wn1, incs=[("V_", 1)])
        P.op("vector", lambda v, par=par, q=q: v.tensor_tensor(
            out=zb[q], in0=ab[par], in1=gam, op=OP.mult),
            waits=[("V_", P.now("V_"))], incs=[("V_", 1)])
        wsum = [("V_", P.now("V_")), v_xb_m[tc]]
        if (1, tc % 8) in lnV_last:
            wsum.append(("V_", lnV_last[(1, tc % 8)]))
        c_ss2 = lncol(1, tc, "ssum")
        P.op("vector", lambda v, q=q, xq=tc % 4, c_ss2=c_ss2: v.scalar_tensor_tensor(
            out=zb[q], in0=zb[q], in1=xt[xq], scalar=1.0,
            op0=OP.mult, op1=OP.add, accum_out=c_ss2),
            waits=wsum, incs=[("V_", 1)])
        v_w[tc] = P.now("V_")
        if tc + 4 < TC:
            P.op("gpsimd", lambda g, tc=tc, xq=tc % 4: g.dma_start(
                out=xt[xq], in_=xN[(tc + 4) * 128:(tc + 5) * 128, :]),
                waits=[("V_", v_w[tc])], incs=[(f"s_x{tc % 4}", 16)])
            sx_cnt[tc % 4] += 16

    def b_step_G(tc):
        par = tc % 2
        q = tc % 8
        rs2, nb2, vnb2 = ln_back(1, tc)
        wap = [("V_", vnb2)]
        P.op("vector", lambda v, rs2=rs2, nb2=nb2, q=q: v.tensor_scalar(
            out=zb[q], in0=zb[q], scalar1=rs2, scalar2=nb2,
            op0=OP.mult, op1=OP.add),
            waits=wap, incs=[("V_", 1)])
        x1bf_S[tc] = P.now("V_")
        ap32_S[tc] = P.now("V_")
        P.op("sync", lambda s, tc=tc, q=q: s.dma_start(
            out=x1s[tc * 128:(tc + 1) * 128, :], in_=zb[q]),
            waits=[("V_", x1bf_S[tc])], incs=[(f"s_x1w{tc % 4}", 16)])
        for kc in range(KC):
            w = [("V_", x1bf_S[tc])] if kc == 0 else []
            if tc % 4 == 0 and len(evC) > 0 and kc == 0:
                w.append(("S_", evC[-1]))
            P.op("tensor", lambda t, kc=kc, tc=tc: t.transpose(
                out=psC[kc // 2][:, (kc % 2) * 512 + (tc % 4) * 128:
                                 (kc % 2) * 512 + (tc % 4) * 128 + 128],
                in_=zb[tc % 8][:, kc * 128:(kc + 1) * 128],
                identity=idf),
                waits=w, incs=[("T_", 1)] if kc == KC - 1 else [])
        x1T_quad[tc] = P.now("T_")
        if tc % 4 == 3:
            for kc in range(KC):
                wev = [("T_", x1T_quad[tc])] if kc == 0 else []
                if kc == 0 and len(evC) == 0:
                    wev.append(("V_", yV[-1]))
                P.op("scalar", lambda sc, kc=kc, tc=tc: sc.copy(
                    out=x1fm[:, kc * T + (tc - 3) * 128: kc * T + (tc + 1) * 128],
                    in_=psC[kc // 2][:, (kc % 2) * 512:(kc % 2) * 512 + 512]),
                    waits=wev, incs=[("S_", 1)])
            evC.append(P.now("S_"))

    assert TC % 4 == 0
    sq2_mark = [0] * TC
    wsum2_mark = [0] * TC
    NQ = TC // 4

    def bH1(qq):
        tcs = [4 * qq + j for j in range(4)]
        for tc in tcs:
            b_step_A(tc)
            ln_sq(0, tc, zb[tc % 8], v_z[tc], dumpB)
        ln_cm_a(0, v_z[tcs[-1]], pr0=(4 * qq) % 8, nw=4)

    def bH2(qq):
        tcs = [4 * qq + j for j in range(4)]
        ln_cm_b(0, pr0=(4 * qq) % 8, nw=4)
        for tc in tcs:
            b_step_D(tc)
            ln_sq(1, tc, zb[tc % 8], v_w[tc], dumpB)
            sq2_mark[tc] = P.now("S_")
        ln_cm_a(1, v_w[tcs[-1]], pr0=(4 * qq) % 8, nw=4)

    def bH3(qq):
        tcs = [4 * qq + j for j in range(4)]
        ln_cm_b(1, pr0=(4 * qq) % 8, nw=4)
        for tc in tcs:
            b_step_G(tc)

    for qq in range(NQ + 1):
        if qq < NQ:
            bH1(qq)
        if qq >= 1:
            bH3(qq - 1)
        if qq < NQ:
            bH2(qq)

    vB_done = P.now("V_")
    sB_done = P.now("S_")
    plan_wd(wd_rest, [("S_", sB_done), ("V_", vB_done), ("T_", x1T_quad[-1]),
                      ("s_x1w0", P.now("s_x1w0")), ("s_x1w1", P.now("s_x1w1")),
                      ("s_x1w2", P.now("s_x1w2")), ("s_x1w3", P.now("s_x1w3"))])

    def plan_wgu(m, waits):
        par = m % 2
        P.op("gpsimd", lambda g, m=m, par=par: g.dma_start(
            out=wg_b[par].rearrange("p (kc h) -> p kc h", kc=KC),
            in_=wgT[:, m * 128:(m + 1) * 128].rearrange("(kc p) h -> p kc h", p=128)),
            waits=waits, incs=[(f"g_wgu{par}", 16)])
        P.op("gpsimd", lambda g, m=m, par=par: g.dma_start(
            out=wu_b[par].rearrange("p (kc h) -> p kc h", kc=KC),
            in_=wuT[:, m * 128:(m + 1) * 128].rearrange("(kc p) h -> p kc h", p=128)),
            waits=[], incs=[(f"g_wgu{par}", 16)])

    plan_wgu(0, [])
    plan_wgu(1, [])
    gwgu_cnt = [32, 32]
    siluS = []
    hhV = []
    tG = []
    tU = []
    for m in range(M):
        par = m % 2
        w = [(f"g_wgu{par}", gwgu_cnt[par])]
        if m == 0:
            w += [("S_", evC[-1]), ("V_", vB_done), ("S_", sB_done)]
        if m >= 1:
            w.append(("S_", siluS[m - 1]))
        for k in range(KC):
            for ts in range(NTS):
                def mmg(t, par=par, k=k, ts=ts, m=m):
                    return t.matmul(
                        out=psG[:, ts * MV:(ts + 1) * MV],
                        lhsT=wg_b[par][:, k * 128:(k + 1) * 128],
                        rhs=x1fm[:, k * T + ts * MV: k * T + ts * MV + MV],
                        start=(k == 0), stop=(k == KC - 1))
                last = (k == KC - 1 and ts == NTS - 1)
                P.op("tensor", mmg, waits=w, incs=[("T_", 1)] if last else [])
                w = []
        tG.append(P.now("T_"))
        w = []
        if m >= 1:
            w.append(("V_", hhV[m - 1]))
        elif m == 0:
            pass
        for k in range(KC):
            for ts in range(NTS):
                def mmu(t, par=par, k=k, ts=ts, m=m):
                    return t.matmul(
                        out=psU[:, ts * MV:(ts + 1) * MV],
                        lhsT=wu_b[par][:, k * 128:(k + 1) * 128],
                        rhs=x1fm[:, k * T + ts * MV: k * T + ts * MV + MV],
                        start=(k == 0), stop=(k == KC - 1))
                last = (k == KC - 1 and ts == NTS - 1)
                P.op("tensor", mmu, waits=w, incs=[("T_", 1)] if last else [])
                w = []
        tU.append(P.now("T_"))
        if m + 2 < M:
            plan_wgu(m + 2, [("T_", tU[m])])
            gwgu_cnt[par] += 32
        w = [("T_", tG[m])]
        if m >= 2:
            w.append(("V_", hhV[m - 2]))
        if m == 0:
            w.append(("V_", vB_done))
        P.op("scalar", lambda sc, par=par: sc.activation(
            out=sg[par][:, 0:T], in_=psG[:, 0:T], func=AF.Silu, scale=QS),
            waits=w, incs=[("S_", 1)])
        siluS.append(P.now("S_"))
        w = [("T_", tU[m]), ("S_", siluS[m])]
        if m == 0:
            w.append(("T_", projT[3]))
        if m == 8:
            w.append(("T_", tyT[-1]))
        if m == 16:
            w.append(("V_", yV[-1]))
            w.append(("S_", sB_done))
        P.op("vector", lambda v, par=par, m=m: v.scalar_tensor_tensor(
            out=hh[:, m * T:(m + 1) * T], in0=psU[:, 0:T], scalar=QS,
            in1=sg[par][:, 0:T], op0=OP.mult, op1=OP.mult),
            waits=w, incs=[("V_", 1)])
        hhV.append(P.now("V_"))

    vv_marks = [0] * TC
    sx_cnt_at = {}
    ln3S = [0] * TC

    def d_step_A(tc):
        par = tc % 2
        q = tc % 4
        w = [("s_x1w0", P.now("s_x1w0")), ("s_x1w1", P.now("s_x1w1")),
             ("s_x1w2", P.now("s_x1w2")), ("s_x1w3", P.now("s_x1w3"))]
        if tc < 4:
            w.append(("V_", hhV[-1]))
        else:
            w.append(("V_", vv_marks[tc - 4]))
        P.op("gpsimd", lambda g, tc=tc, q=q: g.dma_start(
            out=xt[q], in_=x1s[tc * 128:(tc + 1) * 128, :]),
            waits=w, incs=[(f"s_x{q}", 16)])
        sx_cnt[q] += 16
        sx_cnt_at[tc] = sx_cnt[q]
        w = []
        if tc == 0:
            w += [("g_wd", 16 * 24), ("V_", hhV[-1]), ("S_", siluS[-1])]
        if tc == 1:
            w.append(("S_", siluS[-1]))
        if tc >= 2:
            w.append(("V_", vv_marks[tc - 2]))
        for m in range(M):
            for half in range(2):
                def mmd(t, m=m, half=half, par=par, tc=tc):
                    return t.matmul(
                        out=psD[par][:, half * 512:(half + 1) * 512],
                        lhsT=hh[:, m * T + tc * 128: m * T + (tc + 1) * 128],
                        rhs=wd_sb[:, m * 1024 + half * 512: m * 1024 + (half + 1) * 512],
                        start=(m == 0), stop=(m == M - 1))
                last = (m == M - 1 and half == 1)
                P.op("tensor", mmd, waits=w, incs=[("T_", 1)] if last else [])
                w = []
        t_d = P.now("T_")
        w = [("T_", t_d), (f"s_x{tc % 4}", sx_cnt_at[tc])]
        if tc >= 2:
            w.append((f"s_out{par}", 16 * (tc // 2)))
        if (2, tc % 4) in lnV_last:
            w.append(("V_", lnV_last[(2, tc % 4)]))
        c_ss3 = lncol(2, tc, "ssum")
        P.op("vector", lambda v, par=par, q=tc % 4, c_ss3=c_ss3: v.scalar_tensor_tensor(
            out=outst[par], in0=psD[par], scalar=QS, in1=xt[q],
            op0=OP.mult, op1=OP.add, accum_out=c_ss3),
            waits=w, incs=[("V_", 1)])
        vv_marks[tc] = P.now("V_")

    def d_step_G(tc):
        par = tc % 2
        rs3, nb3, vnb3 = ln_back(2, tc)
        P.op("scalar", lambda sc, rs3=rs3, nb3=nb3, par=par: sc.activation(
            out=outst[par], in_=outst[par], func=AF.Identity, bias=nb3, scale=rs3),
            waits=[("V_", vnb3)], incs=[("S_", 1)])
        ln3S[tc] = P.now("S_")
        lnS_last[(2, tc % 4)] = P.now("S_")
        P.op("sync", lambda s, tc=tc, par=par: s.dma_start(
            out=out_d[tc * 128:(tc + 1) * 128, :], in_=outst[par]),
            waits=[("S_", ln3S[tc])], incs=[(f"s_out{par}", 16)])

    for pp in range(TC // 2):
        tcs = [2 * pp, 2 * pp + 1]
        for tc in tcs:
            d_step_A(tc)
        for tc in tcs:
            ln_sq(2, tc, outst[tc % 2], vv_marks[tc], dumpC)
        ln_cm_a(2, vv_marks[tcs[-1]], pr0=(2 * pp) % 8, nw=2)
        ln_cm_b(2, pr0=(2 * pp) % 8, nw=2)
        for tc in tcs:
            d_step_G(tc)

    P.op("sync", lambda s: s.wait_ge(sems["s_out0"], 16 * ((TC + 1) // 2)),
         waits=[], incs=[])
    P.op("sync", lambda s: s.wait_ge(sems["s_out1"], 16 * (TC // 2)),
         waits=[], incs=[])

    import contextlib
    es = contextlib.ExitStack()
    sems = {}
    for nm in semnames:
        sems[nm] = es.enter_context(nc.semaphore(nm))
    block = es.enter_context(nc.Block())

    def emit(eng, name):
        for fn, waits, incs in P.items[name]:
            for s, v in waits:
                eng.wait_ge(sems[s], v)
            instr = fn(eng)
            assert instr is not None or not incs, (name, waits, incs)
            if instr is not None:
                for s, k in incs:
                    instr.then_inc(sems[s], k)

    block.sync(lambda e: emit(e, "sync"))
    block.gpsimd(lambda e: emit(e, "gpsimd"))
    block.tensor(lambda e: emit(e, "tensor"))
    block.vector(lambda e: emit(e, "vector"))
    block.scalar(lambda e: emit(e, "scalar"))
    es.close()
    arena_ctx.__exit__(None, None, None)
    ps_ctx.__exit__(None, None, None)
    return nc


def _shards(L0_attn_x):
    B, S, _ = L0_attn_x.shape
    tok = (B * S) // NCORES
    core_slices = []
    for c in range(NCORES):
        seq = c // (NCORES // B)
        off = (c % (NCORES // B)) * tok
        core_slices.append((seq, off, tok))
    return core_slices


def kernel(L0_attn_x, L0_attn_Wi, L0_attn_bi, L0_attn_Wf, L0_attn_bf,
           L0_attn_Wo, L0_attn_bo, L0_attn_Ws, L0_attn_bs,
           L0_attn_gamma, L0_attn_beta, L0_ffn_w_gate, L0_ffn_w_up,
           L0_ffn_w_down):
    B, S, Dm = L0_attn_x.shape
    assert Dm == D
    T = (B * S) // NCORES
    x = np.ascontiguousarray(L0_attn_x, dtype=np.float32)
    bias_pack = np.concatenate(
        [np.ascontiguousarray(np.asarray(b, np.float32)).reshape(8, 128).T
         for b in (L0_attn_bf, L0_attn_bi, L0_attn_bs, L0_attn_bo)], axis=1)
    bias_pack = np.ascontiguousarray(bias_pack, np.float32)
    ident = np.eye(128, dtype=np.float32)
    wgT = np.ascontiguousarray(np.asarray(L0_ffn_w_gate).T, np.int32)
    wuT = np.ascontiguousarray(np.asarray(L0_ffn_w_up).T, np.int32)
    wdT = np.ascontiguousarray(np.asarray(L0_ffn_w_down).T, np.int32)
    Wi = np.ascontiguousarray(np.asarray(L0_attn_Wi), np.float32)
    Wf = np.ascontiguousarray(np.asarray(L0_attn_Wf), np.float32)
    Wo = np.ascontiguousarray(np.asarray(L0_attn_Wo), np.float32)
    Ws = np.ascontiguousarray(np.asarray(L0_attn_Ws), np.float32)
    gam = np.ascontiguousarray(np.asarray(L0_attn_gamma), np.float32)
    bet = np.ascontiguousarray(np.asarray(L0_attn_beta), np.float32)

    nc = build_nc(T)
    in_maps = []
    for c, (seq, off, tok) in enumerate(_shards(x)):
        xs = x[seq, off:off + tok, :]
        sel = np.zeros((128, 8), np.float32)
        sel[:, c % 4] = 1.0
        in_maps.append({
            "xT": np.ascontiguousarray(xs.T),
            "xN": np.ascontiguousarray(xs),
            "Wi": Wi, "Wf": Wf, "Wo": Wo, "Ws": Ws,
            "biases": bias_pack, "gamma": gam, "beta": bet,
            "sel": sel, "ident": ident,
            "wgT": wgT, "wuT": wuT, "wdT": wdT,
        })
    res = run_bass_kernel_spmd(nc, in_maps, core_ids=list(range(NCORES)))
    out = np.empty((B, S, D), np.float32)
    for c, (seq, off, tok) in enumerate(_shards(x)):
        out[seq, off:off + tok, :] = res.results[c]["out"]
    return out


# version 30
# speedup vs baseline: 1.0292x; 1.0039x over previous
import numpy as np
import concourse.bass as bass
import concourse.mybir as mybir
from concourse.bass_utils import run_bass_kernel_spmd

F32 = mybir.dt.float32
F16 = mybir.dt.float16
BF16 = mybir.dt.bfloat16
I32 = mybir.dt.int32
AF = mybir.ActivationFunctionType
OP = mybir.AluOpType

D = 1024
HID = 3072
KC = 8
M = 24
NCORES = 8
LN_EPS = 1e-5
QS = 1.0 / 127.0

ENGINES = ("sync", "gpsimd", "tensor", "vector", "scalar")


class Planner:

    def __init__(self):
        self.items = {e: [] for e in ENGINES}
        self.cnt = {}

    def op(self, eng, fn, waits=(), incs=()):
        for s, k in incs:
            self.cnt[s] = self.cnt.get(s, 0) + k
        self.items[eng].append((fn, list(waits), list(incs)))

    def now(self, sem):
        return self.cnt.get(sem, 0)


def build_nc(T):
    assert T % 128 == 0
    TC = T // 128
    MV = min(512, T)
    NTS = T // MV
    nc = bass.Bass("TRN2", target_bir_lowering=False)

    xT = nc.dram_tensor("xT", [D, T], F32, kind="ExternalInput").ap()
    xN = nc.dram_tensor("xN", [T, D], F32, kind="ExternalInput").ap()
    Wi = nc.dram_tensor("Wi", [D, D], F32, kind="ExternalInput").ap()
    Wf = nc.dram_tensor("Wf", [D, D], F32, kind="ExternalInput").ap()
    Wo = nc.dram_tensor("Wo", [D, D], F32, kind="ExternalInput").ap()
    Ws = nc.dram_tensor("Ws", [D, D], F32, kind="ExternalInput").ap()
    biases = nc.dram_tensor("biases", [128, 32], F32, kind="ExternalInput").ap()
    gamma = nc.dram_tensor("gamma", [D], F32, kind="ExternalInput").ap()
    beta = nc.dram_tensor("beta", [D], F32, kind="ExternalInput").ap()
    sel = nc.dram_tensor("sel", [128, 8], F32, kind="ExternalInput").ap()
    ident = nc.dram_tensor("ident", [128, 128], F32, kind="ExternalInput").ap()
    wgT = nc.dram_tensor("wgT", [D, HID], I32, kind="ExternalInput").ap()
    wuT = nc.dram_tensor("wuT", [D, HID], I32, kind="ExternalInput").ap()
    wdT = nc.dram_tensor("wdT", [HID, D], I32, kind="ExternalInput").ap()
    out_d = nc.dram_tensor("out", [T, D], F32, kind="ExternalOutput").ap()
    cc_in = nc.dram_tensor("cc_in", [128, 16], F32, kind="Internal").ap()
    cc_out = nc.dram_tensor("cc_out", [4, 128, 16], F32, kind="Internal",
                            addr_space="Local").ap()
    x1s = nc.dram_tensor("x1s", [T, D], F16, kind="Internal").ap()

    GW = 4 * T
    O_XFM = 0
    O_G1 = GW
    O_G2 = 2 * GW
    O_G3 = 3 * GW
    O_Q = 4 * GW
    QW_W0 = O_Q
    QW_W1 = O_Q + 4096
    QW_SCR = O_Q + 8192
    QW_Z = QW_SCR + max(T // 2, 1024)
    QW_N1 = QW_Z + 1024
    Q_END = QW_N1 + 1024
    O_WD = O_Q
    O_X1BF = Q_END
    assert Q_END + 1024 - O_Q <= 12288
    O_P = O_Q + 12288
    P_GAM = O_P
    P_BET = O_P + 512
    P_BIA = O_P + 1024
    P_SEL = P_BIA + 32
    P_IDF = P_SEL + 8
    P_IDB = P_IDF + 64
    P_SUM = P_IDB + 64
    P_GATH = P_SUM + 16
    P_HRUN = P_GATH + 128
    P_HSELA = P_HRUN + 8
    P_HSELB = P_HSELA + 8
    P_COLS = P_HSELB + 8
    P_NH = P_COLS + 144
    P_SL = P_NH + 8
    P_XT = P_SL + 8
    P_OUTST = P_XT + 2048
    P_WGU = P_OUTST + 2048
    P_END = P_WGU + 2048
    NWORDS = P_END
    assert NWORDS <= 53000, NWORDS

    P = Planner()
    semnames = ([f"g_xf{k}" for k in range(KC)]
                + [f"g_wA{m}" for m in range(8)] + [f"g_wB{m}" for m in range(8)]
                + ["g_w0", "g_w1", "g_small", "g_gb", "g_wd", "r_bc", "l_bc", "g_wgu0",
                "g_wgu1", "s_x0", "s_x1", "s_x2", "s_x3", "s_cc", "s_x1w0", "s_x1w1", "s_x1w2", "s_x1w3", "s_wf0", "s_wf1", "s_out0", "s_out1",
                "T_", "V_", "S_", "G_"])

    arena_ctx = nc.sbuf_tensor("AR", [128, NWORDS], F32)
    AR = arena_ctx.__enter__()
    ps_ctx = nc.psum_tensor("PS", [128, 4096], F32)
    PS = ps_ctx.__enter__()
    ar = AR[:]
    ar16 = ar.bitcast(F16)
    arbf = ar.bitcast(BF16)
    ps = PS[:]
    ps16 = ps.bitcast(F16)
    psbf = ps.bitcast(BF16)

    def f32v(off, n):
        return ar[:, off:off + n]

    def f16v(off, n):
        return ar16[:, 2 * off: 2 * off + n]

    def bf16v(off, n):
        return arbf[:, 2 * off: 2 * off + n]

    xfm = bf16v(O_XFM, 8 * T)
    G1 = f16v(O_G1, 8 * T)
    G2 = f16v(O_G2, 8 * T)
    G3 = f16v(O_G3, 8 * T)
    x1fm = f16v(O_G3, 8 * T)
    hh = bf16v(O_XFM, 24 * T)
    Wb = [bf16v(QW_W0, 8 * 1024), bf16v(QW_W1, 8 * 1024)]
    scr = f16v(QW_SCR, T)
    zb = [f16v(QW_Z, 1024), f16v(QW_Z + 512, 1024),
          f16v(QW_SCR, 1024), f16v(QW_SCR + 512, 1024),
          f16v(P_OUTST, 1024), f16v(P_OUTST + 512, 1024),
          f16v(P_OUTST + 1024, 1024), f16v(P_OUTST + 1536, 1024)]
    ab = [f16v(QW_N1, 1024), f16v(QW_N1 + 512, 1024)]
    wd_sb = bf16v(O_WD, 24 * 1024)
    x1bf = [f16v(O_X1BF, 1024), f16v(O_X1BF + 512, 1024)]
    gam = f16v(P_GAM, 1024)
    bet = f16v(P_BET, 1024)
    bia = f32v(P_BIA, 32)
    selv = f32v(P_SEL, 8)
    idf = f16v(P_IDF, 128)
    idb = bf16v(P_IDB, 128)
    summ = f32v(P_SUM, 16)
    gath = f32v(P_GATH, 128)
    Hrun = f32v(P_HRUN, 8)
    HselA = f32v(P_HSELA, 8)
    HselB = f32v(P_HSELB, 8)
    nh = f32v(P_NH, 1)
    nh4 = f32v(P_NH, 4)
    sumlog = f32v(P_SL, 8)

    def col(i):
        return f32v(P_COLS + i, 1)

    xt = [f16v(P_XT, 1024), f16v(P_XT + 512, 1024),
          f16v(P_XT + 1024, 1024), f16v(P_XT + 1536, 1024)]
    sg = [f16v(P_XT, T), f16v(P_XT + 1024, T)]
    outst = [f32v(P_OUTST, 1024), f32v(P_OUTST + 1024, 1024)]
    wg_b = [bf16v(P_WGU, 1024), bf16v(P_WGU + 512, 1024)]
    wu_b = [bf16v(P_WGU + 1024, 1024), bf16v(P_WGU + 1536, 1024)]

    def grpA(par):
        return ps[:, par * 2048: par * 2048 + T]

    psB = [ps16[:, 0:1024], ps16[:, 1024:2048],
           ps16[:, 6144:7168], ps16[:, 7168:8192]]
    dumpB = G2[:, 0:1024]
    psC = [ps16[:, (2 + j) * 1024:(3 + j) * 1024] for j in range(4)]
    psG = ps[:, 0:T]
    psU = ps[:, 2048:2048 + T]
    psD = [ps[:, 0:1024], ps[:, 1024:2048]]
    dumpC = ps[:, 2048:3072]

    for fn in [
        lambda g: g.dma_start(out=bia, in_=biases),
        lambda g: g.dma_start(out=selv, in_=sel),
        lambda g: g.dma_start(out=idf, in_=ident),
        lambda g: g.dma_start(out=idb, in_=ident),
        lambda g: g.dma_start(out=gam, in_=gamma.partition_broadcast(128)),
        lambda g: g.dma_start(out=bet, in_=beta.partition_broadcast(128)),
    ]:
        P.op("gpsimd", fn, incs=[("g_small", 16)])
    staged_wf = (4 * T >= 8192)
    if staged_wf:
        for h in range(2):
            P.op("sync", lambda s, h=h: s.dma_start(
                out=f32v(O_G3 + h * 4096, 4096).rearrange("p (kc e) -> p kc e", kc=4),
                in_=Wf[h * 512:(h + 1) * 512, :].rearrange("(kc p) e -> p kc e", p=128)),
                incs=[(f"s_wf{h}", 16)])
        for h in range(2):
            P.op("vector", lambda v, h=h: v.tensor_copy(
                out=Wb[0][:, h * 4096:(h + 1) * 4096],
                in_=f32v(O_G3 + h * 4096, 4096)),
                waits=[(f"s_wf{h}", 16)], incs=[("V_", 1)])
    else:
        P.op("gpsimd", lambda g: g.dma_start(
            out=Wb[0].rearrange("p (kc e) -> p kc e", kc=KC),
            in_=Wf.rearrange("(kc p) e -> p kc e", p=128)), incs=[("g_w0", 16)])
    v_wf = P.now("V_")
    for k in range(KC):
        P.op("gpsimd", lambda g, k=k: g.dma_start(
            out=xfm[:, k * T:(k + 1) * T],
            in_=xT[k * 128:(k + 1) * 128, :]), incs=[(f"g_xf{k}", 16)])
    P.op("gpsimd", lambda g: g.dma_start(
        out=Wb[1].rearrange("p (kc e) -> p kc e", kc=KC),
        in_=Wi.rearrange("(kc p) e -> p kc e", p=128)), incs=[("g_w1", 16)])
    P.op("gpsimd", lambda g: g.memset(nh4[:], -0.5), incs=[("G_", 1)])

    for j in range(min(4, TC)):
        P.op("gpsimd", lambda g, j=j: g.dma_start(
            out=xt[j], in_=xN[j * 128:(j + 1) * 128, :]),
            incs=[(f"s_x{j}", 16)])
    sx_cnt = [16, 16, 16, 16]

    projs = [
        (0, AF.Sigmoid, G1),
        (1, AF.Sigmoid, G3),
        (0, AF.Tanh, G2),
        (1, AF.Sigmoid, G2),
    ]
    evA = []
    uV = []
    projT = []
    def plan_proj(p):
        slot, act, dst = projs[p]
        waits = []
        if p == 0:
            waits = [("V_", v_wf)] if staged_wf else [("g_w0", 16)]
        elif p == 1:
            waits = [("g_w1", 16)]
        elif p == 2:
            waits = [("g_w0", 16 if staged_wf else 32)]
        elif p == 3:
            waits = [("g_w1", 32)]
        for m in range(8):
            gidx = p * 8 + m
            par = gidx % 2
            w = list(waits) if m == 0 else []
            if gidx >= 2:
                w.append(("S_", evA[gidx - 2]))
            if p == 3:
                pass
            for k in range(KC):
                if p == 0 and m == 0:
                    w.append((f"g_xf{k}", 16))
                for ts in range(NTS):
                    def mm(t, par=par, k=k, ts=ts, slot=slot, m=m):
                        return t.matmul(
                            out=grpA(par)[:, ts * MV:(ts + 1) * MV],
                            lhsT=Wb[slot][:, k * 1024 + m * 128: k * 1024 + (m + 1) * 128],
                            rhs=xfm[:, k * T + ts * MV: k * T + ts * MV + MV],
                            start=(k == 0), stop=(k == KC - 1))
                    last = (k == KC - 1 and ts == NTS - 1)
                    P.op("tensor", mm, waits=w, incs=[("T_", 1)] if last else [])
                    w = []
            tdone = P.now("T_")
            ew = [("T_", tdone)]
            if p == 3:
                ew.append(("V_", uV[m]))

            def ev(sc, par=par, act=act, dst=dst, m=m, p=p):
                return sc.activation(out=dst[:, m * T:(m + 1) * T],
                                     in_=grpA(par)[:, 0:T], func=act,
                                     bias=bia[:, p * 8 + m: p * 8 + m + 1])
            if p == 0 and m == 0:
                ew.append(("g_small", 96))
            if p == 1 and m == 0 and staged_wf:
                ew.append(("V_", v_wf))
            P.op("scalar", ev, waits=ew, incs=[("S_", 1)])
            evA.append(P.now("S_"))
        projT.append(P.now("T_"))
        if p == 0:
            P.op("gpsimd", lambda g: g.dma_start(
                out=Wb[0].rearrange("p (kc e) -> p kc e", kc=KC),
                in_=Ws.rearrange("(kc p) e -> p kc e", p=128)),
                waits=[("T_", projT[0])], incs=[("g_w0", 16)])
        if p == 1:
            P.op("gpsimd", lambda g: g.dma_start(
                out=Wb[1].rearrange("p (kc e) -> p kc e", kc=KC),
                in_=Wo.rearrange("(kc p) e -> p kc e", p=128)),
                waits=[("T_", projT[1])], incs=[("g_w1", 16)])


    plan_proj(0)
    plan_proj(1)
    plan_proj(2)

    lndump = f16v(O_X1BF, 2 * T)
    for kc in range(KC):
        P.op("scalar", lambda sc, kc=kc: sc.activation(
            out=lndump[:, 0:T], in_=G1[:, kc * T:(kc + 1) * T], func=AF.Ln,
            accum_out=sumlog[:, kc:kc + 1]),
            waits=[("S_", P.now("S_"))], incs=[("S_", 1)])
    P.op("scalar", lambda sc: sc.activation(
        out=summ[:, 0:8], in_=sumlog, func=AF.Exp),
        waits=[("S_", P.now("S_"))], incs=[("S_", 1)])
    s_atot = P.now("S_")
    for kc in range(KC):
        P.op("vector", lambda v, kc=kc: v.tensor_tensor(
            out=G3[:, kc * T:(kc + 1) * T], in0=G3[:, kc * T:(kc + 1) * T],
            in1=G2[:, kc * T:(kc + 1) * T], op=OP.mult),
            waits=[("S_", evA[16 + kc]), ("V_", P.now("V_"))], incs=[("V_", 1)])
        uV.append(P.now("V_"))
        P.op("vector", lambda v, kc=kc: v.tensor_tensor_scan(
            out=scr, data0=G1[:, kc * T:(kc + 1) * T],
            data1=G3[:, kc * T:(kc + 1) * T], initial=0.0,
            op0=OP.mult, op1=OP.add),
            waits=[("S_", evA[kc]), ("V_", P.now("V_"))], incs=[("V_", 1)])
        P.op("vector", lambda v, kc=kc: v.tensor_copy(
            out=summ[:, 8 + kc:9 + kc], in_=scr[:, T - 1:T]),
            waits=[("V_", P.now("V_"))], incs=[("V_", 1)])
    v_summ = P.now("V_")
    plan_proj(3)


    P.op("sync", lambda s: s.dma_start(out=cc_in, in_=summ),
         waits=[("V_", v_summ), ("S_", s_atot)], incs=[("s_cc", 16)])
    P.op("gpsimd", lambda g: g.collective_compute(
        "AllGather", OP.bypass, replica_groups=[[0, 1, 2, 3], [4, 5, 6, 7]],
        ins=[cc_in], outs=[cc_out]),
        waits=[("s_cc", 16)], incs=[("G_", 1)])
    P.op("gpsimd", lambda g: g.dma_start(
        out=gath[:, 0:64].rearrange("p (r c) -> p r c", r=4),
        in_=cc_out.rearrange("r p c -> p r c")),
        waits=[("G_", P.now("G_"))], incs=[("g_gb", 16)])

    P.op("vector", lambda v: v.memset(Hrun[:], 0.0),
         waits=[("g_gb", 16), ("g_small", 96)], incs=[("V_", 1)])
    P.op("vector", lambda v: v.memset(HselA[:], 0.0),
         waits=[("V_", P.now("V_"))], incs=[("V_", 1)])
    hsel_cur, hsel_alt = HselA, HselB
    for r in range(4):
        P.op("vector", lambda v, r=r, a=hsel_alt, c=hsel_cur: v.scalar_tensor_tensor(
            out=a, in0=Hrun[:], scalar=selv[:, r:r + 1], in1=c,
            op0=OP.mult, op1=OP.add),
            waits=[("V_", P.now("V_"))], incs=[("V_", 1)])
        hsel_cur, hsel_alt = hsel_alt, hsel_cur
        if r < 3:
            P.op("vector", lambda v, r=r: v.tensor_tensor(
                out=Hrun[:], in0=Hrun[:], in1=gath[:, r * 16:r * 16 + 8], op=OP.mult),
                waits=[("V_", P.now("V_"))], incs=[("V_", 1)])
            P.op("vector", lambda v, r=r: v.tensor_tensor(
                out=Hrun[:], in0=Hrun[:], in1=gath[:, r * 16 + 8:r * 16 + 16], op=OP.add),
                waits=[("V_", P.now("V_"))], incs=[("V_", 1)])
    Hfin = hsel_cur

    yV = []
    for kc in range(KC):
        P.op("vector", lambda v, kc=kc: v.tensor_tensor_scan(
            out=scr, data0=G1[:, kc * T:(kc + 1) * T],
            data1=G3[:, kc * T:(kc + 1) * T], initial=Hfin[:, kc:kc + 1],
            op0=OP.mult, op1=OP.add),
            waits=[("V_", P.now("V_"))], incs=[("V_", 1)])
        P.op("vector", lambda v, kc=kc: v.tensor_tensor(
            out=G1[:, kc * T:(kc + 1) * T], in0=G2[:, kc * T:(kc + 1) * T],
            in1=scr, op=OP.mult),
            waits=[("V_", P.now("V_")), ("S_", evA[24 + kc])], incs=[("V_", 1)])
        yV.append(P.now("V_"))

    scr_words = max(T // 2, 1024)
    n_scr_ch = scr_words // 512
    wd_cnt = [0]

    def plan_wd(mlist, waits):
        for mi in mlist:
            P.op("gpsimd", lambda g, mi=mi: g.dma_start(
                out=wd_sb[:, mi * 1024:(mi + 1) * 1024],
                in_=wdT[mi * 128:(mi + 1) * 128, :]),
                waits=waits if mi == mlist[0] else [], incs=[("g_wd", 16)])
            wd_cnt[0] += 1

    def plan_wgu(m, waits):
        par = m % 2
        P.op("gpsimd", lambda g, m=m, par=par: g.dma_start(
            out=wg_b[par].rearrange("p (kc h) -> p kc h", kc=KC),
            in_=wgT[:, m * 128:(m + 1) * 128].rearrange("(kc p) h -> p kc h", p=128)),
            waits=waits, incs=[(f"g_wgu{par}", 16)])
        P.op("gpsimd", lambda g, m=m, par=par: g.dma_start(
            out=wu_b[par].rearrange("p (kc h) -> p kc h", kc=KC),
            in_=wuT[:, m * 128:(m + 1) * 128].rearrange("(kc p) h -> p kc h", p=128)),
            waits=[], incs=[(f"g_wgu{par}", 16)])

    plan_wgu(0, [])
    plan_wgu(1, [])
    plan_wd(list(range(0, 16)), [("T_", projT[3])])
    wd_rest = list(range(16, 24))

    colnames = ("s2", "ssum", "mu", "var", "rstd", "nb")
    colix = {}
    for stg in range(3):
        for ni, name in enumerate(colnames):
            for pr in range(8):
                colix[(stg, pr, name)] = stg * 48 + ni * 8 + pr
    lnV_last = {}
    lnS_last = {}
    tyT = [0] * TC
    v_z = [0] * TC
    v_w = [0] * TC
    v_xb_m = [0] * TC
    x1bf_S = [0] * TC
    ap32_S = [0] * TC
    x1T_quad = [0] * TC
    evC = []
    sq_mark = {}

    def lncol(stage, tc, name):
        return col(colix[(stage, tc % 8, name)])

    def ln_sq(stage, tc, buf, vwait, dump):
        key = (stage, tc % 8)
        c_s2 = lncol(stage, tc, "s2")
        wsq = [("V_", vwait), ("S_", P.now("S_"))]
        if stage < 2:
            wsq.append(("V_", yV[-1]))
        if key in lnV_last:
            wsq.append(("V_", lnV_last[key]))
        P.op("scalar", lambda sc, buf=buf, c_s2=c_s2, dump=dump: sc.activation(
            out=dump, in_=buf, func=AF.Square, accum_out=c_s2),
            waits=wsq, incs=[("S_", 1)])
        sq_mark[key] = P.now("S_")

    cm_gp = {}

    def ln_cm_a(stage, vwait, pr0=0, nw=4):
        def c4(name):
            ni = colnames.index(name)
            return f32v(P_COLS + stage * 48 + ni * 8 + pr0, nw)
        ss4, mu4, va4, rs4 = c4("ssum"), c4("mu"), c4("var"), c4("rstd")
        s24 = c4("s2")
        P.op("vector", lambda v: v.tensor_tensor(
            out=mu4, in0=ss4, in1=ss4, op=OP.mult),
            waits=[("V_", vwait)], incs=[("V_", 1)])
        P.op("vector", lambda v: v.scalar_tensor_tensor(
            out=va4, in0=mu4, scalar=-1.0 / D, in1=s24, op0=OP.mult, op1=OP.add),
            waits=[("V_", P.now("V_")), ("S_", sq_mark[(stage, pr0 + nw - 1)])],
            incs=[("V_", 1)])
        P.op("vector", lambda v: v.tensor_scalar(
            out=va4, in0=va4, scalar1=1.0 / D, scalar2=LN_EPS,
            op0=OP.mult, op1=OP.add),
            waits=[("V_", P.now("V_"))], incs=[("V_", 1)])
        vv = P.now("V_")
        if stage < 2:
            P.op("vector", lambda v: v.reciprocal(out=rs4, in_=va4),
                 waits=[("V_", vv)], incs=[("V_", 1)])
            P.op("scalar", lambda sc: sc.activation(
                out=rs4, in_=rs4, func=AF.Sqrt),
                waits=[("V_", P.now("V_")), ("S_", P.now("S_"))], incs=[("S_", 1)])
            cm_gp[(stage, pr0)] = ("S_", P.now("S_"))
        else:
            P.op("gpsimd", lambda g: g.tensor_tensor(
                out=rs4, in0=va4, in1=nh4[:, 0:nw], op=OP.pow),
                waits=[("V_", vv), ("G_", 1)], incs=[("G_", 1)])
            cm_gp[(stage, pr0)] = ("G_", P.now("G_"))

    def ln_cm_b(stage, pr0=0, nw=4):
        def c4(name):
            ni = colnames.index(name)
            return f32v(P_COLS + stage * 48 + ni * 8 + pr0, nw)
        ss4, rs4, nb4 = c4("ssum"), c4("rstd"), c4("nb")
        P.op("vector", lambda v: v.scalar_tensor_tensor(
            out=nb4, in0=ss4, scalar=-1.0 / D, in1=rs4, op0=OP.mult, op1=OP.mult),
            waits=[cm_gp[(stage, pr0)], ("V_", P.now("V_"))],
            incs=[("V_", 1)])
        for pr in range(pr0, pr0 + nw):
            lnV_last[(stage, pr)] = P.now("V_")

    def ln_back(stage, tc):
        return (lncol(stage, tc, "rstd"), lncol(stage, tc, "nb"),
                lnV_last[(stage, tc % 8)])

    def b_step_A(tc):
        pb = tc % 4
        for kc in range(KC):
            w = []
            if tc < 4:
                w.append(("V_", yV[kc]))
                if kc == 0:
                    w.append(("S_", evA[31]))
                    if tc == 0:
                        w.append(("g_small", 96))
            elif kc == 0:
                w.append(("V_", v_z[tc - 4]))
            P.op("tensor", lambda t, kc=kc, pb=pb, tc=tc: t.transpose(
                out=psB[pb][:, kc * 128:(kc + 1) * 128],
                in_=G1[:, kc * T + tc * 128: kc * T + (tc + 1) * 128],
                identity=idf),
                waits=w, incs=[("T_", 1)] if kc == KC - 1 else [])
        tyT[tc] = P.now("T_")
        q = tc % 8
        w = [("T_", tyT[tc]), (f"s_x{tc % 4}", 16 * (tc // 4 + 1))]
        if tc >= 8:
            w.append(("V_", ap32_S[tc - 8]))
            w.append(("T_", x1T_quad[tc - 8]))
            w.append((f"s_x1w{(tc - 8) % 4}", 16 * (tc // 4 - 1)))
        if tc in (2, 3):
            w.append(("V_", yV[-1]))
        if (0, q) in lnV_last:
            w.append(("V_", lnV_last[(0, q)]))
        c_ss1 = lncol(0, tc, "ssum")
        P.op("vector", lambda v, pb=pb, q=q, c_ss1=c_ss1: v.scalar_tensor_tensor(
            out=zb[q], in0=psB[pb], scalar=1.0, in1=xt[tc % 4],
            op0=OP.mult, op1=OP.add, accum_out=c_ss1),
            waits=w, incs=[("V_", 1)])
        v_z[tc] = P.now("V_")
        P.op("vector", lambda v, q=tc % 4: v.tensor_tensor(
            out=xt[q], in0=xt[q], in1=bet, op=OP.add),
            waits=[("V_", v_z[tc])], incs=[("V_", 1)])
        v_xb_m[tc] = ("V_", P.now("V_"))

    def b_step_D(tc):
        par = tc % 2
        q = tc % 8
        rs1, nb1, vnb1 = ln_back(0, tc)
        wn1 = [("V_", vnb1)]
        if tc >= 2:
            wn1.append(("V_", v_w[tc - 2]))
        P.op("vector", lambda v, rs1=rs1, nb1=nb1, par=par, q=q: v.tensor_scalar(
            out=ab[par], in0=zb[q], scalar1=rs1, scalar2=nb1,
            op0=OP.mult, op1=OP.add),
            waits=wn1, incs=[("V_", 1)])
        P.op("vector", lambda v, par=par, q=q: v.tensor_tensor(
            out=zb[q], in0=ab[par], in1=gam, op=OP.mult),
            waits=[("V_", P.now("V_"))], incs=[("V_", 1)])
        wsum = [("V_", P.now("V_")), v_xb_m[tc]]
        if (1, tc % 8) in lnV_last:
            wsum.append(("V_", lnV_last[(1, tc % 8)]))
        c_ss2 = lncol(1, tc, "ssum")
        P.op("vector", lambda v, q=q, xq=tc % 4, c_ss2=c_ss2: v.scalar_tensor_tensor(
            out=zb[q], in0=zb[q], in1=xt[xq], scalar=1.0,
            op0=OP.mult, op1=OP.add, accum_out=c_ss2),
            waits=wsum, incs=[("V_", 1)])
        v_w[tc] = P.now("V_")
        if tc + 4 < TC:
            P.op("gpsimd", lambda g, tc=tc, xq=tc % 4: g.dma_start(
                out=xt[xq], in_=xN[(tc + 4) * 128:(tc + 5) * 128, :]),
                waits=[("V_", v_w[tc])], incs=[(f"s_x{tc % 4}", 16)])
            sx_cnt[tc % 4] += 16

    def b_step_G(tc):
        par = tc % 2
        q = tc % 8
        rs2, nb2, vnb2 = ln_back(1, tc)
        wap = [("V_", vnb2)]
        P.op("vector", lambda v, rs2=rs2, nb2=nb2, q=q: v.tensor_scalar(
            out=zb[q], in0=zb[q], scalar1=rs2, scalar2=nb2,
            op0=OP.mult, op1=OP.add),
            waits=wap, incs=[("V_", 1)])
        x1bf_S[tc] = P.now("V_")
        ap32_S[tc] = P.now("V_")
        P.op("sync", lambda s, tc=tc, q=q: s.dma_start(
            out=x1s[tc * 128:(tc + 1) * 128, :], in_=zb[q]),
            waits=[("V_", x1bf_S[tc])], incs=[(f"s_x1w{tc % 4}", 16)])
        for kc in range(KC):
            w = [("V_", x1bf_S[tc])] if kc == 0 else []
            if tc % 4 == 0 and len(evC) > 0 and kc == 0:
                w.append(("S_", evC[-1]))
            P.op("tensor", lambda t, kc=kc, tc=tc: t.transpose(
                out=psC[kc // 2][:, (kc % 2) * 512 + (tc % 4) * 128:
                                 (kc % 2) * 512 + (tc % 4) * 128 + 128],
                in_=zb[tc % 8][:, kc * 128:(kc + 1) * 128],
                identity=idf),
                waits=w, incs=[("T_", 1)] if kc == KC - 1 else [])
        x1T_quad[tc] = P.now("T_")
        if tc % 4 == 3:
            for kc in range(KC):
                wev = [("T_", x1T_quad[tc])] if kc == 0 else []
                if kc == 0 and len(evC) == 0:
                    wev.append(("V_", yV[-1]))
                P.op("scalar", lambda sc, kc=kc, tc=tc: sc.copy(
                    out=x1fm[:, kc * T + (tc - 3) * 128: kc * T + (tc + 1) * 128],
                    in_=psC[kc // 2][:, (kc % 2) * 512:(kc % 2) * 512 + 512]),
                    waits=wev, incs=[("S_", 1)])
            evC.append(P.now("S_"))

    assert TC % 4 == 0
    sq2_mark = [0] * TC
    wsum2_mark = [0] * TC
    NQ = TC // 4

    def bH1(qq):
        tcs = [4 * qq + j for j in range(4)]
        for tc in tcs:
            b_step_A(tc)
            ln_sq(0, tc, zb[tc % 8], v_z[tc], dumpB)
        ln_cm_a(0, v_z[tcs[-1]], pr0=(4 * qq) % 8, nw=4)

    def bH2(qq):
        tcs = [4 * qq + j for j in range(4)]
        ln_cm_b(0, pr0=(4 * qq) % 8, nw=4)
        for tc in tcs:
            b_step_D(tc)
            ln_sq(1, tc, zb[tc % 8], v_w[tc], dumpB)
            sq2_mark[tc] = P.now("S_")
        ln_cm_a(1, v_w[tcs[-1]], pr0=(4 * qq) % 8, nw=4)

    def bH3(qq):
        tcs = [4 * qq + j for j in range(4)]
        ln_cm_b(1, pr0=(4 * qq) % 8, nw=4)
        for tc in tcs:
            b_step_G(tc)

    for qq in range(NQ + 1):
        if qq < NQ:
            bH1(qq)
        if qq >= 1:
            bH3(qq - 1)
        if qq < NQ:
            bH2(qq)

    vB_done = P.now("V_")
    sB_done = P.now("S_")
    plan_wd(wd_rest, [("S_", sB_done), ("V_", vB_done), ("T_", x1T_quad[-1]),
                      ("s_x1w0", P.now("s_x1w0")), ("s_x1w1", P.now("s_x1w1")),
                      ("s_x1w2", P.now("s_x1w2")), ("s_x1w3", P.now("s_x1w3"))])

    gwgu_cnt = [32, 32]
    siluS = []
    hhV = []
    tG = []
    tU = []
    for m in range(M):
        par = m % 2
        w = [(f"g_wgu{par}", gwgu_cnt[par])]
        if m == 0:
            w += [("S_", evC[-1]), ("V_", vB_done), ("S_", sB_done)]
        if m >= 1:
            w.append(("S_", siluS[m - 1]))
        for k in range(KC):
            for ts in range(NTS):
                def mmg(t, par=par, k=k, ts=ts, m=m):
                    return t.matmul(
                        out=psG[:, ts * MV:(ts + 1) * MV],
                        lhsT=wg_b[par][:, k * 128:(k + 1) * 128],
                        rhs=x1fm[:, k * T + ts * MV: k * T + ts * MV + MV],
                        start=(k == 0), stop=(k == KC - 1))
                last = (k == KC - 1 and ts == NTS - 1)
                P.op("tensor", mmg, waits=w, incs=[("T_", 1)] if last else [])
                w = []
        tG.append(P.now("T_"))
        w = []
        if m >= 1:
            w.append(("V_", hhV[m - 1]))
        elif m == 0:
            pass
        for k in range(KC):
            for ts in range(NTS):
                def mmu(t, par=par, k=k, ts=ts, m=m):
                    return t.matmul(
                        out=psU[:, ts * MV:(ts + 1) * MV],
                        lhsT=wu_b[par][:, k * 128:(k + 1) * 128],
                        rhs=x1fm[:, k * T + ts * MV: k * T + ts * MV + MV],
                        start=(k == 0), stop=(k == KC - 1))
                last = (k == KC - 1 and ts == NTS - 1)
                P.op("tensor", mmu, waits=w, incs=[("T_", 1)] if last else [])
                w = []
        tU.append(P.now("T_"))
        if m + 2 < M:
            plan_wgu(m + 2, [("T_", tU[m])])
            gwgu_cnt[par] += 32
        w = [("T_", tG[m])]
        if m >= 2:
            w.append(("V_", hhV[m - 2]))
        if m == 0:
            w.append(("V_", vB_done))
        P.op("scalar", lambda sc, par=par: sc.activation(
            out=sg[par][:, 0:T], in_=psG[:, 0:T], func=AF.Silu, scale=QS),
            waits=w, incs=[("S_", 1)])
        siluS.append(P.now("S_"))
        w = [("T_", tU[m]), ("S_", siluS[m])]
        if m == 0:
            w.append(("T_", projT[3]))
        if m == 8:
            w.append(("T_", tyT[-1]))
        if m == 16:
            w.append(("V_", yV[-1]))
            w.append(("S_", sB_done))
        P.op("vector", lambda v, par=par, m=m: v.scalar_tensor_tensor(
            out=hh[:, m * T:(m + 1) * T], in0=psU[:, 0:T], scalar=QS,
            in1=sg[par][:, 0:T], op0=OP.mult, op1=OP.mult),
            waits=w, incs=[("V_", 1)])
        hhV.append(P.now("V_"))

    vv_marks = [0] * TC
    sx_cnt_at = {}
    ln3S = [0] * TC

    def d_step_A(tc):
        par = tc % 2
        q = tc % 4
        w = [("s_x1w0", P.now("s_x1w0")), ("s_x1w1", P.now("s_x1w1")),
             ("s_x1w2", P.now("s_x1w2")), ("s_x1w3", P.now("s_x1w3"))]
        if tc < 4:
            w.append(("V_", hhV[-1]))
        else:
            w.append(("V_", vv_marks[tc - 4]))
        P.op("gpsimd", lambda g, tc=tc, q=q: g.dma_start(
            out=xt[q], in_=x1s[tc * 128:(tc + 1) * 128, :]),
            waits=w, incs=[(f"s_x{q}", 16)])
        sx_cnt[q] += 16
        sx_cnt_at[tc] = sx_cnt[q]
        w = []
        if tc == 0:
            w += [("g_wd", 16 * 24), ("V_", hhV[-1]), ("S_", siluS[-1])]
        if tc == 1:
            w.append(("S_", siluS[-1]))
        if tc >= 2:
            w.append(("V_", vv_marks[tc - 2]))
        for m in range(M):
            for half in range(2):
                def mmd(t, m=m, half=half, par=par, tc=tc):
                    return t.matmul(
                        out=psD[par][:, half * 512:(half + 1) * 512],
                        lhsT=hh[:, m * T + tc * 128: m * T + (tc + 1) * 128],
                        rhs=wd_sb[:, m * 1024 + half * 512: m * 1024 + (half + 1) * 512],
                        start=(m == 0), stop=(m == M - 1))
                last = (m == M - 1 and half == 1)
                P.op("tensor", mmd, waits=w, incs=[("T_", 1)] if last else [])
                w = []
        t_d = P.now("T_")
        w = [("T_", t_d), (f"s_x{tc % 4}", sx_cnt_at[tc])]
        if tc >= 2:
            w.append((f"s_out{par}", 16 * (tc // 2)))
        if (2, tc % 4) in lnV_last:
            w.append(("V_", lnV_last[(2, tc % 4)]))
        c_ss3 = lncol(2, tc, "ssum")
        P.op("vector", lambda v, par=par, q=tc % 4, c_ss3=c_ss3: v.scalar_tensor_tensor(
            out=outst[par], in0=psD[par], scalar=QS, in1=xt[q],
            op0=OP.mult, op1=OP.add, accum_out=c_ss3),
            waits=w, incs=[("V_", 1)])
        vv_marks[tc] = P.now("V_")

    def d_step_G(tc):
        par = tc % 2
        rs3, nb3, vnb3 = ln_back(2, tc)
        P.op("scalar", lambda sc, rs3=rs3, nb3=nb3, par=par: sc.activation(
            out=outst[par], in_=outst[par], func=AF.Identity, bias=nb3, scale=rs3),
            waits=[("V_", vnb3)], incs=[("S_", 1)])
        ln3S[tc] = P.now("S_")
        lnS_last[(2, tc % 4)] = P.now("S_")
        P.op("sync", lambda s, tc=tc, par=par: s.dma_start(
            out=out_d[tc * 128:(tc + 1) * 128, :], in_=outst[par]),
            waits=[("S_", ln3S[tc])], incs=[(f"s_out{par}", 16)])

    for pp in range(TC // 2):
        tcs = [2 * pp, 2 * pp + 1]
        for tc in tcs:
            d_step_A(tc)
        for tc in tcs:
            ln_sq(2, tc, outst[tc % 2], vv_marks[tc], dumpC)
        ln_cm_a(2, vv_marks[tcs[-1]], pr0=(2 * pp) % 8, nw=2)
        ln_cm_b(2, pr0=(2 * pp) % 8, nw=2)
        for tc in tcs:
            d_step_G(tc)

    P.op("sync", lambda s: s.wait_ge(sems["s_out0"], 16 * ((TC + 1) // 2)),
         waits=[], incs=[])
    P.op("sync", lambda s: s.wait_ge(sems["s_out1"], 16 * (TC // 2)),
         waits=[], incs=[])

    import contextlib
    es = contextlib.ExitStack()
    sems = {}
    for nm in semnames:
        sems[nm] = es.enter_context(nc.semaphore(nm))
    block = es.enter_context(nc.Block())

    def emit(eng, name):
        for fn, waits, incs in P.items[name]:
            for s, v in waits:
                eng.wait_ge(sems[s], v)
            instr = fn(eng)
            assert instr is not None or not incs, (name, waits, incs)
            if instr is not None:
                for s, k in incs:
                    instr.then_inc(sems[s], k)

    block.sync(lambda e: emit(e, "sync"))
    block.gpsimd(lambda e: emit(e, "gpsimd"))
    block.tensor(lambda e: emit(e, "tensor"))
    block.vector(lambda e: emit(e, "vector"))
    block.scalar(lambda e: emit(e, "scalar"))
    es.close()
    arena_ctx.__exit__(None, None, None)
    ps_ctx.__exit__(None, None, None)
    return nc


def _shards(L0_attn_x):
    B, S, _ = L0_attn_x.shape
    tok = (B * S) // NCORES
    core_slices = []
    for c in range(NCORES):
        seq = c // (NCORES // B)
        off = (c % (NCORES // B)) * tok
        core_slices.append((seq, off, tok))
    return core_slices


def kernel(L0_attn_x, L0_attn_Wi, L0_attn_bi, L0_attn_Wf, L0_attn_bf,
           L0_attn_Wo, L0_attn_bo, L0_attn_Ws, L0_attn_bs,
           L0_attn_gamma, L0_attn_beta, L0_ffn_w_gate, L0_ffn_w_up,
           L0_ffn_w_down):
    B, S, Dm = L0_attn_x.shape
    assert Dm == D
    T = (B * S) // NCORES
    x = np.ascontiguousarray(L0_attn_x, dtype=np.float32)
    bias_pack = np.concatenate(
        [np.ascontiguousarray(np.asarray(b, np.float32)).reshape(8, 128).T
         for b in (L0_attn_bf, L0_attn_bi, L0_attn_bs, L0_attn_bo)], axis=1)
    bias_pack = np.ascontiguousarray(bias_pack, np.float32)
    ident = np.eye(128, dtype=np.float32)
    wgT = np.ascontiguousarray(np.asarray(L0_ffn_w_gate).T, np.int32)
    wuT = np.ascontiguousarray(np.asarray(L0_ffn_w_up).T, np.int32)
    wdT = np.ascontiguousarray(np.asarray(L0_ffn_w_down).T, np.int32)
    Wi = np.ascontiguousarray(np.asarray(L0_attn_Wi), np.float32)
    Wf = np.ascontiguousarray(np.asarray(L0_attn_Wf), np.float32)
    Wo = np.ascontiguousarray(np.asarray(L0_attn_Wo), np.float32)
    Ws = np.ascontiguousarray(np.asarray(L0_attn_Ws), np.float32)
    gam = np.ascontiguousarray(np.asarray(L0_attn_gamma), np.float32)
    bet = np.ascontiguousarray(np.asarray(L0_attn_beta), np.float32)

    nc = build_nc(T)
    in_maps = []
    for c, (seq, off, tok) in enumerate(_shards(x)):
        xs = x[seq, off:off + tok, :]
        sel = np.zeros((128, 8), np.float32)
        sel[:, c % 4] = 1.0
        in_maps.append({
            "xT": np.ascontiguousarray(xs.T),
            "xN": np.ascontiguousarray(xs),
            "Wi": Wi, "Wf": Wf, "Wo": Wo, "Ws": Ws,
            "biases": bias_pack, "gamma": gam, "beta": bet,
            "sel": sel, "ident": ident,
            "wgT": wgT, "wuT": wuT, "wdT": wdT,
        })
    res = run_bass_kernel_spmd(nc, in_maps, core_ids=list(range(NCORES)))
    out = np.empty((B, S, D), np.float32)
    for c, (seq, off, tok) in enumerate(_shards(x)):
        out[seq, off:off + tok, :] = res.results[c]["out"]
    return out


# version 31
# speedup vs baseline: 1.0390x; 1.0095x over previous
import numpy as np
import concourse.bass as bass
import concourse.mybir as mybir
from concourse.bass_utils import run_bass_kernel_spmd

F32 = mybir.dt.float32
F16 = mybir.dt.float16
BF16 = mybir.dt.bfloat16
I32 = mybir.dt.int32
AF = mybir.ActivationFunctionType
OP = mybir.AluOpType

D = 1024
HID = 3072
KC = 8
M = 24
NCORES = 8
LN_EPS = 1e-5
QS = 1.0 / 127.0

ENGINES = ("sync", "gpsimd", "tensor", "vector", "scalar")


class Planner:

    def __init__(self):
        self.items = {e: [] for e in ENGINES}
        self.cnt = {}

    def op(self, eng, fn, waits=(), incs=()):
        for s, k in incs:
            self.cnt[s] = self.cnt.get(s, 0) + k
        self.items[eng].append((fn, list(waits), list(incs)))

    def now(self, sem):
        return self.cnt.get(sem, 0)


def build_nc(T):
    assert T % 128 == 0
    TC = T // 128
    MV = min(512, T)
    NTS = T // MV
    nc = bass.Bass("TRN2", target_bir_lowering=False)

    xT = nc.dram_tensor("xT", [D, T], F32, kind="ExternalInput").ap()
    xN = nc.dram_tensor("xN", [T, D], F32, kind="ExternalInput").ap()
    Wi = nc.dram_tensor("Wi", [D, D], F32, kind="ExternalInput").ap()
    Wf = nc.dram_tensor("Wf", [D, D], F32, kind="ExternalInput").ap()
    Wo = nc.dram_tensor("Wo", [D, D], F32, kind="ExternalInput").ap()
    Ws = nc.dram_tensor("Ws", [D, D], F32, kind="ExternalInput").ap()
    biases = nc.dram_tensor("biases", [128, 32], F32, kind="ExternalInput").ap()
    gamma = nc.dram_tensor("gamma", [D], F32, kind="ExternalInput").ap()
    beta = nc.dram_tensor("beta", [D], F32, kind="ExternalInput").ap()
    sel = nc.dram_tensor("sel", [128, 8], F32, kind="ExternalInput").ap()
    ident = nc.dram_tensor("ident", [128, 128], F32, kind="ExternalInput").ap()
    wgT = nc.dram_tensor("wgT", [D, HID], I32, kind="ExternalInput").ap()
    wuT = nc.dram_tensor("wuT", [D, HID], I32, kind="ExternalInput").ap()
    wdT = nc.dram_tensor("wdT", [HID, D], I32, kind="ExternalInput").ap()
    out_d = nc.dram_tensor("out", [T, D], F32, kind="ExternalOutput").ap()
    cc_in = nc.dram_tensor("cc_in", [128, 16], F32, kind="Internal").ap()
    cc_out = nc.dram_tensor("cc_out", [4, 128, 16], F32, kind="Internal",
                            addr_space="Local").ap()
    x1s = nc.dram_tensor("x1s", [T, D], F16, kind="Internal").ap()

    GW = 4 * T
    O_XFM = 0
    O_G1 = GW
    O_G2 = 2 * GW
    O_G3 = 3 * GW
    O_Q = 4 * GW
    QW_W0 = O_Q
    QW_W1 = O_Q + 4096
    QW_SCR = O_Q + 8192
    QW_Z = QW_SCR + max(T // 2, 1024)
    QW_N1 = QW_Z + 1024
    Q_END = QW_N1 + 1024
    O_WD = O_Q
    O_X1BF = Q_END
    assert Q_END + 1024 - O_Q <= 12288
    O_P = O_Q + 12288
    P_GAM = O_P
    P_BET = O_P + 512
    P_BIA = O_P + 1024
    P_SEL = P_BIA + 32
    P_IDF = P_SEL + 8
    P_IDB = P_IDF + 64
    P_SUM = P_IDB + 64
    P_GATH = P_SUM + 16
    P_HRUN = P_GATH + 128
    P_HSELA = P_HRUN + 8
    P_HSELB = P_HSELA + 8
    P_COLS = P_HSELB + 8
    P_NH = P_COLS + 144
    P_SL = P_NH + 8
    P_XT = P_SL + 8
    P_OUTST = P_XT + 2048
    P_WGU = P_OUTST + 2048
    P_END = P_WGU + 2048
    NWORDS = P_END
    assert NWORDS <= 53000, NWORDS

    P = Planner()
    semnames = ([f"g_xf{k}" for k in range(KC)]
                + [f"g_wA{m}" for m in range(8)] + [f"g_wB{m}" for m in range(8)]
                + ["g_w0", "g_w1", "g_small", "g_gb", "g_wd", "r_bc", "l_bc", "g_wgu0",
                "g_wgu1", "s_x0", "s_x1", "s_x2", "s_x3", "s_cc", "s_x1w0", "s_x1w1", "s_x1w2", "s_x1w3", "s_wf0", "s_wf1", "s_out0", "s_out1",
                "T_", "V_", "S_", "G_"])

    arena_ctx = nc.sbuf_tensor("AR", [128, NWORDS], F32)
    AR = arena_ctx.__enter__()
    ps_ctx = nc.psum_tensor("PS", [128, 4096], F32)
    PS = ps_ctx.__enter__()
    ar = AR[:]
    ar16 = ar.bitcast(F16)
    arbf = ar.bitcast(BF16)
    ps = PS[:]
    ps16 = ps.bitcast(F16)
    psbf = ps.bitcast(BF16)

    def f32v(off, n):
        return ar[:, off:off + n]

    def f16v(off, n):
        return ar16[:, 2 * off: 2 * off + n]

    def bf16v(off, n):
        return arbf[:, 2 * off: 2 * off + n]

    xfm = bf16v(O_XFM, 8 * T)
    G1 = f16v(O_G1, 8 * T)
    G2 = f16v(O_G2, 8 * T)
    G3 = f16v(O_G3, 8 * T)
    x1fm = f16v(O_G3, 8 * T)
    hh = bf16v(O_XFM, 24 * T)
    Wb = [bf16v(QW_W0, 8 * 1024), bf16v(QW_W1, 8 * 1024)]
    scr = f16v(QW_SCR, T)
    zb = [f16v(QW_Z, 1024), f16v(QW_Z + 512, 1024),
          f16v(QW_SCR, 1024), f16v(QW_SCR + 512, 1024),
          f16v(P_OUTST, 1024), f16v(P_OUTST + 512, 1024),
          f16v(P_OUTST + 1024, 1024), f16v(P_OUTST + 1536, 1024)]
    ab = [f16v(QW_N1, 1024), f16v(QW_N1 + 512, 1024)]
    wd_sb = bf16v(O_WD, 24 * 1024)
    x1bf = [f16v(O_X1BF, 1024), f16v(O_X1BF + 512, 1024)]
    gam = f16v(P_GAM, 1024)
    bet = f16v(P_BET, 1024)
    bia = f32v(P_BIA, 32)
    selv = f32v(P_SEL, 8)
    idf = f16v(P_IDF, 128)
    idb = bf16v(P_IDB, 128)
    summ = f32v(P_SUM, 16)
    gath = f32v(P_GATH, 128)
    Hrun = f32v(P_HRUN, 8)
    HselA = f32v(P_HSELA, 8)
    HselB = f32v(P_HSELB, 8)
    nh = f32v(P_NH, 1)
    nh4 = f32v(P_NH, 4)
    sumlog = f32v(P_SL, 8)

    def col(i):
        return f32v(P_COLS + i, 1)

    xt = [f16v(P_XT, 1024), f16v(P_XT + 512, 1024),
          f16v(P_XT + 1024, 1024), f16v(P_XT + 1536, 1024)]
    sg = [f16v(P_XT, T), f16v(P_XT + 1024, T)]
    outst = [f32v(P_OUTST, 1024), f32v(P_OUTST + 1024, 1024)]
    wg_b = [bf16v(P_WGU, 1024), bf16v(P_WGU + 512, 1024)]
    wu_b = [bf16v(P_WGU + 1024, 1024), bf16v(P_WGU + 1536, 1024)]

    def grpA(par):
        return ps[:, par * 2048: par * 2048 + T]

    psB = [ps16[:, 0:1024], ps16[:, 1024:2048],
           ps16[:, 6144:7168], ps16[:, 7168:8192]]
    dumpB = G2[:, 0:1024]
    psC = [ps16[:, (2 + j) * 1024:(3 + j) * 1024] for j in range(4)]
    psG = ps[:, 0:T]
    psU = ps[:, 2048:2048 + T]
    psD = [ps[:, 0:1024], ps[:, 1024:2048], ps[:, 2048:3072]]
    dumpC = x1fm[:, 0:1024]

    for fn in [
        lambda g: g.dma_start(out=bia, in_=biases),
        lambda g: g.dma_start(out=selv, in_=sel),
        lambda g: g.dma_start(out=idf, in_=ident),
        lambda g: g.dma_start(out=idb, in_=ident),
        lambda g: g.dma_start(out=gam, in_=gamma.partition_broadcast(128)),
        lambda g: g.dma_start(out=bet, in_=beta.partition_broadcast(128)),
    ]:
        P.op("gpsimd", fn, incs=[("g_small", 16)])
    staged_wf = (4 * T >= 8192)
    if staged_wf:
        for h in range(2):
            P.op("sync", lambda s, h=h: s.dma_start(
                out=f32v(O_G3 + h * 4096, 4096).rearrange("p (kc e) -> p kc e", kc=4),
                in_=Wf[h * 512:(h + 1) * 512, :].rearrange("(kc p) e -> p kc e", p=128)),
                incs=[(f"s_wf{h}", 16)])
        for h in range(2):
            P.op("vector", lambda v, h=h: v.tensor_copy(
                out=Wb[0][:, h * 4096:(h + 1) * 4096],
                in_=f32v(O_G3 + h * 4096, 4096)),
                waits=[(f"s_wf{h}", 16)], incs=[("V_", 1)])
    else:
        P.op("gpsimd", lambda g: g.dma_start(
            out=Wb[0].rearrange("p (kc e) -> p kc e", kc=KC),
            in_=Wf.rearrange("(kc p) e -> p kc e", p=128)), incs=[("g_w0", 16)])
    v_wf = P.now("V_")
    for k in range(KC):
        P.op("gpsimd", lambda g, k=k: g.dma_start(
            out=xfm[:, k * T:(k + 1) * T],
            in_=xT[k * 128:(k + 1) * 128, :]), incs=[(f"g_xf{k}", 16)])
    P.op("gpsimd", lambda g: g.dma_start(
        out=Wb[1].rearrange("p (kc e) -> p kc e", kc=KC),
        in_=Wi.rearrange("(kc p) e -> p kc e", p=128)), incs=[("g_w1", 16)])
    P.op("gpsimd", lambda g: g.memset(nh4[:], -0.5), incs=[("G_", 1)])

    for j in range(min(4, TC)):
        P.op("gpsimd", lambda g, j=j: g.dma_start(
            out=xt[j], in_=xN[j * 128:(j + 1) * 128, :]),
            incs=[(f"s_x{j}", 16)])
    sx_cnt = [16, 16, 16, 16]

    projs = [
        (0, AF.Sigmoid, G1),
        (1, AF.Sigmoid, G3),
        (0, AF.Tanh, G2),
        (1, AF.Sigmoid, G2),
    ]
    evA = []
    uV = []
    projT = []
    def plan_proj(p):
        slot, act, dst = projs[p]
        waits = []
        if p == 0:
            waits = [("V_", v_wf)] if staged_wf else [("g_w0", 16)]
        elif p == 1:
            waits = [("g_w1", 16)]
        elif p == 2:
            waits = [("g_w0", 16 if staged_wf else 32)]
        elif p == 3:
            waits = [("g_w1", 32)]
        for m in range(8):
            gidx = p * 8 + m
            par = gidx % 2
            w = list(waits) if m == 0 else []
            if gidx >= 2:
                w.append(("S_", evA[gidx - 2]))
            if p == 3:
                pass
            for k in range(KC):
                if p == 0 and m == 0:
                    w.append((f"g_xf{k}", 16))
                for ts in range(NTS):
                    def mm(t, par=par, k=k, ts=ts, slot=slot, m=m):
                        return t.matmul(
                            out=grpA(par)[:, ts * MV:(ts + 1) * MV],
                            lhsT=Wb[slot][:, k * 1024 + m * 128: k * 1024 + (m + 1) * 128],
                            rhs=xfm[:, k * T + ts * MV: k * T + ts * MV + MV],
                            start=(k == 0), stop=(k == KC - 1))
                    last = (k == KC - 1 and ts == NTS - 1)
                    P.op("tensor", mm, waits=w, incs=[("T_", 1)] if last else [])
                    w = []
            tdone = P.now("T_")
            ew = [("T_", tdone)]
            if p == 3:
                ew.append(("V_", uV[m]))

            def ev(sc, par=par, act=act, dst=dst, m=m, p=p):
                return sc.activation(out=dst[:, m * T:(m + 1) * T],
                                     in_=grpA(par)[:, 0:T], func=act,
                                     bias=bia[:, p * 8 + m: p * 8 + m + 1])
            if p == 0 and m == 0:
                ew.append(("g_small", 96))
            if p == 1 and m == 0 and staged_wf:
                ew.append(("V_", v_wf))
            P.op("scalar", ev, waits=ew, incs=[("S_", 1)])
            evA.append(P.now("S_"))
        projT.append(P.now("T_"))
        if p == 0:
            P.op("gpsimd", lambda g: g.dma_start(
                out=Wb[0].rearrange("p (kc e) -> p kc e", kc=KC),
                in_=Ws.rearrange("(kc p) e -> p kc e", p=128)),
                waits=[("T_", projT[0])], incs=[("g_w0", 16)])
        if p == 1:
            P.op("gpsimd", lambda g: g.dma_start(
                out=Wb[1].rearrange("p (kc e) -> p kc e", kc=KC),
                in_=Wo.rearrange("(kc p) e -> p kc e", p=128)),
                waits=[("T_", projT[1])], incs=[("g_w1", 16)])


    plan_proj(0)
    plan_proj(1)
    plan_proj(2)

    lndump = f16v(O_X1BF, 2 * T)
    for kc in range(KC):
        P.op("scalar", lambda sc, kc=kc: sc.activation(
            out=lndump[:, 0:T], in_=G1[:, kc * T:(kc + 1) * T], func=AF.Ln,
            accum_out=sumlog[:, kc:kc + 1]),
            waits=[("S_", P.now("S_"))], incs=[("S_", 1)])
    P.op("scalar", lambda sc: sc.activation(
        out=summ[:, 0:8], in_=sumlog, func=AF.Exp),
        waits=[("S_", P.now("S_"))], incs=[("S_", 1)])
    s_atot = P.now("S_")
    for kc in range(KC):
        P.op("vector", lambda v, kc=kc: v.tensor_tensor(
            out=G3[:, kc * T:(kc + 1) * T], in0=G3[:, kc * T:(kc + 1) * T],
            in1=G2[:, kc * T:(kc + 1) * T], op=OP.mult),
            waits=[("S_", evA[16 + kc]), ("V_", P.now("V_"))], incs=[("V_", 1)])
        uV.append(P.now("V_"))
        P.op("vector", lambda v, kc=kc: v.tensor_tensor_scan(
            out=scr, data0=G1[:, kc * T:(kc + 1) * T],
            data1=G3[:, kc * T:(kc + 1) * T], initial=0.0,
            op0=OP.mult, op1=OP.add),
            waits=[("S_", evA[kc]), ("V_", P.now("V_"))], incs=[("V_", 1)])
        P.op("vector", lambda v, kc=kc: v.tensor_copy(
            out=summ[:, 8 + kc:9 + kc], in_=scr[:, T - 1:T]),
            waits=[("V_", P.now("V_"))], incs=[("V_", 1)])
    v_summ = P.now("V_")
    plan_proj(3)


    P.op("sync", lambda s: s.dma_start(out=cc_in, in_=summ),
         waits=[("V_", v_summ), ("S_", s_atot)], incs=[("s_cc", 16)])
    P.op("gpsimd", lambda g: g.collective_compute(
        "AllGather", OP.bypass, replica_groups=[[0, 1, 2, 3], [4, 5, 6, 7]],
        ins=[cc_in], outs=[cc_out]),
        waits=[("s_cc", 16)], incs=[("G_", 1)])
    P.op("gpsimd", lambda g: g.dma_start(
        out=gath[:, 0:64].rearrange("p (r c) -> p r c", r=4),
        in_=cc_out.rearrange("r p c -> p r c")),
        waits=[("G_", P.now("G_"))], incs=[("g_gb", 16)])

    P.op("vector", lambda v: v.memset(Hrun[:], 0.0),
         waits=[("g_gb", 16), ("g_small", 96)], incs=[("V_", 1)])
    P.op("vector", lambda v: v.memset(HselA[:], 0.0),
         waits=[("V_", P.now("V_"))], incs=[("V_", 1)])
    hsel_cur, hsel_alt = HselA, HselB
    for r in range(4):
        P.op("vector", lambda v, r=r, a=hsel_alt, c=hsel_cur: v.scalar_tensor_tensor(
            out=a, in0=Hrun[:], scalar=selv[:, r:r + 1], in1=c,
            op0=OP.mult, op1=OP.add),
            waits=[("V_", P.now("V_"))], incs=[("V_", 1)])
        hsel_cur, hsel_alt = hsel_alt, hsel_cur
        if r < 3:
            P.op("vector", lambda v, r=r: v.tensor_tensor(
                out=Hrun[:], in0=Hrun[:], in1=gath[:, r * 16:r * 16 + 8], op=OP.mult),
                waits=[("V_", P.now("V_"))], incs=[("V_", 1)])
            P.op("vector", lambda v, r=r: v.tensor_tensor(
                out=Hrun[:], in0=Hrun[:], in1=gath[:, r * 16 + 8:r * 16 + 16], op=OP.add),
                waits=[("V_", P.now("V_"))], incs=[("V_", 1)])
    Hfin = hsel_cur

    yV = []
    for kc in range(KC):
        P.op("vector", lambda v, kc=kc: v.tensor_tensor_scan(
            out=scr, data0=G1[:, kc * T:(kc + 1) * T],
            data1=G3[:, kc * T:(kc + 1) * T], initial=Hfin[:, kc:kc + 1],
            op0=OP.mult, op1=OP.add),
            waits=[("V_", P.now("V_"))], incs=[("V_", 1)])
        P.op("vector", lambda v, kc=kc: v.tensor_tensor(
            out=G1[:, kc * T:(kc + 1) * T], in0=G2[:, kc * T:(kc + 1) * T],
            in1=scr, op=OP.mult),
            waits=[("V_", P.now("V_")), ("S_", evA[24 + kc])], incs=[("V_", 1)])
        yV.append(P.now("V_"))

    scr_words = max(T // 2, 1024)
    n_scr_ch = scr_words // 512
    wd_cnt = [0]

    def plan_wd(mlist, waits):
        for mi in mlist:
            P.op("gpsimd", lambda g, mi=mi: g.dma_start(
                out=wd_sb[:, mi * 1024:(mi + 1) * 1024],
                in_=wdT[mi * 128:(mi + 1) * 128, :]),
                waits=waits if mi == mlist[0] else [], incs=[("g_wd", 16)])
            wd_cnt[0] += 1

    def plan_wgu(m, waits):
        par = m % 2
        P.op("gpsimd", lambda g, m=m, par=par: g.dma_start(
            out=wg_b[par].rearrange("p (kc h) -> p kc h", kc=KC),
            in_=wgT[:, m * 128:(m + 1) * 128].rearrange("(kc p) h -> p kc h", p=128)),
            waits=waits, incs=[(f"g_wgu{par}", 16)])
        P.op("gpsimd", lambda g, m=m, par=par: g.dma_start(
            out=wu_b[par].rearrange("p (kc h) -> p kc h", kc=KC),
            in_=wuT[:, m * 128:(m + 1) * 128].rearrange("(kc p) h -> p kc h", p=128)),
            waits=[], incs=[(f"g_wgu{par}", 16)])

    plan_wgu(0, [])
    plan_wgu(1, [])
    plan_wd(list(range(0, 16)), [("T_", projT[3])])
    wd_rest = list(range(16, 24))

    colnames = ("s2", "ssum", "mu", "var", "rstd", "nb")
    colix = {}
    for stg in range(3):
        for ni, name in enumerate(colnames):
            for pr in range(8):
                colix[(stg, pr, name)] = stg * 48 + ni * 8 + pr
    lnV_last = {}
    lnS_last = {}
    tyT = [0] * TC
    v_z = [0] * TC
    v_w = [0] * TC
    v_xb_m = [0] * TC
    x1bf_S = [0] * TC
    ap32_S = [0] * TC
    x1T_quad = [0] * TC
    evC = []
    sq_mark = {}

    def lncol(stage, tc, name):
        return col(colix[(stage, tc % 8, name)])

    def ln_sq(stage, tc, buf, vwait, dump):
        key = (stage, tc % 8)
        c_s2 = lncol(stage, tc, "s2")
        wsq = [("V_", vwait), ("S_", P.now("S_"))]
        if stage < 2:
            wsq.append(("V_", yV[-1]))
        else:
            wsq.append(("T_", tU[-1]))
        if key in lnV_last:
            wsq.append(("V_", lnV_last[key]))
        P.op("scalar", lambda sc, buf=buf, c_s2=c_s2, dump=dump: sc.activation(
            out=dump, in_=buf, func=AF.Square, accum_out=c_s2),
            waits=wsq, incs=[("S_", 1)])
        sq_mark[key] = P.now("S_")

    cm_gp = {}

    def ln_cm_a(stage, vwait, pr0=0, nw=4):
        def c4(name):
            ni = colnames.index(name)
            return f32v(P_COLS + stage * 48 + ni * 8 + pr0, nw)
        ss4, mu4, va4, rs4 = c4("ssum"), c4("mu"), c4("var"), c4("rstd")
        s24 = c4("s2")
        P.op("vector", lambda v: v.tensor_tensor(
            out=mu4, in0=ss4, in1=ss4, op=OP.mult),
            waits=[("V_", vwait)], incs=[("V_", 1)])
        P.op("vector", lambda v: v.scalar_tensor_tensor(
            out=va4, in0=mu4, scalar=-1.0 / D, in1=s24, op0=OP.mult, op1=OP.add),
            waits=[("V_", P.now("V_")), ("S_", sq_mark[(stage, pr0 + nw - 1)])],
            incs=[("V_", 1)])
        P.op("vector", lambda v: v.tensor_scalar(
            out=va4, in0=va4, scalar1=1.0 / D, scalar2=LN_EPS,
            op0=OP.mult, op1=OP.add),
            waits=[("V_", P.now("V_"))], incs=[("V_", 1)])
        vv = P.now("V_")
        if stage < 2:
            P.op("vector", lambda v: v.reciprocal(out=rs4, in_=va4),
                 waits=[("V_", vv)], incs=[("V_", 1)])
            P.op("scalar", lambda sc: sc.activation(
                out=rs4, in_=rs4, func=AF.Sqrt),
                waits=[("V_", P.now("V_")), ("S_", P.now("S_"))], incs=[("S_", 1)])
            cm_gp[(stage, pr0)] = ("S_", P.now("S_"))
        else:
            P.op("gpsimd", lambda g: g.tensor_tensor(
                out=rs4, in0=va4, in1=nh4[:, 0:nw], op=OP.pow),
                waits=[("V_", vv), ("G_", 1)], incs=[("G_", 1)])
            cm_gp[(stage, pr0)] = ("G_", P.now("G_"))

    def ln_cm_b(stage, pr0=0, nw=4):
        def c4(name):
            ni = colnames.index(name)
            return f32v(P_COLS + stage * 48 + ni * 8 + pr0, nw)
        ss4, rs4, nb4 = c4("ssum"), c4("rstd"), c4("nb")
        P.op("vector", lambda v: v.scalar_tensor_tensor(
            out=nb4, in0=ss4, scalar=-1.0 / D, in1=rs4, op0=OP.mult, op1=OP.mult),
            waits=[cm_gp[(stage, pr0)], ("V_", P.now("V_"))],
            incs=[("V_", 1)])
        for pr in range(pr0, pr0 + nw):
            lnV_last[(stage, pr)] = P.now("V_")

    def ln_back(stage, tc):
        return (lncol(stage, tc, "rstd"), lncol(stage, tc, "nb"),
                lnV_last[(stage, tc % 8)])

    def b_step_A(tc):
        pb = tc % 4
        for kc in range(KC):
            w = []
            if tc < 4:
                w.append(("V_", yV[kc]))
                if kc == 0:
                    w.append(("S_", evA[31]))
                    if tc == 0:
                        w.append(("g_small", 96))
            elif kc == 0:
                w.append(("V_", v_z[tc - 4]))
            P.op("tensor", lambda t, kc=kc, pb=pb, tc=tc: t.transpose(
                out=psB[pb][:, kc * 128:(kc + 1) * 128],
                in_=G1[:, kc * T + tc * 128: kc * T + (tc + 1) * 128],
                identity=idf),
                waits=w, incs=[("T_", 1)] if kc == KC - 1 else [])
        tyT[tc] = P.now("T_")
        q = tc % 8
        w = [("T_", tyT[tc]), (f"s_x{tc % 4}", 16 * (tc // 4 + 1))]
        if tc >= 8:
            w.append(("V_", ap32_S[tc - 8]))
            w.append(("T_", x1T_quad[tc - 8]))
            w.append((f"s_x1w{(tc - 8) % 4}", 16 * (tc // 4 - 1)))
        if tc in (2, 3):
            w.append(("V_", yV[-1]))
        if (0, q) in lnV_last:
            w.append(("V_", lnV_last[(0, q)]))
        c_ss1 = lncol(0, tc, "ssum")
        P.op("vector", lambda v, pb=pb, q=q, c_ss1=c_ss1: v.scalar_tensor_tensor(
            out=zb[q], in0=psB[pb], scalar=1.0, in1=xt[tc % 4],
            op0=OP.mult, op1=OP.add, accum_out=c_ss1),
            waits=w, incs=[("V_", 1)])
        v_z[tc] = P.now("V_")
        P.op("vector", lambda v, q=tc % 4: v.tensor_tensor(
            out=xt[q], in0=xt[q], in1=bet, op=OP.add),
            waits=[("V_", v_z[tc])], incs=[("V_", 1)])
        v_xb_m[tc] = ("V_", P.now("V_"))

    def b_step_D(tc):
        par = tc % 2
        q = tc % 8
        rs1, nb1, vnb1 = ln_back(0, tc)
        wn1 = [("V_", vnb1)]
        if tc >= 2:
            wn1.append(("V_", v_w[tc - 2]))
        P.op("vector", lambda v, rs1=rs1, nb1=nb1, par=par, q=q: v.tensor_scalar(
            out=ab[par], in0=zb[q], scalar1=rs1, scalar2=nb1,
            op0=OP.mult, op1=OP.add),
            waits=wn1, incs=[("V_", 1)])
        P.op("vector", lambda v, par=par, q=q: v.tensor_tensor(
            out=zb[q], in0=ab[par], in1=gam, op=OP.mult),
            waits=[("V_", P.now("V_"))], incs=[("V_", 1)])
        wsum = [("V_", P.now("V_")), v_xb_m[tc]]
        if (1, tc % 8) in lnV_last:
            wsum.append(("V_", lnV_last[(1, tc % 8)]))
        c_ss2 = lncol(1, tc, "ssum")
        P.op("vector", lambda v, q=q, xq=tc % 4, c_ss2=c_ss2: v.scalar_tensor_tensor(
            out=zb[q], in0=zb[q], in1=xt[xq], scalar=1.0,
            op0=OP.mult, op1=OP.add, accum_out=c_ss2),
            waits=wsum, incs=[("V_", 1)])
        v_w[tc] = P.now("V_")
        if tc + 4 < TC:
            P.op("gpsimd", lambda g, tc=tc, xq=tc % 4: g.dma_start(
                out=xt[xq], in_=xN[(tc + 4) * 128:(tc + 5) * 128, :]),
                waits=[("V_", v_w[tc])], incs=[(f"s_x{tc % 4}", 16)])
            sx_cnt[tc % 4] += 16

    def b_step_G(tc):
        par = tc % 2
        q = tc % 8
        rs2, nb2, vnb2 = ln_back(1, tc)
        wap = [("V_", vnb2)]
        P.op("vector", lambda v, rs2=rs2, nb2=nb2, q=q: v.tensor_scalar(
            out=zb[q], in0=zb[q], scalar1=rs2, scalar2=nb2,
            op0=OP.mult, op1=OP.add),
            waits=wap, incs=[("V_", 1)])
        x1bf_S[tc] = P.now("V_")
        ap32_S[tc] = P.now("V_")
        P.op("sync", lambda s, tc=tc, q=q: s.dma_start(
            out=x1s[tc * 128:(tc + 1) * 128, :], in_=zb[q]),
            waits=[("V_", x1bf_S[tc])], incs=[(f"s_x1w{tc % 4}", 16)])
        for kc in range(KC):
            w = [("V_", x1bf_S[tc])] if kc == 0 else []
            if tc % 4 == 0 and len(evC) > 0 and kc == 0:
                w.append(("S_", evC[-1]))
            P.op("tensor", lambda t, kc=kc, tc=tc: t.transpose(
                out=psC[kc // 2][:, (kc % 2) * 512 + (tc % 4) * 128:
                                 (kc % 2) * 512 + (tc % 4) * 128 + 128],
                in_=zb[tc % 8][:, kc * 128:(kc + 1) * 128],
                identity=idf),
                waits=w, incs=[("T_", 1)] if kc == KC - 1 else [])
        x1T_quad[tc] = P.now("T_")
        if tc % 4 == 3:
            for kc in range(KC):
                wev = [("T_", x1T_quad[tc])] if kc == 0 else []
                if kc == 0 and len(evC) == 0:
                    wev.append(("V_", yV[-1]))
                P.op("scalar", lambda sc, kc=kc, tc=tc: sc.copy(
                    out=x1fm[:, kc * T + (tc - 3) * 128: kc * T + (tc + 1) * 128],
                    in_=psC[kc // 2][:, (kc % 2) * 512:(kc % 2) * 512 + 512]),
                    waits=wev, incs=[("S_", 1)])
            evC.append(P.now("S_"))

    assert TC % 4 == 0
    sq2_mark = [0] * TC
    wsum2_mark = [0] * TC
    NQ = TC // 4

    def bH1(qq):
        tcs = [4 * qq + j for j in range(4)]
        for tc in tcs:
            b_step_A(tc)
            ln_sq(0, tc, zb[tc % 8], v_z[tc], dumpB)
        ln_cm_a(0, v_z[tcs[-1]], pr0=(4 * qq) % 8, nw=4)

    def bH2(qq):
        tcs = [4 * qq + j for j in range(4)]
        ln_cm_b(0, pr0=(4 * qq) % 8, nw=4)
        for tc in tcs:
            b_step_D(tc)
            ln_sq(1, tc, zb[tc % 8], v_w[tc], dumpB)
            sq2_mark[tc] = P.now("S_")
        ln_cm_a(1, v_w[tcs[-1]], pr0=(4 * qq) % 8, nw=4)

    def bH3(qq):
        tcs = [4 * qq + j for j in range(4)]
        ln_cm_b(1, pr0=(4 * qq) % 8, nw=4)
        for tc in tcs:
            b_step_G(tc)

    for qq in range(NQ + 1):
        if qq < NQ:
            bH1(qq)
        if qq >= 1:
            bH3(qq - 1)
        if qq < NQ:
            bH2(qq)

    vB_done = P.now("V_")
    sB_done = P.now("S_")
    plan_wd(wd_rest, [("S_", sB_done), ("V_", vB_done), ("T_", x1T_quad[-1]),
                      ("s_x1w0", P.now("s_x1w0")), ("s_x1w1", P.now("s_x1w1")),
                      ("s_x1w2", P.now("s_x1w2")), ("s_x1w3", P.now("s_x1w3"))])

    gwgu_cnt = [32, 32]
    siluS = []
    hhV = []
    tG = []
    tU = []
    for m in range(M):
        par = m % 2
        w = [(f"g_wgu{par}", gwgu_cnt[par])]
        if m == 0:
            w += [("S_", evC[-1]), ("V_", vB_done), ("S_", sB_done)]
        if m >= 1:
            w.append(("S_", siluS[m - 1]))
        for k in range(KC):
            for ts in range(NTS):
                def mmg(t, par=par, k=k, ts=ts, m=m):
                    return t.matmul(
                        out=psG[:, ts * MV:(ts + 1) * MV],
                        lhsT=wg_b[par][:, k * 128:(k + 1) * 128],
                        rhs=x1fm[:, k * T + ts * MV: k * T + ts * MV + MV],
                        start=(k == 0), stop=(k == KC - 1))
                last = (k == KC - 1 and ts == NTS - 1)
                P.op("tensor", mmg, waits=w, incs=[("T_", 1)] if last else [])
                w = []
        tG.append(P.now("T_"))
        w = []
        if m >= 1:
            w.append(("V_", hhV[m - 1]))
        elif m == 0:
            pass
        for k in range(KC):
            for ts in range(NTS):
                def mmu(t, par=par, k=k, ts=ts, m=m):
                    return t.matmul(
                        out=psU[:, ts * MV:(ts + 1) * MV],
                        lhsT=wu_b[par][:, k * 128:(k + 1) * 128],
                        rhs=x1fm[:, k * T + ts * MV: k * T + ts * MV + MV],
                        start=(k == 0), stop=(k == KC - 1))
                last = (k == KC - 1 and ts == NTS - 1)
                P.op("tensor", mmu, waits=w, incs=[("T_", 1)] if last else [])
                w = []
        tU.append(P.now("T_"))
        if m + 2 < M:
            plan_wgu(m + 2, [("T_", tU[m])])
            gwgu_cnt[par] += 32
        w = [("T_", tG[m])]
        if m >= 2:
            w.append(("V_", hhV[m - 2]))
        if m == 0:
            w.append(("V_", vB_done))
        P.op("scalar", lambda sc, par=par: sc.activation(
            out=sg[par][:, 0:T], in_=psG[:, 0:T], func=AF.Silu, scale=QS),
            waits=w, incs=[("S_", 1)])
        siluS.append(P.now("S_"))
        w = [("T_", tU[m]), ("S_", siluS[m])]
        if m == 0:
            w.append(("T_", projT[3]))
        if m == 8:
            w.append(("T_", tyT[-1]))
        if m == 16:
            w.append(("V_", yV[-1]))
            w.append(("S_", sB_done))
        P.op("vector", lambda v, par=par, m=m: v.scalar_tensor_tensor(
            out=hh[:, m * T:(m + 1) * T], in0=psU[:, 0:T], scalar=QS,
            in1=sg[par][:, 0:T], op0=OP.mult, op1=OP.mult),
            waits=w, incs=[("V_", 1)])
        hhV.append(P.now("V_"))

    vv_marks = [0] * TC
    sx_cnt_at = {}
    ln3S = [0] * TC

    def d_step_A(tc):
        par = tc % 2
        q = tc % 4
        w = [("s_x1w0", P.now("s_x1w0")), ("s_x1w1", P.now("s_x1w1")),
             ("s_x1w2", P.now("s_x1w2")), ("s_x1w3", P.now("s_x1w3"))]
        if tc < 4:
            w.append(("V_", hhV[-1]))
        else:
            w.append(("V_", vv_marks[tc - 4]))
        P.op("gpsimd", lambda g, tc=tc, q=q: g.dma_start(
            out=xt[q], in_=x1s[tc * 128:(tc + 1) * 128, :]),
            waits=w, incs=[(f"s_x{q}", 16)])
        sx_cnt[q] += 16
        sx_cnt_at[tc] = sx_cnt[q]
        w = []
        if tc == 0:
            w += [("g_wd", 16 * 24), ("V_", hhV[-1]), ("S_", siluS[-1])]
        if tc in (1, 2):
            w.append(("S_", siluS[-1]))
        if tc >= 3:
            w.append(("V_", vv_marks[tc - 3]))
        for m in range(M):
            for half in range(2):
                def mmd(t, m=m, half=half, par=tc % 3, tc=tc):
                    return t.matmul(
                        out=psD[par][:, half * 512:(half + 1) * 512],
                        lhsT=hh[:, m * T + tc * 128: m * T + (tc + 1) * 128],
                        rhs=wd_sb[:, m * 1024 + half * 512: m * 1024 + (half + 1) * 512],
                        start=(m == 0), stop=(m == M - 1))
                last = (m == M - 1 and half == 1)
                P.op("tensor", mmd, waits=w, incs=[("T_", 1)] if last else [])
                w = []
        t_d = P.now("T_")
        w = [("T_", t_d), (f"s_x{tc % 4}", sx_cnt_at[tc])]
        if tc >= 2:
            w.append((f"s_out{par}", 16 * (tc // 2)))
        if (2, tc % 4) in lnV_last:
            w.append(("V_", lnV_last[(2, tc % 4)]))
        c_ss3 = lncol(2, tc, "ssum")
        P.op("vector", lambda v, par=par, p3=tc % 3, q=tc % 4, c_ss3=c_ss3: v.scalar_tensor_tensor(
            out=outst[par], in0=psD[p3], scalar=QS, in1=xt[q],
            op0=OP.mult, op1=OP.add, accum_out=c_ss3),
            waits=w, incs=[("V_", 1)])
        vv_marks[tc] = P.now("V_")

    def d_step_G(tc):
        par = tc % 2
        rs3, nb3, vnb3 = ln_back(2, tc)
        P.op("scalar", lambda sc, rs3=rs3, nb3=nb3, par=par: sc.activation(
            out=outst[par], in_=outst[par], func=AF.Identity, bias=nb3, scale=rs3),
            waits=[("V_", vnb3)], incs=[("S_", 1)])
        ln3S[tc] = P.now("S_")
        lnS_last[(2, tc % 4)] = P.now("S_")
        P.op("sync", lambda s, tc=tc, par=par: s.dma_start(
            out=out_d[tc * 128:(tc + 1) * 128, :], in_=outst[par]),
            waits=[("S_", ln3S[tc])], incs=[(f"s_out{par}", 16)])

    for pp in range(TC // 2):
        tcs = [2 * pp, 2 * pp + 1]
        for tc in tcs:
            d_step_A(tc)
        for tc in tcs:
            ln_sq(2, tc, outst[tc % 2], vv_marks[tc], dumpC)
        ln_cm_a(2, vv_marks[tcs[-1]], pr0=(2 * pp) % 8, nw=2)
        ln_cm_b(2, pr0=(2 * pp) % 8, nw=2)
        for tc in tcs:
            d_step_G(tc)

    P.op("sync", lambda s: s.wait_ge(sems["s_out0"], 16 * ((TC + 1) // 2)),
         waits=[], incs=[])
    P.op("sync", lambda s: s.wait_ge(sems["s_out1"], 16 * (TC // 2)),
         waits=[], incs=[])

    import contextlib
    es = contextlib.ExitStack()
    sems = {}
    for nm in semnames:
        sems[nm] = es.enter_context(nc.semaphore(nm))
    block = es.enter_context(nc.Block())

    def emit(eng, name):
        for fn, waits, incs in P.items[name]:
            for s, v in waits:
                eng.wait_ge(sems[s], v)
            instr = fn(eng)
            assert instr is not None or not incs, (name, waits, incs)
            if instr is not None:
                for s, k in incs:
                    instr.then_inc(sems[s], k)

    block.sync(lambda e: emit(e, "sync"))
    block.gpsimd(lambda e: emit(e, "gpsimd"))
    block.tensor(lambda e: emit(e, "tensor"))
    block.vector(lambda e: emit(e, "vector"))
    block.scalar(lambda e: emit(e, "scalar"))
    es.close()
    arena_ctx.__exit__(None, None, None)
    ps_ctx.__exit__(None, None, None)
    return nc


def _shards(L0_attn_x):
    B, S, _ = L0_attn_x.shape
    tok = (B * S) // NCORES
    core_slices = []
    for c in range(NCORES):
        seq = c // (NCORES // B)
        off = (c % (NCORES // B)) * tok
        core_slices.append((seq, off, tok))
    return core_slices


def kernel(L0_attn_x, L0_attn_Wi, L0_attn_bi, L0_attn_Wf, L0_attn_bf,
           L0_attn_Wo, L0_attn_bo, L0_attn_Ws, L0_attn_bs,
           L0_attn_gamma, L0_attn_beta, L0_ffn_w_gate, L0_ffn_w_up,
           L0_ffn_w_down):
    B, S, Dm = L0_attn_x.shape
    assert Dm == D
    T = (B * S) // NCORES
    x = np.ascontiguousarray(L0_attn_x, dtype=np.float32)
    bias_pack = np.concatenate(
        [np.ascontiguousarray(np.asarray(b, np.float32)).reshape(8, 128).T
         for b in (L0_attn_bf, L0_attn_bi, L0_attn_bs, L0_attn_bo)], axis=1)
    bias_pack = np.ascontiguousarray(bias_pack, np.float32)
    ident = np.eye(128, dtype=np.float32)
    wgT = np.ascontiguousarray(np.asarray(L0_ffn_w_gate).T, np.int32)
    wuT = np.ascontiguousarray(np.asarray(L0_ffn_w_up).T, np.int32)
    wdT = np.ascontiguousarray(np.asarray(L0_ffn_w_down).T, np.int32)
    Wi = np.ascontiguousarray(np.asarray(L0_attn_Wi), np.float32)
    Wf = np.ascontiguousarray(np.asarray(L0_attn_Wf), np.float32)
    Wo = np.ascontiguousarray(np.asarray(L0_attn_Wo), np.float32)
    Ws = np.ascontiguousarray(np.asarray(L0_attn_Ws), np.float32)
    gam = np.ascontiguousarray(np.asarray(L0_attn_gamma), np.float32)
    bet = np.ascontiguousarray(np.asarray(L0_attn_beta), np.float32)

    nc = build_nc(T)
    in_maps = []
    for c, (seq, off, tok) in enumerate(_shards(x)):
        xs = x[seq, off:off + tok, :]
        sel = np.zeros((128, 8), np.float32)
        sel[:, c % 4] = 1.0
        in_maps.append({
            "xT": np.ascontiguousarray(xs.T),
            "xN": np.ascontiguousarray(xs),
            "Wi": Wi, "Wf": Wf, "Wo": Wo, "Ws": Ws,
            "biases": bias_pack, "gamma": gam, "beta": bet,
            "sel": sel, "ident": ident,
            "wgT": wgT, "wuT": wuT, "wdT": wdT,
        })
    res = run_bass_kernel_spmd(nc, in_maps, core_ids=list(range(NCORES)))
    out = np.empty((B, S, D), np.float32)
    for c, (seq, off, tok) in enumerate(_shards(x)):
        out[seq, off:off + tok, :] = res.results[c]["out"]
    return out
